# version 23
# baseline (speedup 1.0000x reference)
import numpy as np
import ml_dtypes
from contextlib import ExitStack
import concourse.bass as bass
import concourse.mybir as mybir
from concourse.bass_utils import run_bass_kernel_spmd

F32 = mybir.dt.float32
BF16 = mybir.dt.bfloat16
AF = mybir.ActivationFunctionType
ALU = mybir.AluOpType
AX = mybir.AxisListType
f32 = np.float32
BF = ml_dtypes.bfloat16

D = 1024
S = 8192
TOK = 2048
E = 8
FF = 2048
DC = 8
EC = 8
FC = 16
QB = 4
NG = 8
CEXP = 9.0
PAIR_O = False

ENGINES = ["sync", "tensor", "vector", "scalar", "gpsimd"]
SD, GD, SC, CD = "sd", "gd", "sc", "cd"
NPOOL = 10
PE, DV, AC = "pe", "dve", "act"


class KB:

    def __init__(self, nc):
        self.nc = nc
        self.ops = {e: [] for e in ENGINES}
        self.cnt = {}
        self.semh = {}

    def at(self, sem):
        return self.cnt.get(sem, 0)

    def emit(self, eng, fn, waits=(), inc=None):
        def norm(x):
            if isinstance(x[1], tuple):
                return x[1]
            return x

        w = []
        for x in waits:
            if x is None:
                continue
            if isinstance(x, list):
                w.extend(norm(y) for y in x if y is not None and norm(y)[1] > 0)
            elif norm(x)[1] > 0:
                w.append(norm(x))
        v = None
        if inc is not None:
            self.cnt[inc[0]] = self.cnt.get(inc[0], 0) + inc[1]
            v = self.cnt[inc[0]]
        self.ops[eng].append((fn, w, inc))
        return v

    def barrier(self):
        snap = [(s, v) for s, v in self.cnt.items() if v > 0]
        for eng in ENGINES:
            self.ops[eng].append((None, list(snap), None))

    def build(self):
        for name in list(self.cnt):
            if name not in self.semh:
                self.semh[name] = self.nc.alloc_semaphore(name)

        def replayer(eng_name):
            def f(eng):
                last = {}
                for fn, waits, inc in self.ops[eng_name]:
                    for s_, v_ in waits:
                        if last.get(s_, 0) >= v_:
                            continue
                        eng.wait_ge(self.semh[s_], v_)
                        last[s_] = v_
                    if fn is None:
                        continue
                    inst = fn(eng)
                    if inc is not None:
                        inst.then_inc(self.semh[inc[0]], inc[1])
            return f

        with self.nc.Block() as blk:
            blk.sync(replayer("sync"))
            blk.tensor(replayer("tensor"))
            blk.vector(replayer("vector"))
            blk.scalar(replayer("scalar"))
            blk.gpsimd(replayer("gpsimd"))


def build_program(debug=False):
    nc = bass.Bass("TRN2", target_bir_lowering=False, debug=False)
    kb = KB(nc)

    xT = nc.dram_tensor("xT", [D, S], F32, kind="ExternalInput").ap()
    xTo = nc.dram_tensor("xTo", [D, TOK], F32, kind="ExternalInput").ap()
    xN = nc.dram_tensor("xN", [S, D], F32, kind="ExternalInput").ap()
    wqT = nc.dram_tensor("wqT", [D, D], F32, kind="ExternalInput").ap()
    wk2 = nc.dram_tensor("wk2", [D, D], F32, kind="ExternalInput").ap()
    bq = nc.dram_tensor("bq", [128, EC], F32, kind="ExternalInput").ap()
    router = nc.dram_tensor("router", [D, E], F32, kind="ExternalInput").ap()
    wish = nc.dram_tensor("wish", [D, FF], F32, kind="ExternalInput").ap()
    wssh = nc.dram_tensor("wssh", [D, FF], F32, kind="ExternalInput").ap()
    wosh = nc.dram_tensor("wosh", [FF, D], F32, kind="ExternalInput").ap()
    win = nc.dram_tensor("win", [E, D, FF], F32, kind="ExternalInput").ap()
    wsw = nc.dram_tensor("wsw", [E, D, FF], F32, kind="ExternalInput").ap()
    wout = nc.dram_tensor("wout", [E, FF, D], F32, kind="ExternalInput").ap()
    ones128 = nc.dram_tensor("ones128", [128, 1], BF16, kind="ExternalInput").ap()
    ones1f = nc.dram_tensor("ones1f", [1, 128], F32, kind="ExternalInput").ap()
    id128f = nc.dram_tensor("id128f", [128, 128], F32, kind="ExternalInput").ap()
    id8f = nc.dram_tensor("id8f", [8, 8], F32, kind="ExternalInput").ap()
    sel8 = nc.dram_tensor("sel8", [8, E * 128], F32, kind="ExternalInput").ap()

    zT = nc.dram_tensor("zT", [D, TOK], F32, kind="ExternalOutput").ap()
    y1b_h = nc.dram_tensor("y1b_h", [D, TOK], BF16, kind="Internal").ap()
    y1f_h = nc.dram_tensor("y1f_h", [D, TOK], F32,
                           kind="ExternalOutput" if debug else "Internal").ap()
    if debug >= 2:
        gat_h = nc.dram_tensor("gat_h", [8, TOK], F32, kind="ExternalOutput").ap()
        acc_h = nc.dram_tensor("acc_h", [D, TOK], F32, kind="ExternalOutput").ap()
        sco_h = nc.dram_tensor("sco_h", [8, TOK], F32, kind="ExternalOutput").ap()
        stok_h = nc.dram_tensor("stok_h", [128, 128], F32, kind="ExternalOutput").ap()
        gtok_h = nc.dram_tensor("gtok_h", [128, 128], F32, kind="ExternalOutput").ap()

    xT_r = xT.rearrange("(c p) t -> p c t", p=128)
    xN_r = xN.rearrange("(c p) d -> p c d", p=128)
    xTo_r = xTo.rearrange("(c p) t -> p c t", p=128)
    wqT_r = wqT.rearrange("(c p) e -> p c e", p=128)
    wk2_r = wk2.rearrange("(c p) d -> p c d", p=128)
    y1f_r = y1f_h.rearrange("(c p) t -> p c t", p=128)
    zT_r = zT.rearrange("(c p) t -> p c t", p=128)
    rtr_r = router.rearrange("(c p) e -> p c e", p=128)
    win_r = [win[e].rearrange("(c p) f -> p c f", p=128) for e in range(E)] + \
            [wish.rearrange("(c p) f -> p c f", p=128)]
    wsw_r = [wsw[e].rearrange("(c p) f -> p c f", p=128) for e in range(E)] + \
            [wssh.rearrange("(c p) f -> p c f", p=128)]
    wout_r = [wout[e].rearrange("(c p) d -> p c d", p=128) for e in range(E)] + \
             [wosh.rearrange("(c p) d -> p c d", p=128)]

    es = ExitStack()
    with es:
        sb = lambda name, shape, dt: es.enter_context(nc.sbuf_tensor(name, shape, dt))
        ones_sb = sb("ones_sb", [128, 1], BF16)
        ones1f_sb = sb("ones1f_sb", [1, 128], F32)
        bq_sb = sb("bq_sb", [128, EC], F32)
        cs_sb = sb("cs_sb", [128, DC], F32)
        cs_tmp = sb("cs_tmp", [128, DC], F32)
        tmp1_sb = sb("tmp1_sb", [128, 1], F32)
        rA = sb("rA", [1, 512], F32)
        rB = sb("rB", [1, 512], F32)
        rC = sb("rC", [1, 512], F32)
        rD = sb("rD", [1, 512], F32)
        bc0 = sb("bc0", [128, 512], F32)
        bc1 = sb("bc1", [128, 512], F32)

        pb = [es.enter_context(nc.psum_tensor(f"pb{i}", [128, 512], F32))
              for i in range(8)]
        bank_free = {}
        state = {"ln_war": None}

        dma_idx = {SD: 0, GD: 0, CD: 0}

        def dma(qsem, fn, waits=()):
            eng = {GD: "gpsimd", CD: "scalar"}.get(qsem, "sync")
            if qsem == SC:
                return kb.emit(eng, fn, waits=waits, inc=(SC, 16))
            phys = f"{qsem}{dma_idx[qsem] % NPOOL}"
            dma_idx[qsem] += 1
            v = kb.emit(eng, fn, waits=waits, inc=(phys, 16))
            return (phys, v)

        def bw(i):
            return bank_free.get(i)

        ln_war = {}

        def emit_ln_a(src, ub, ready, rows, banks):
            rS, rQ = rows
            bA, bB = banks
            war = ln_war.get(ub.name)
            cps = []
            for dc in range(DC):
                cps.append(kb.emit("vector", lambda v, dc_=dc: v.tensor_copy(
                    ub[:, dc_, :], src(dc_)),
                    waits=list(ready) + ([war] if dc == 0 else []), inc=(DV, 1)))
            b = bw(bA)
            last = None
            for dc in range(DC):
                last = kb.emit("tensor", lambda t, dc_=dc, b_=bA: t.matmul(
                    pb[b_][0:1, :], ones_sb[:], ub[:, dc_, :],
                    start=(dc_ == 0), stop=(dc_ == DC - 1)),
                    waits=[(DV, cps[dc]), b if dc == 0 else None],
                    inc=(PE, 1) if dc == DC - 1 else None)
            s1 = kb.emit("vector", lambda v, b_=bA: v.tensor_copy(rS[:], pb[b_][0:1, :]),
                         waits=[(PE, last)], inc=(DV, 1))
            bank_free[bA] = (DV, s1)
            sqs = []
            for dc in range(DC):
                sqs.append(kb.emit("vector", lambda v, dc_=dc: v.tensor_mul(
                    ub[:, dc_, :], ub[:, dc_, :], ub[:, dc_, :]),
                    waits=[(PE, last)], inc=(DV, 1)))
            b = bw(bB)
            last2 = None
            for dc in range(DC):
                last2 = kb.emit("tensor", lambda t, dc_=dc, b_=bB: t.matmul(
                    pb[b_][0:1, :], ones_sb[:], ub[:, dc_, :],
                    start=(dc_ == 0), stop=(dc_ == DC - 1)),
                    waits=[(DV, sqs[dc]), b if dc == 0 else None],
                    inc=(PE, 1) if dc == DC - 1 else None)
            ln_war[ub.name] = (PE, last2)
            s2 = kb.emit("vector", lambda v, b_=bB: v.tensor_copy(rQ[:], pb[b_][0:1, :]),
                         waits=[(PE, last2)], inc=(DV, 1))
            bank_free[bB] = (DV, s2)
            return dict(s1=s1, s2=s2)

        def emit_ln_b(ctx, out_cb, rows, banks, bcs):
            rS, rQ, rT, rU = rows
            bA, bB = banks
            bcA, bcB = bcs
            kb.emit("vector", lambda v: v.tensor_scalar(
                rS[:], rS[:], 1.0 / D, None, op0=ALU.mult),
                waits=[(DV, ctx["s1"])], inc=(DV, 1))
            kb.emit("vector", lambda v: v.tensor_scalar(
                rQ[:], rQ[:], 1.0 / D, None, op0=ALU.mult),
                waits=[(DV, ctx["s2"])], inc=(DV, 1))
            kb.emit("vector", lambda v: v.tensor_mul(rT[:], rS[:], rS[:]),
                    waits=[(DV, kb.at(DV))], inc=(DV, 1))
            vv = kb.emit("vector", lambda v: v.tensor_sub(rQ[:], rQ[:], rT[:]),
                         waits=[(DV, kb.at(DV))], inc=(DV, 1))
            a1 = kb.emit("scalar", lambda sc: sc.activation(
                rT[:], rQ[:], AF.Ln, bias=f32(1e-5)),
                waits=[(DV, vv)], inc=(AC, 1))
            a2 = kb.emit("scalar", lambda sc: sc.activation(
                rU[:], rT[:], AF.Exp, scale=f32(-0.5)),
                waits=[(AC, a1)], inc=(AC, 1))
            mr = kb.emit("vector", lambda v: v.tensor_mul(rT[:], rS[:], rU[:]),
                         waits=[(AC, a2)], inc=(DV, 1))
            b = bw(bA)
            p1 = kb.emit("tensor", lambda t, b_=bA: t.matmul(
                pb[b_][:], ones1f_sb[:], rU[:], start=True, stop=True),
                waits=[(AC, a2), b], inc=(PE, 1))
            b1 = kb.emit("vector", lambda v, b_=bA: v.tensor_copy(bcA[:], pb[b_][:]),
                         waits=[(PE, p1)], inc=(DV, 1))
            bank_free[bA] = (DV, b1)
            b = bw(bB)
            p2 = kb.emit("tensor", lambda t, b_=bB: t.matmul(
                pb[b_][:], ones1f_sb[:], rT[:], start=True, stop=True),
                waits=[(DV, mr), b], inc=(PE, 1))
            b2 = kb.emit("vector", lambda v, b_=bB: v.tensor_copy(bcB[:], pb[b_][:]),
                         waits=[(PE, p2)], inc=(DV, 1))
            bank_free[bB] = (DV, b2)
            for dc in range(DC):
                out_cb(dc, b1, b2)

        dma(SC, lambda s: s.dma_start(ones_sb[:], ones128))
        dma(SC, lambda s: s.dma_start(ones1f_sb[:], ones1f))
        dma(SC, lambda s: s.dma_start(bq_sb[:], bq))
        c0 = kb.at(SC)

        ph12 = ExitStack()
        with ph12:
            q2T = ph12.enter_context(nc.sbuf_tensor("q2T", [128, DC, TOK], BF16))
            ksf = ph12.enter_context(nc.sbuf_tensor("ksf", [128, DC, 512], F32))
            kg = [ph12.enter_context(nc.sbuf_tensor(f"kg{i}", [128, EC, 1024], BF16))
                  for i in range(2)]
            vg = [ph12.enter_context(nc.sbuf_tensor(f"vg{i}", [128, 8, 1024], BF16))
                  for i in range(2)]
            vg_ld = {}
            vg_free = [None, None]
            kg_rdy = {}
            ksf_free = None
            cs_parts = {"n": 0}

            def emit_kg(qb, g, i, kg_free_i):
                nonlocal ksf_free
                for h in range(2):
                    ld = dma(SD, lambda s, g_=g, h_=h: s.dma_start(
                        ksf[:], xT_r[:, :, g_ * 1024 + h_ * 512:
                                      g_ * 1024 + (h_ + 1) * 512]),
                        waits=[ksf_free])
                    casts = []
                    for dc in range(DC):
                        ws = []
                        if dc == 0:
                            ws = [(SD, ld)] + ([kg_free_i] if h == 0 else [])
                        casts.append(kb.emit(
                            "scalar", lambda sc, dc_=dc, h_=h, i_=i: sc.activation(
                                kg[i_][:, dc_, h_ * 512:(h_ + 1) * 512],
                                ksf[:, dc_, :], AF.Copy),
                            waits=ws, inc=(AC, 1)))
                    ksf_free = [(AC, casts[-1])]
                    if qb == 0:
                        rd = kb.emit("vector", lambda v: v.reduce_sum(
                            cs_tmp[:], ksf[:], axis=AX.X),
                            waits=[(SD, ld)], inc=(DV, 1))
                        if cs_parts["n"] == 0:
                            kb.emit("vector", lambda v: v.tensor_copy(
                                cs_sb[:], cs_tmp[:]),
                                waits=[(DV, rd)], inc=(DV, 1))
                        else:
                            kb.emit("vector", lambda v: v.tensor_add(
                                cs_sb[:], cs_sb[:], cs_tmp[:]),
                                waits=[(DV, rd)], inc=(DV, 1))
                        cs_parts["n"] += 1
                        if cs_parts["n"] == 2 * NG:
                            kb.emit("vector", lambda v: v.tensor_scalar(
                                cs_sb[:], cs_sb[:], 2.0 / S, None, op0=ALU.mult),
                                waits=[(DV, kb.at(DV))], inc=(DV, 1))
                        ksf_free.append((DV, rd))
                kg_rdy[(qb, g)] = (AC, kb.at(AC))

            ph1 = ExitStack()
            with ph1:
                wq_sb = ph1.enter_context(nc.sbuf_tensor("wq_sb", [128, DC, D], BF16))
                wk2_sb = ph1.enter_context(nc.sbuf_tensor("wk2_sb", [128, DC, D], BF16))
                qTb = [ph1.enter_context(nc.sbuf_tensor(f"qTb{i}", [128, EC, 512], BF16))
                       for i in range(2)]
                xq = [ph1.enter_context(nc.sbuf_tensor(f"xq{i}", [128, DC, 512], BF16))
                      for i in range(2)]

                wready = dma(GD, lambda g: g.dma_start(wq_sb[:], wqT_r))
                w2ready = dma(GD, lambda g: g.dma_start(wk2_sb[:], wk2_r))

                lds = {}
                qp_dv = {}
                q2_dv = {}
                q2_lpe = {}
                for t in range(2):
                    lds[t] = dma(GD, lambda g, t_=t: g.dma_start(
                        xq[t_][:], xTo_r[:, :, t_ * 512:(t_ + 1) * 512]), waits=[])

                def emit_qp(t):
                    i = t % 2
                    lpe = None
                    for ec in range(EC):
                        bi = ec % 2
                        last_mm = None
                        for dc in range(DC):
                            ws = []
                            if dc == 0:
                                ws = [(GD, lds[t]), (GD, wready), bw(bi)]
                            last_mm = kb.emit(
                                "tensor",
                                lambda tn, ec_=ec, dc_=dc, i_=i, b_=bi: tn.matmul(
                                    pb[b_][:], wq_sb[:, dc_, ec_ * 128:(ec_ + 1) * 128],
                                    xq[i_][:, dc_, :], start=(dc_ == 0), stop=(dc_ == 7)),
                                waits=ws, inc=((PE, 1) if dc == 7 else None))
                        lpe = last_mm
                        ws2 = [(PE, last_mm), (SC, c0)]
                        if ec == 0 and t >= 2:
                            ws2.append((PE, q2_lpe[t - 2]))
                        ev = kb.emit(
                            "vector",
                            lambda v, ec_=ec, b_=bi, i_=i: v.tensor_scalar(
                                qTb[i_][:, ec_, :], pb[b_][:],
                                bq_sb[:, ec_:ec_ + 1], None, op0=ALU.add),
                            waits=ws2, inc=(DV, 1))
                        bank_free[bi] = (DV, ev)
                    qp_dv[t] = kb.at(DV)
                    if t + 2 < QB:
                        lds[t + 2] = dma(GD, lambda g, t_=t + 2, i_=i: g.dma_start(
                            xq[i_][:], xTo_r[:, :, t_ * 512:(t_ + 1) * 512]),
                            waits=[(PE, lpe)])

                def emit_q2(t):
                    i = t % 2
                    for dc in range(DC):
                        b2 = 2 + dc % 2
                        last_mm = None
                        for ec in range(EC):
                            ws = []
                            if ec == 0:
                                ws = [(DV, qp_dv[t]), (GD, w2ready), bw(b2)]
                            last_mm = kb.emit(
                                "tensor",
                                lambda tn, ec_=ec, dc_=dc, b_=b2, i_=i: tn.matmul(
                                    pb[b_][:], wk2_sb[:, ec_, dc_ * 128:(dc_ + 1) * 128],
                                    qTb[i_][:, ec_, :],
                                    start=(ec_ == 0), stop=(ec_ == 7)),
                                waits=ws, inc=((PE, 1) if ec == 7 else None))
                        ev = kb.emit(
                            "vector",
                            lambda v, dc_=dc, b_=b2, t_=t: v.tensor_copy(
                                q2T[:, dc_, t_ * 512:(t_ + 1) * 512], pb[b_][:]),
                            waits=[(PE, last_mm)], inc=(DV, 1))
                        bank_free[b2] = (DV, ev)
                    q2_dv[t] = kb.at(DV)
                    q2_lpe[t] = kb.at(PE)

                emit_qp(0)
                emit_qp(1)
                emit_q2(0)
                emit_qp(2)
                emit_q2(1)
                emit_qp(3)
                emit_q2(2)
                emit_q2(3)

                for g0 in range(2):
                    vg_ld[(0, g0)] = dma(GD, lambda gp, g_=g0: gp.dma_start(
                        vg[g_ % 2][:], xN_r[:, g_ * 8:(g_ + 1) * 8, :]),
                        waits=[])

            ph1_pe = kb.at(PE)

            ph2 = ExitStack()
            with ph2:
                Pt = [ph2.enter_context(nc.sbuf_tensor(f"Pt{i}", [128, 8, 512], BF16))
                      for i in range(2)]
                Obuf = [ph2.enter_context(nc.sbuf_tensor(f"O{i}", [128, DC, 512], F32))
                        for i in range(2)]
                xow = ph2.enter_context(nc.sbuf_tensor("xow", [128, DC, 512], F32))
                ub2 = ph2.enter_context(nc.sbuf_tensor("ub2", [128, DC, 512], BF16))
                y1f_st = [ph2.enter_context(nc.sbuf_tensor(f"y1f{i}", [128, 512], F32))
                          for i in range(2)]
                y1b_st = [ph2.enter_context(nc.sbuf_tensor(f"y1b{i}", [128, 512], BF16))
                          for i in range(2)]

                kg_free = [None, None]
                Pt_free = [None, None]
                y1f_free = [None, None]
                y1b_free = [None, None]
                exp_done = {}
                kg_ld = {}
                xow_ld = {}
                qb_lpe = {}
                epi_udv = {}
                epi_ctx = {}
                xc_dv = {}

                def emit_xc(qb):
                    last = None
                    for dc in range(DC):
                        last = kb.emit("vector", lambda v, dc_=dc: v.tensor_scalar(
                            xow[:, dc_, :], xow[:, dc_, :], 2.0, cs_sb[:, dc_:dc_ + 1],
                            op0=ALU.mult, op1=ALU.subtract),
                            waits=[(SD, xow_ld[qb]), (DV, kb.at(DV))], inc=(DV, 1))
                    xc_dv[qb] = last

                def emit_slot(qb, g):
                    O = Obuf[qb % 2]
                    lb4 = 4 if qb % 2 == 0 else 7
                    if g < NG:
                        i = g % 2
                        emit_kg(qb, g, i, kg_free[i])
                        if (qb, g) not in vg_ld:
                            vg_ld[(qb, g)] = dma(GD, lambda gp, g_=g, i_=i:
                                gp.dma_start(vg[i_][:],
                                             xN_r[:, g_ * 8:(g_ + 1) * 8, :]),
                                waits=[vg_free[i]])
                        for tt in range(8):
                            lb = tt % 2
                            last = None
                            for ec in range(EC):
                                ws = []
                                if ec == 0:
                                    ws = [kg_rdy[(qb, g)], bw(lb),
                                          (DV, q2_dv[qb])]
                                last = kb.emit(
                                    "tensor",
                                    lambda t, ec_=ec, tt_=tt, i_=i, lb_=lb, qb_=qb:
                                    t.matmul(
                                        pb[lb_][:],
                                        kg[i_][:, ec_, tt_ * 128:(tt_ + 1) * 128],
                                        q2T[:, ec_, qb_ * 512:(qb_ + 1) * 512],
                                        start=(ec_ == 0), stop=(ec_ == 7)),
                                    waits=ws, inc=((PE, 1) if ec == 7 else None))
                            ws = [(PE, last)]
                            if tt == 0:
                                ws.append(Pt_free[i])
                            ex = kb.emit(
                                "scalar",
                                lambda sc, tt_=tt, i_=i, lb_=lb: sc.activation(
                                    Pt[i_][:, tt_, :], pb[lb_][:], AF.Exp,
                                    bias=f32(-CEXP), scale=f32(1.0 / 32.0)),
                                waits=ws, inc=(AC, 1))
                            exp_done[(qb, g, tt)] = ex
                            bank_free[lb] = (AC, ex)
                    if g > 0:
                        gp_ = g - 1
                        i = gp_ % 2
                        for tt in range(8):
                            ws = [(AC, exp_done[(qb, gp_, tt)])]
                            if gp_ == 0 and tt == 0:
                                ws.append(bw(lb4))
                            kb.emit(
                                "tensor",
                                lambda t, tt_=tt, i_=i, lb_=lb4,
                                       st_=(gp_ == 0 and tt == 0),
                                       sp_=(gp_ == NG - 1 and tt == 7): t.matmul(
                                    pb[lb_][0:1, :], ones_sb[:], Pt[i_][:, tt_, :],
                                    start=st_, stop=sp_),
                                waits=ws, inc=((PE, 1) if tt == 7 else None))
                        if gp_ == NG - 1:
                            qb_lpe[qb] = kb.at(PE)
                        first_half = (gp_ % 2 == 0) and PAIR_O
                        for dc in range(DC):
                            ob = 2 + dc % 2
                            last = None
                            for tt in range(8):
                                ws = []
                                if tt == 0:
                                    ws = [(AC, exp_done[(qb, gp_, 7)]),
                                          (GD, vg_ld[(qb, gp_)])]
                                    if first_half or not PAIR_O:
                                        ws.append(bw(ob))
                                last = kb.emit(
                                    "tensor",
                                    lambda t, dc_=dc, tt_=tt, i_=i, ob_=ob,
                                           st_=(tt == 0 and (first_half or not PAIR_O)),
                                           sp_=((not first_half) and tt == 7): t.matmul(
                                        pb[ob_][:],
                                        vg[i_][:, tt_, dc_ * 128:(dc_ + 1) * 128],
                                        Pt[i_][:, tt_, :],
                                        start=st_, stop=sp_),
                                    waits=ws, inc=((PE, 1) if tt == 7 else None))
                            if not first_half:
                                pair = gp_ // 2 if PAIR_O else gp_
                                if pair == 0:
                                    ev = kb.emit("vector",
                                                 lambda v, dc_=dc, ob_=ob, O_=O: v.tensor_copy(
                                                     O_[:, dc_, :], pb[ob_][:]),
                                                 waits=[(PE, last)], inc=(DV, 1))
                                else:
                                    ev = kb.emit("vector",
                                                 lambda v, dc_=dc, ob_=ob, O_=O: v.tensor_add(
                                                     O_[:, dc_, :], O_[:, dc_, :], pb[ob_][:]),
                                                 waits=[(PE, last)], inc=(DV, 1))
                                bank_free[ob] = (DV, ev)
                        Pt_free[i] = (PE, kb.at(PE))
                        vg_free[i] = (PE, kb.at(PE))
                    if g < NG:
                        kg_free[g % 2] = (PE, kb.at(PE))

                def emit_epi_a(qb):
                    O = Obuf[qb % 2]
                    lb4 = 4 if qb % 2 == 0 else 7
                    c1 = kb.emit("vector", lambda v, lb_=lb4: v.tensor_copy(
                        rC[:], pb[lb_][0:1, :]),
                        waits=[(PE, qb_lpe[qb])], inc=(DV, 1))
                    bank_free[lb4] = (DV, c1)
                    p1 = kb.emit("tensor", lambda t: t.matmul(
                        pb[5][:], ones1f_sb[:], rC[:], start=True, stop=True),
                        waits=[(DV, c1), bw(5)], inc=(PE, 1))
                    c3 = kb.emit("vector", lambda v: v.reciprocal(bc0[:], pb[5][:]),
                                 waits=[(PE, p1)], inc=(DV, 1))
                    bank_free[5] = (DV, c3)
                    for dc in range(DC):
                        kb.emit("vector", lambda v, dc_=dc, O_=O: v.tensor_mul(
                            O_[:, dc_, :], O_[:, dc_, :], bc0[:]),
                            waits=[(DV, c3), (DV, xc_dv[qb])], inc=(DV, 1))
                        kb.emit("vector", lambda v, dc_=dc, O_=O: v.tensor_add(
                            O_[:, dc_, :], O_[:, dc_, :], xow[:, dc_, :]),
                            waits=[(DV, kb.at(DV))], inc=(DV, 1))
                    epi_udv[qb] = kb.at(DV)
                    if qb + 1 < QB:
                        xow_ld[qb + 1] = dma(CD, lambda s, qb_=qb + 1: s.dma_start(
                            xow[:], xTo_r[:, :, qb_ * 512:(qb_ + 1) * 512]),
                            waits=[(DV, epi_udv[qb])])

                def emit_epi_b1(qb):
                    O = Obuf[qb % 2]
                    epi_ctx[qb] = emit_ln_a(lambda dc: O[:, dc, :], ub2,
                                            [(DV, epi_udv[qb])], (rA, rB), (5, 6))

                def emit_epi_b2(qb):
                    O = Obuf[qb % 2]

                    def y1_out(dc, b1, b2, qb_=qb, O_=O):
                        i = dc % 2
                        w1 = kb.emit("vector", lambda v, dc_=dc, i_=i: v.tensor_mul(
                            y1f_st[i_][:], O_[:, dc_, :], bc0[:]),
                            waits=[(DV, max(b1, b2)), y1f_free[i]], inc=(DV, 1))
                        w2 = kb.emit("vector", lambda v, i_=i: v.tensor_sub(
                            y1f_st[i_][:], y1f_st[i_][:], bc1[:]),
                            waits=[(DV, w1)], inc=(DV, 1))
                        cb = kb.emit("vector", lambda v, i_=i: v.tensor_copy(
                            y1b_st[i_][:], y1f_st[i_][:]),
                            waits=[(DV, w2), y1b_free[i]], inc=(DV, 1))
                        d1 = dma(CD, lambda s, dc_=dc, i_=i: s.dma_start(
                            y1f_r[:, dc_, qb_ * 512:(qb_ + 1) * 512], y1f_st[i_][:]),
                            waits=[(DV, w2)])
                        d2 = dma(CD, lambda s, dc_=dc, i_=i: s.dma_start(
                            y1b_h[dc_ * 128:(dc_ + 1) * 128,
                                  qb_ * 512:(qb_ + 1) * 512], y1b_st[i_][:]),
                            waits=[(DV, cb)])
                        y1f_free[i] = [(CD, d1), (DV, cb)]
                        y1b_free[i] = (CD, d2)

                    emit_ln_b(epi_ctx[qb], y1_out, (rA, rB, rC, rD), (5, 6), (bc0, bc1))

                xow_ld[0] = dma(CD, lambda s: s.dma_start(
                    xow[:], xTo_r[:, :, 0:512]), waits=[(PE, ph1_pe)])
                for qb in range(QB):
                    for g in range(0 if qb == 0 else 3, NG + 1):
                        emit_slot(qb, g)
                        if qb == 0 and g == NG - 1:
                            emit_xc(0)
                    if qb + 1 < QB:
                        emit_slot(qb + 1, 0)
                        emit_epi_a(qb)
                        emit_slot(qb + 1, 1)
                        emit_epi_b1(qb)
                        emit_slot(qb + 1, 2)
                        emit_epi_b2(qb)
                        emit_xc(qb + 1)
                    else:
                        emit_epi_a(qb)
                        emit_epi_b1(qb)
                        emit_epi_b2(qb)
                kb.barrier()
        ph34 = ExitStack()
        with ph34:
            acc = ph34.enter_context(nc.sbuf_tensor("acc", [128, DC, TOK], F32))
            ph3 = ExitStack()
            with ph3:
                y1s = [ph3.enter_context(nc.sbuf_tensor(f"y1s{i}", [128, TOK], BF16))
                       for i in range(2)]
                wib = [ph3.enter_context(nc.sbuf_tensor(f"wib{i}", [128, DC, 128], BF16))
                       for i in range(2)]
                wsb2 = [ph3.enter_context(nc.sbuf_tensor(f"wsb{i}", [128, DC, 128], BF16))
                        for i in range(2)]
                wob = [ph3.enter_context(nc.sbuf_tensor(f"wob{i}", [128, 8, 128], BF16))
                       for i in range(2)]
                gbt = ph3.enter_context(nc.sbuf_tensor("gbt", [128, 4, 512], BF16))
                hm = [ph3.enter_context(nc.sbuf_tensor(f"hm{i}", [128, 512], BF16))
                      for i in range(2)]
                rt_sb = ph3.enter_context(nc.sbuf_tensor("rt_sb", [128, DC, 8], BF16))
                sel_sb = ph3.enter_context(nc.sbuf_tensor("sel_sb", [8, E * 128], F32))
                id128_sb = ph3.enter_context(nc.sbuf_tensor("id128_sb", [128, 128], F32))
                id8_sb = ph3.enter_context(nc.sbuf_tensor("id8_sb", [8, 8], F32))
                gatesT = ph3.enter_context(nc.sbuf_tensor("gatesT", [8, TOK], F32))

                dma(SC, lambda s: s.dma_start(sel_sb[:], sel8))
                dma(SC, lambda s: s.dma_start(id128_sb[:], id128f))
                dma(SC, lambda s: s.dma_start(id8_sb[:], id8f))
                cph3 = kb.at(SC)

                y1s_free = [None, None]
                wib_free = [None, None]
                wsb_free = [None, None]
                wob_free = [None, None]
                hm_free = [None, None]

                gsc = ExitStack()
                with gsc:
                    scoresT = gsc.enter_context(nc.sbuf_tensor("scoresT", [8, TOK], F32))
                    s_tok = gsc.enter_context(nc.sbuf_tensor("s_tok", [128, 16, 8], F32))
                    g_tok = gsc.enter_context(nc.sbuf_tensor("g_tok", [128, 16, 8], F32))
                    m1_sb = [gsc.enter_context(nc.sbuf_tensor(f"m1_sb{i}", [128, 1], F32)) for i in range(2)]
                    m2_sb = [gsc.enter_context(nc.sbuf_tensor(f"m2_sb{i}", [128, 1], F32)) for i in range(2)]
                    msk_sb = [gsc.enter_context(nc.sbuf_tensor(f"msk_sb{i}", [128, 8], F32)) for i in range(2)]
                    sk2_sb = [gsc.enter_context(nc.sbuf_tensor(f"sk2_sb{i}", [128, 8], F32)) for i in range(2)]
                    sig_sb = [gsc.enter_context(nc.sbuf_tensor(f"sig_sb{i}", [128, 8], F32)) for i in range(2)]

                    rt_ld = dma(GD, lambda g: g.dma_start(rt_sb[:], rtr_r))
                    last_rt = {}
                    for dc in range(DC):
                        i = dc % 2
                        ld = dma(SD, lambda s, dc_=dc, i_=i: s.dma_start(
                            y1s[i_][:], y1b_h[dc_ * 128:(dc_ + 1) * 128, :]),
                            waits=(y1s_free[i] or []))
                        for at in range(4):
                            ws = [(SD, ld), (GD, rt_ld)]
                            if dc == 0:
                                ws.append(bw(at))
                            last_rt[at] = kb.emit(
                                "tensor",
                                lambda t, dc_=dc, at_=at, i_=i: t.matmul(
                                    pb[at_][0:8, :], rt_sb[:, dc_, :],
                                    y1s[i_][:, at_ * 512:(at_ + 1) * 512],
                                    start=(dc_ == 0), stop=(dc_ == 7)),
                                waits=ws, inc=(PE, 1) if at == 3 else None)
                        y1s_free[i] = [(PE, kb.at(PE))]
                    rt_pe = kb.at(PE)
                    for at in range(4):
                        ev = kb.emit("vector", lambda v, at_=at: v.tensor_copy(
                            scoresT[:, at_ * 512:(at_ + 1) * 512], pb[at_][0:8, :]),
                            waits=[(PE, rt_pe)], inc=(DV, 1))
                        bank_free[at] = (DV, ev)
                    sc_dv = kb.at(DV)

                    for j in range(16):
                        bk4 = 4 + j % 2
                        tp = kb.emit("tensor", lambda t, j_=j, b_=bk4: t.transpose(
                            pb[b_][:, 0:8], scoresT[:, j_ * 128:(j_ + 1) * 128], id8_sb[:]),
                            waits=[(DV, sc_dv), (SC, cph3), bw(bk4)], inc=(PE, 1))
                        cv = kb.emit("vector", lambda v, j_=j, b_=bk4: v.tensor_copy(
                            s_tok[:, j_, :], pb[b_][:, 0:8]),
                            waits=[(PE, tp)], inc=(DV, 1))
                        bank_free[bk4] = (DV, cv)
                    stok_dv = kb.at(DV)
                    gmul = {}
                    for j in range(16):
                        p = j % 2
                        r1 = kb.emit("vector", lambda v, j_=j, p_=p: v.reduce_max(
                            m1_sb[p_][:], s_tok[:, j_, :], axis=AX.X),
                            waits=[(DV, stok_dv if j < 2 else gmul[j - 2])], inc=(DV, 1))
                        k1 = kb.emit("vector", lambda v, j_=j, p_=p: v.tensor_scalar(
                            msk_sb[p_][:], s_tok[:, j_, :], m1_sb[p_][:, 0:1], None,
                            op0=ALU.is_ge),
                            waits=[(DV, r1)], inc=(DV, 1))
                        k2 = kb.emit("vector", lambda v, j_=j, p_=p: v.scalar_tensor_tensor(
                            sk2_sb[p_][:], msk_sb[p_][:], -1e30, s_tok[:, j_, :],
                            op0=ALU.mult, op1=ALU.add),
                            waits=[(DV, k1)], inc=(DV, 1))
                        r2 = kb.emit("vector", lambda v, p_=p: v.reduce_max(
                            m2_sb[p_][:], sk2_sb[p_][:], axis=AX.X),
                            waits=[(DV, k2)], inc=(DV, 1))
                        k3 = kb.emit("vector", lambda v, j_=j, p_=p: v.tensor_scalar(
                            msk_sb[p_][:], s_tok[:, j_, :], m2_sb[p_][:, 0:1], None,
                            op0=ALU.is_ge),
                            waits=[(DV, r2)], inc=(DV, 1))
                        sg = kb.emit("scalar", lambda sc, j_=j, p_=p: sc.activation(
                            sig_sb[p_][:], s_tok[:, j_, :], AF.Sigmoid),
                            waits=[(DV, stok_dv if j < 2 else gmul[j - 2])], inc=(AC, 1))
                        gmul[j] = kb.emit("vector", lambda v, j_=j, p_=p: v.tensor_mul(
                            g_tok[:, j_, :], sig_sb[p_][:], msk_sb[p_][:]),
                            waits=[(DV, k3), (AC, sg)], inc=(DV, 1))
                    for j in range(16):
                        bk4 = 4 + j % 2
                        tp = kb.emit("tensor", lambda t, j_=j, b_=bk4: t.transpose(
                            pb[b_][0:8, 0:128], g_tok[:, j_, :], id128_sb[:]),
                            waits=[(DV, gmul[j]), bw(bk4)], inc=(PE, 1))
                        cv = kb.emit("vector", lambda v, j_=j, b_=bk4: v.tensor_copy(
                            gatesT[:, j_ * 128:(j_ + 1) * 128], pb[b_][0:8, 0:128]),
                            waits=[(PE, tp)], inc=(DV, 1))
                        bank_free[bk4] = (DV, cv)
                    gat_dv = kb.at(DV)
                    if debug >= 2:
                        dma(SC, lambda s: s.dma_start(gat_h, gatesT[:]),
                            waits=[(DV, gat_dv)])
                        dma(SC, lambda s: s.dma_start(sco_h, scoresT[:]),
                            waits=[(DV, gat_dv)])
                        dma(SC, lambda s: s.dma_start(
                            stok_h, s_tok[:].rearrange("p a b -> p (a b)")),
                            waits=[(DV, gat_dv)])
                        dma(SC, lambda s: s.dma_start(
                            gtok_h, g_tok[:].rearrange("p a b -> p (a b)")),
                            waits=[(DV, gat_dv)])
                routed = ph3.enter_context(nc.sbuf_tensor("routed", [128, DC, TOK], BF16))
                hbuf = ph3.enter_context(nc.sbuf_tensor("hbuf", [128, 8, TOK], BF16))

                routed_war = None
                h_war = None
                gb_dv = {}
                routed_dv = 0

                def emit_gb_routed(e):
                    nonlocal routed_dv
                    if e < 8:
                        for at in range(4):
                            bnk = 4 + at % 2
                            pp = kb.emit("tensor", lambda t, e_=e, at_=at, b_=bnk: t.matmul(
                                pb[b_][:], sel_sb[:, e_ * 128:(e_ + 1) * 128],
                                gatesT[:, at_ * 512:(at_ + 1) * 512],
                                start=True, stop=True),
                                waits=[(DV, gat_dv), bw(bnk)], inc=(PE, 1))
                            gb_dv[at] = kb.emit("vector", lambda v, at_=at, b_=bnk:
                                v.tensor_copy(gbt[:, at_, :], pb[b_][:]),
                                waits=[(PE, pp)], inc=(DV, 1))
                            bank_free[bnk] = (DV, gb_dv[at])
                    for dc in range(DC):
                        i = dc % 2
                        ld = dma(SD, lambda s, dc_=dc, i_=i: s.dma_start(
                            y1s[i_][:], y1b_h[dc_ * 128:(dc_ + 1) * 128, :]),
                            waits=(y1s_free[i] or []))
                        if e < 8:
                            lastv = None
                            for at in range(4):
                                lastv = kb.emit("vector",
                                    lambda v, dc_=dc, at_=at, i_=i: v.tensor_mul(
                                        routed[:, dc_, at_ * 512:(at_ + 1) * 512],
                                        y1s[i_][:, at_ * 512:(at_ + 1) * 512],
                                        gbt[:, at_, :]),
                                    waits=[(SD, ld), (DV, gb_dv[at]), routed_war],
                                    inc=(DV, 1))
                        else:
                            lastv = kb.emit("vector", lambda v, dc_=dc, i_=i: v.tensor_copy(
                                routed[:, dc_, :], y1s[i_][:]),
                                waits=[(SD, ld), routed_war], inc=(DV, 1))
                        y1s_free[i] = [(DV, lastv)]
                    routed_dv = kb.at(DV)

                emit_gb_routed(0)
                for e in range(9):
                    for half in range(2):
                        for fcl in range(8):
                            fc = half * 8 + fcl
                            i = fc % 2
                            ldw = dma(GD, lambda g, e_=e, fc_=fc, i_=i: g.dma_start(
                                wib[i_][:], win_r[e_][:, :, fc_ * 128:(fc_ + 1) * 128]),
                                waits=[wib_free[i]])
                            lds = dma(GD, lambda g, e_=e, fc_=fc, i_=i: g.dma_start(
                                wsb2[i_][:], wsw_r[e_][:, :, fc_ * 128:(fc_ + 1) * 128]),
                                waits=[wsb_free[i]])
                            for at in range(4):
                                lastm = lasts = None
                                for dc in range(DC):
                                    ws = []
                                    if dc == 0:
                                        ws = [(GD, lds), (DV, routed_dv), bw(at)]
                                    lastm = kb.emit("tensor",
                                        lambda t, at_=at, dc_=dc, i_=i: t.matmul(
                                            pb[at_][:], wib[i_][:, dc_, :],
                                            routed[:, dc_, at_ * 512:(at_ + 1) * 512],
                                            start=(dc_ == 0), stop=(dc_ == 7)),
                                        waits=ws, inc=(PE, 1) if dc == 7 else None)
                                    ws2 = [bw(4 + at)] if dc == 0 else []
                                    lasts = kb.emit("tensor",
                                        lambda t, at_=at, dc_=dc, i_=i: t.matmul(
                                            pb[4 + at_][:], wsb2[i_][:, dc_, :],
                                            routed[:, dc_, at_ * 512:(at_ + 1) * 512],
                                            start=(dc_ == 0), stop=(dc_ == 7)),
                                        waits=ws2, inc=(PE, 1) if dc == 7 else None)
                                j = (fc * 4 + at) % 2
                                sl = kb.emit("scalar", lambda sc, at_=at, j_=j: sc.activation(
                                    hm[j_][:], pb[at_][:], AF.Silu),
                                    waits=[(PE, lastm), hm_free[j]], inc=(AC, 1))
                                bank_free[at] = (AC, sl)
                                hmv = kb.emit("vector",
                                    lambda v, fcl_=fcl, at_=at, j_=j: v.tensor_mul(
                                        hbuf[:, fcl_, at_ * 512:(at_ + 1) * 512],
                                        hm[j_][:], pb[4 + at_][:]),
                                    waits=[(AC, sl), (PE, lasts), h_war], inc=(DV, 1))
                                bank_free[4 + at] = (DV, hmv)
                                hm_free[j] = (DV, hmv)
                            wib_free[i] = (PE, kb.at(PE))
                            wsb_free[i] = (PE, kb.at(PE))
                        h_dv = kb.at(DV)
                        if half == 1:
                            routed_war = (PE, kb.at(PE))
                            if e < 8:
                                emit_gb_routed(e + 1)
                        for dc in range(DC):
                            i = dc % 2
                            ld1 = dma(GD, lambda g, e_=e, h_=half, dc_=dc, i_=i: g.dma_start(
                                wob[i_][:], wout_r[e_][:, h_ * 8:(h_ + 1) * 8,
                                                       dc_ * 128:(dc_ + 1) * 128]),
                                waits=[wob_free[i]])
                            for at in range(4):
                                bnk = (dc % 2) * 4 + at
                                last = None
                                for fcl in range(8):
                                    ws = []
                                    if fcl == 0:
                                        ws = [(GD, ld1), (DV, h_dv), bw(bnk)]
                                    last = kb.emit("tensor",
                                        lambda t, at_=at, fc_=fcl, i_=i, b_=bnk:
                                        t.matmul(
                                            pb[b_][:], wob[i_][:, fc_, :],
                                            hbuf[:, fc_, at_ * 512:(at_ + 1) * 512],
                                            start=(fc_ == 0), stop=(fc_ == 7)),
                                        waits=ws, inc=(PE, 1) if fcl == 7 else None)
                                if e == 0 and half == 0:
                                    ev = kb.emit("vector",
                                        lambda v, dc_=dc, at_=at, b_=bnk: v.tensor_copy(
                                            acc[:, dc_, at_ * 512:(at_ + 1) * 512], pb[b_][:]),
                                        waits=[(PE, last)], inc=(DV, 1))
                                else:
                                    ev = kb.emit("vector",
                                        lambda v, dc_=dc, at_=at, b_=bnk: v.tensor_add(
                                            acc[:, dc_, at_ * 512:(at_ + 1) * 512],
                                            acc[:, dc_, at_ * 512:(at_ + 1) * 512], pb[b_][:]),
                                        waits=[(PE, last)], inc=(DV, 1))
                                bank_free[bnk] = (DV, ev)
                            wob_free[i] = (PE, kb.at(PE))
                        h_war = (PE, kb.at(PE))
                if debug >= 2:
                    acc_dv = kb.at(DV)
                    for dc in range(DC):
                        dma(SD, lambda s, dc_=dc: s.dma_start(
                            acc_h.rearrange("(c p) t -> p c t", p=128)[:, dc_, :],
                            acc[:, dc_, :]), waits=[(DV, acc_dv)])
                kb.barrier()

            ph4 = ExitStack()
            with ph4:
                y1fb = [ph4.enter_context(nc.sbuf_tensor(f"y1fb{i}", [128, DC, 512], F32))
                        for i in range(2)]
                x2 = [ph4.enter_context(nc.sbuf_tensor(f"x2_{i}", [128, DC, 512], F32))
                      for i in range(2)]
                ub4 = [ph4.enter_context(nc.sbuf_tensor(f"ub4_{i}", [128, DC, 512], BF16))
                       for i in range(2)]
                zst = [ph4.enter_context(nc.sbuf_tensor(f"zst{i}", [128, 512], F32))
                       for i in range(2)]
                rows2 = tuple(ph4.enter_context(nc.sbuf_tensor(f"r2_{i}", [1, 512], F32))
                              for i in range(4))
                bc2 = ph4.enter_context(nc.sbuf_tensor("bc2", [128, 512], F32))
                bc3 = ph4.enter_context(nc.sbuf_tensor("bc3", [128, 512], F32))
                y1fb_free = [None, None]
                zst_free = [None, None]
                x2_war = [None, None]
                ctx4 = {}
                x2dv4 = {}

                rowsets = [(rA, rB, rC, rD), rows2]
                banksets = [(5, 6), (4, 7)]
                bcsets = [(bc0, bc1), (bc2, bc3)]

                def emit_x2(at):
                    i = at % 2
                    ld = dma(SD, lambda s, at_=at, i_=i: s.dma_start(
                        y1fb[i_][:], y1f_r[:, :, at_ * 512:(at_ + 1) * 512]),
                        waits=(y1fb_free[i] or []))
                    last = None
                    for dc in range(DC):
                        last = kb.emit("vector", lambda v, dc_=dc, at_=at, i_=i: v.tensor_add(
                            x2[i_][:, dc_, :], y1fb[i_][:, dc_, :],
                            acc[:, dc_, at_ * 512:(at_ + 1) * 512]),
                            waits=[(SD, ld)] + ([x2_war[i]] if dc == 0 else []),
                            inc=(DV, 1))
                    x2dv4[at] = last
                    y1fb_free[i] = [(DV, last)]

                def emit_lna4(at):
                    i = at % 2
                    ctx4[at] = emit_ln_a(lambda dc: x2[i][:, dc, :], ub4[i],
                                         [(DV, x2dv4[at])], rowsets[i][0:2], banksets[i])

                def emit_lnb4(at):
                    i = at % 2

                    def z_out(dc, b1, b2, at_=at, i_=i):
                        j = dc % 2
                        bcA, bcB = bcsets[i_]
                        w1 = kb.emit("vector", lambda v, dc_=dc, j_=j: v.tensor_mul(
                            zst[j_][:], x2[i_][:, dc_, :], bcA[:]),
                            waits=[(DV, max(b1, b2)), zst_free[j]], inc=(DV, 1))
                        w2 = kb.emit("vector", lambda v, j_=j: v.tensor_sub(
                            zst[j_][:], zst[j_][:], bcB[:]),
                            waits=[(DV, w1)], inc=(DV, 1))
                        d1 = dma(CD, lambda s, dc_=dc, j_=j: s.dma_start(
                            zT_r[:, dc_, at_ * 512:(at_ + 1) * 512], zst[j_][:]),
                            waits=[(DV, w2)])
                        zst_free[j] = (CD, d1)
                        if dc == DC - 1:
                            x2_war[i_] = (DV, kb.at(DV))

                    emit_ln_b(ctx4[at], z_out, rowsets[i], banksets[i], bcsets[i])

                emit_x2(0)
                emit_lna4(0)
                emit_x2(1)
                emit_lna4(1)
                emit_lnb4(0)
                emit_x2(2)
                emit_lna4(2)
                emit_lnb4(1)
                emit_x2(3)
                emit_lna4(3)
                emit_lnb4(2)
                emit_lnb4(3)
                kb.barrier()
                kb.barrier()
        kb.build()
    return nc


_CACHE = {}


def _consts():
    sel = np.zeros((8, E * 128), np.float32)
    for e in range(E):
        sel[e, e * 128:(e + 1) * 128] = 1.0
    return {
        "ones128": np.ones((128, 1), BF),
        "ones1f": np.ones((1, 128), np.float32),
        "id128f": np.eye(128, dtype=np.float32),
        "id8f": np.eye(8, dtype=np.float32),
        "sel8": sel,
    }


def kernel(L0_attn_x, L0_attn_Wq, L0_attn_bq, L0_attn_Wk, L0_attn_bk,
           L0_ffn_router_DE, L0_ffn_w_in_shared_FD, L0_ffn_w_out_shared_DF,
           L0_ffn_w_swiglu_FD, L0_ffn_w_in_eDF, L0_ffn_w_swiglu_eDF,
           L0_ffn_w_out_eFD):
    x = np.ascontiguousarray(np.asarray(L0_attn_x, dtype=np.float32))
    B = x.shape[0]
    common = {
        "wqT": np.ascontiguousarray(np.asarray(L0_attn_Wq).T),
        "wk2": np.ascontiguousarray(np.asarray(L0_attn_Wk)),
        "bq": np.ascontiguousarray(np.asarray(L0_attn_bq).reshape(EC, 128).T),
        "router": np.ascontiguousarray(np.asarray(L0_ffn_router_DE)),
        "wish": np.ascontiguousarray(np.asarray(L0_ffn_w_in_shared_FD).T),
        "wssh": np.ascontiguousarray(np.asarray(L0_ffn_w_swiglu_FD).T),
        "wosh": np.ascontiguousarray(np.asarray(L0_ffn_w_out_shared_DF).T),
        "win": np.ascontiguousarray(np.asarray(L0_ffn_w_in_eDF)),
        "wsw": np.ascontiguousarray(np.asarray(L0_ffn_w_swiglu_eDF)),
        "wout": np.ascontiguousarray(np.asarray(L0_ffn_w_out_eFD)),
    }
    common.update(_consts())
    xTb = [np.ascontiguousarray(x[b].T) for b in range(B)]
    in_maps = []
    for c in range(8):
        b = c // 4
        o = (c % 4) * TOK
        m = dict(common)
        m["xT"] = xTb[b]
        m["xN"] = x[b]
        m["xTo"] = np.ascontiguousarray(xTb[b][:, o:o + TOK])
        in_maps.append(m)
    if "nc" not in _CACHE:
        _CACHE["nc"] = build_program(debug=False)
    nc = _CACHE["nc"]
    res = run_bass_kernel_spmd(nc, in_maps, core_ids=list(range(8)))
    out = np.empty((B, S, D), np.float32)
    for c in range(8):
        b = c // 4
        o = (c % 4) * TOK
        out[b, o:o + TOK, :] = res.results[c]["zT"].T
    return out


# version 24
# speedup vs baseline: 1.1813x; 1.1813x over previous
import numpy as np
import ml_dtypes
from contextlib import ExitStack
import concourse.bass as bass
import concourse.mybir as mybir
from concourse.bass_utils import run_bass_kernel_spmd

F32 = mybir.dt.float32
BF16 = mybir.dt.bfloat16
AF = mybir.ActivationFunctionType
ALU = mybir.AluOpType
AX = mybir.AxisListType
f32 = np.float32
BF = ml_dtypes.bfloat16

D = 1024
S = 8192
TOK = 2048
E = 8
FF = 2048
DC = 8
EC = 8
FC = 16
QB = 4
NG = 8
CEXP = 9.0
PAIR_O = False

ENGINES = ["sync", "tensor", "vector", "scalar", "gpsimd"]
SD, GD, SC, CD = "sd", "gd", "sc", "cd"
NPOOL = 10
PE, DV, AC = "pe", "dve", "act"


class KB:

    def __init__(self, nc):
        self.nc = nc
        self.ops = {e: [] for e in ENGINES}
        self.cnt = {}
        self.semh = {}

    def at(self, sem):
        return self.cnt.get(sem, 0)

    def emit(self, eng, fn, waits=(), inc=None):
        def norm(x):
            if isinstance(x[1], tuple):
                return x[1]
            return x

        w = []
        for x in waits:
            if x is None:
                continue
            if isinstance(x, list):
                w.extend(norm(y) for y in x if y is not None and norm(y)[1] > 0)
            elif norm(x)[1] > 0:
                w.append(norm(x))
        v = None
        if inc is not None:
            self.cnt[inc[0]] = self.cnt.get(inc[0], 0) + inc[1]
            v = self.cnt[inc[0]]
        self.ops[eng].append((fn, w, inc))
        return v

    def barrier(self):
        snap = [(s, v) for s, v in self.cnt.items() if v > 0]
        for eng in ENGINES:
            self.ops[eng].append((None, list(snap), None))

    def build(self):
        for name in list(self.cnt):
            if name not in self.semh:
                self.semh[name] = self.nc.alloc_semaphore(name)

        def replayer(eng_name):
            def f(eng):
                last = {}
                for fn, waits, inc in self.ops[eng_name]:
                    for s_, v_ in waits:
                        if last.get(s_, 0) >= v_:
                            continue
                        eng.wait_ge(self.semh[s_], v_)
                        last[s_] = v_
                    if fn is None:
                        continue
                    inst = fn(eng)
                    if inc is not None:
                        inst.then_inc(self.semh[inc[0]], inc[1])
            return f

        with self.nc.Block() as blk:
            blk.sync(replayer("sync"))
            blk.tensor(replayer("tensor"))
            blk.vector(replayer("vector"))
            blk.scalar(replayer("scalar"))
            blk.gpsimd(replayer("gpsimd"))


def build_program(debug=False):
    nc = bass.Bass("TRN2", target_bir_lowering=False, debug=False)
    kb = KB(nc)

    xT = nc.dram_tensor("xT", [D, S], F32, kind="ExternalInput").ap()
    xTo = nc.dram_tensor("xTo", [D, TOK], F32, kind="ExternalInput").ap()
    xN = nc.dram_tensor("xN", [S, D], F32, kind="ExternalInput").ap()
    wqT = nc.dram_tensor("wqT", [D, D], F32, kind="ExternalInput").ap()
    wk2 = nc.dram_tensor("wk2", [D, D], F32, kind="ExternalInput").ap()
    bq = nc.dram_tensor("bq", [128, EC], F32, kind="ExternalInput").ap()
    router = nc.dram_tensor("router", [D, E], F32, kind="ExternalInput").ap()
    wish = nc.dram_tensor("wish", [D, FF], F32, kind="ExternalInput").ap()
    wssh = nc.dram_tensor("wssh", [D, FF], F32, kind="ExternalInput").ap()
    wosh = nc.dram_tensor("wosh", [FF, D], F32, kind="ExternalInput").ap()
    win = nc.dram_tensor("win", [E, D, FF], F32, kind="ExternalInput").ap()
    wsw = nc.dram_tensor("wsw", [E, D, FF], F32, kind="ExternalInput").ap()
    wout = nc.dram_tensor("wout", [E, FF, D], F32, kind="ExternalInput").ap()
    ones128 = nc.dram_tensor("ones128", [128, 1], BF16, kind="ExternalInput").ap()
    ones1f = nc.dram_tensor("ones1f", [1, 128], F32, kind="ExternalInput").ap()
    id128f = nc.dram_tensor("id128f", [128, 128], F32, kind="ExternalInput").ap()
    id8f = nc.dram_tensor("id8f", [8, 8], F32, kind="ExternalInput").ap()
    sel8 = nc.dram_tensor("sel8", [8, E * 128], F32, kind="ExternalInput").ap()

    zT = nc.dram_tensor("zT", [D, TOK], F32, kind="ExternalOutput").ap()
    xbT_h = nc.dram_tensor("xbT_h", [D, S], BF16, kind="Internal").ap()
    y1b_h = nc.dram_tensor("y1b_h", [D, TOK], BF16, kind="Internal").ap()
    y1f_h = nc.dram_tensor("y1f_h", [D, TOK], F32,
                           kind="ExternalOutput" if debug else "Internal").ap()
    if debug >= 2:
        gat_h = nc.dram_tensor("gat_h", [8, TOK], F32, kind="ExternalOutput").ap()
        acc_h = nc.dram_tensor("acc_h", [D, TOK], F32, kind="ExternalOutput").ap()
        sco_h = nc.dram_tensor("sco_h", [8, TOK], F32, kind="ExternalOutput").ap()
        stok_h = nc.dram_tensor("stok_h", [128, 128], F32, kind="ExternalOutput").ap()
        gtok_h = nc.dram_tensor("gtok_h", [128, 128], F32, kind="ExternalOutput").ap()

    xT_r = xT.rearrange("(c p) t -> p c t", p=128)
    xN_r = xN.rearrange("(c p) d -> p c d", p=128)
    xTo_r = xTo.rearrange("(c p) t -> p c t", p=128)
    wqT_r = wqT.rearrange("(c p) e -> p c e", p=128)
    wk2_r = wk2.rearrange("(c p) d -> p c d", p=128)
    xbT_r = xbT_h.rearrange("(c p) t -> p c t", p=128)
    y1f_r = y1f_h.rearrange("(c p) t -> p c t", p=128)
    zT_r = zT.rearrange("(c p) t -> p c t", p=128)
    rtr_r = router.rearrange("(c p) e -> p c e", p=128)
    win_r = [win[e].rearrange("(c p) f -> p c f", p=128) for e in range(E)] + \
            [wish.rearrange("(c p) f -> p c f", p=128)]
    wsw_r = [wsw[e].rearrange("(c p) f -> p c f", p=128) for e in range(E)] + \
            [wssh.rearrange("(c p) f -> p c f", p=128)]
    wout_r = [wout[e].rearrange("(c p) d -> p c d", p=128) for e in range(E)] + \
             [wosh.rearrange("(c p) d -> p c d", p=128)]

    es = ExitStack()
    with es:
        sb = lambda name, shape, dt: es.enter_context(nc.sbuf_tensor(name, shape, dt))
        ones_sb = sb("ones_sb", [128, 1], BF16)
        ones1f_sb = sb("ones1f_sb", [1, 128], F32)
        bq_sb = sb("bq_sb", [128, EC], F32)
        cs_sb = sb("cs_sb", [128, DC], F32)
        cs_tmp = sb("cs_tmp", [128, DC], F32)
        tmp1_sb = sb("tmp1_sb", [128, 1], F32)
        rA = sb("rA", [1, 512], F32)
        rB = sb("rB", [1, 512], F32)
        rC = sb("rC", [1, 512], F32)
        rD = sb("rD", [1, 512], F32)
        bc0 = sb("bc0", [128, 512], F32)
        bc1 = sb("bc1", [128, 512], F32)

        pb = [es.enter_context(nc.psum_tensor(f"pb{i}", [128, 512], F32))
              for i in range(8)]
        bank_free = {}
        state = {"ln_war": None}

        dma_idx = {SD: 0, GD: 0, CD: 0}

        def dma(qsem, fn, waits=()):
            eng = {GD: "gpsimd", CD: "scalar"}.get(qsem, "sync")
            if qsem == SC:
                return kb.emit(eng, fn, waits=waits, inc=(SC, 16))
            phys = f"{qsem}{dma_idx[qsem] % NPOOL}"
            dma_idx[qsem] += 1
            v = kb.emit(eng, fn, waits=waits, inc=(phys, 16))
            return (phys, v)

        def bw(i):
            return bank_free.get(i)

        ln_war = {}

        def emit_ln_a(src, ub, ready, rows, banks):
            rS, rQ = rows
            bA, bB = banks
            war = ln_war.get(ub.name)
            cps = []
            for dc in range(DC):
                cps.append(kb.emit("vector", lambda v, dc_=dc: v.tensor_copy(
                    ub[:, dc_, :], src(dc_)),
                    waits=list(ready) + ([war] if dc == 0 else []), inc=(DV, 1)))
            b = bw(bA)
            last = None
            for dc in range(DC):
                last = kb.emit("tensor", lambda t, dc_=dc, b_=bA: t.matmul(
                    pb[b_][0:1, :], ones_sb[:], ub[:, dc_, :],
                    start=(dc_ == 0), stop=(dc_ == DC - 1)),
                    waits=[(DV, cps[dc]), b if dc == 0 else None],
                    inc=(PE, 1) if dc == DC - 1 else None)
            s1 = kb.emit("vector", lambda v, b_=bA: v.tensor_copy(rS[:], pb[b_][0:1, :]),
                         waits=[(PE, last)], inc=(DV, 1))
            bank_free[bA] = (DV, s1)
            sqs = []
            for dc in range(DC):
                sqs.append(kb.emit("vector", lambda v, dc_=dc: v.tensor_mul(
                    ub[:, dc_, :], ub[:, dc_, :], ub[:, dc_, :]),
                    waits=[(PE, last)], inc=(DV, 1)))
            b = bw(bB)
            last2 = None
            for dc in range(DC):
                last2 = kb.emit("tensor", lambda t, dc_=dc, b_=bB: t.matmul(
                    pb[b_][0:1, :], ones_sb[:], ub[:, dc_, :],
                    start=(dc_ == 0), stop=(dc_ == DC - 1)),
                    waits=[(DV, sqs[dc]), b if dc == 0 else None],
                    inc=(PE, 1) if dc == DC - 1 else None)
            ln_war[ub.name] = (PE, last2)
            s2 = kb.emit("vector", lambda v, b_=bB: v.tensor_copy(rQ[:], pb[b_][0:1, :]),
                         waits=[(PE, last2)], inc=(DV, 1))
            bank_free[bB] = (DV, s2)
            return dict(s1=s1, s2=s2)

        def emit_ln_b(ctx, out_cb, rows, banks, bcs):
            rS, rQ, rT, rU = rows
            bA, bB = banks
            bcA, bcB = bcs
            kb.emit("vector", lambda v: v.tensor_scalar(
                rS[:], rS[:], 1.0 / D, None, op0=ALU.mult),
                waits=[(DV, ctx["s1"])], inc=(DV, 1))
            kb.emit("vector", lambda v: v.tensor_scalar(
                rQ[:], rQ[:], 1.0 / D, None, op0=ALU.mult),
                waits=[(DV, ctx["s2"])], inc=(DV, 1))
            kb.emit("vector", lambda v: v.tensor_mul(rT[:], rS[:], rS[:]),
                    waits=[(DV, kb.at(DV))], inc=(DV, 1))
            vv = kb.emit("vector", lambda v: v.tensor_sub(rQ[:], rQ[:], rT[:]),
                         waits=[(DV, kb.at(DV))], inc=(DV, 1))
            a1 = kb.emit("scalar", lambda sc: sc.activation(
                rT[:], rQ[:], AF.Ln, bias=f32(1e-5)),
                waits=[(DV, vv)], inc=(AC, 1))
            a2 = kb.emit("scalar", lambda sc: sc.activation(
                rU[:], rT[:], AF.Exp, scale=f32(-0.5)),
                waits=[(AC, a1)], inc=(AC, 1))
            mr = kb.emit("vector", lambda v: v.tensor_mul(rT[:], rS[:], rU[:]),
                         waits=[(AC, a2)], inc=(DV, 1))
            b = bw(bA)
            p1 = kb.emit("tensor", lambda t, b_=bA: t.matmul(
                pb[b_][:], ones1f_sb[:], rU[:], start=True, stop=True),
                waits=[(AC, a2), b], inc=(PE, 1))
            b1 = kb.emit("vector", lambda v, b_=bA: v.tensor_copy(bcA[:], pb[b_][:]),
                         waits=[(PE, p1)], inc=(DV, 1))
            bank_free[bA] = (DV, b1)
            b = bw(bB)
            p2 = kb.emit("tensor", lambda t, b_=bB: t.matmul(
                pb[b_][:], ones1f_sb[:], rT[:], start=True, stop=True),
                waits=[(DV, mr), b], inc=(PE, 1))
            b2 = kb.emit("vector", lambda v, b_=bB: v.tensor_copy(bcB[:], pb[b_][:]),
                         waits=[(PE, p2)], inc=(DV, 1))
            bank_free[bB] = (DV, b2)
            for dc in range(DC):
                out_cb(dc, b1, b2)

        dma(SC, lambda s: s.dma_start(ones_sb[:], ones128))
        dma(SC, lambda s: s.dma_start(ones1f_sb[:], ones1f))
        dma(SC, lambda s: s.dma_start(bq_sb[:], bq))
        c0 = kb.at(SC)

        ph12 = ExitStack()
        with ph12:
            q2T = ph12.enter_context(nc.sbuf_tensor("q2T", [128, DC, TOK], BF16))

            ph1 = ExitStack()
            with ph1:
                wq_sb = ph1.enter_context(nc.sbuf_tensor("wq_sb", [128, DC, D], BF16))
                wk2_sb = ph1.enter_context(nc.sbuf_tensor("wk2_sb", [128, DC, D], BF16))
                qT = ph1.enter_context(nc.sbuf_tensor("qT", [128, EC, TOK], BF16))
                xq = [ph1.enter_context(nc.sbuf_tensor(f"xq{i}", [128, DC, 512], BF16))
                      for i in range(4)]
                xsf = [ph1.enter_context(nc.sbuf_tensor(f"xsf{i}", [128, DC, 512], F32))
                       for i in range(2)]
                xs = [ph1.enter_context(nc.sbuf_tensor(f"xs{i}", [128, DC, 512], BF16))
                      for i in range(2)]

                wready = dma(GD, lambda g: g.dma_start(wq_sb[:], wqT_r))
                w2ready = dma(GD, lambda g: g.dma_start(wk2_sb[:], wk2_r))

                lds = {}
                qp_dv = {}
                for t in range(QB):
                    lds[t] = dma(GD, lambda g, t_=t: g.dma_start(
                        xq[t_][:], xTo_r[:, :, t_ * 512:(t_ + 1) * 512]), waits=[])

                def emit_qp(t):
                    lpe = None
                    for ec in range(EC):
                        bi = ec % 2
                        last_mm = None
                        for dc in range(DC):
                            ws = []
                            if dc == 0:
                                ws = [(GD, lds[t]), (GD, wready), bw(bi)]
                            last_mm = kb.emit(
                                "tensor",
                                lambda tn, ec_=ec, dc_=dc, t_=t, b_=bi: tn.matmul(
                                    pb[b_][:], wq_sb[:, dc_, ec_ * 128:(ec_ + 1) * 128],
                                    xq[t_][:, dc_, :], start=(dc_ == 0), stop=(dc_ == 7)),
                                waits=ws, inc=((PE, 1) if dc == 7 else None))
                        lpe = last_mm
                        ev = kb.emit(
                            "vector",
                            lambda v, ec_=ec, b_=bi, t_=t: v.tensor_scalar(
                                qT[:, ec_, t_ * 512:(t_ + 1) * 512], pb[b_][:],
                                bq_sb[:, ec_:ec_ + 1], None, op0=ALU.add),
                            waits=[(PE, last_mm), (SC, c0)], inc=(DV, 1))
                        bank_free[bi] = (DV, ev)
                    qp_dv[t] = kb.at(DV)

                def emit_q2(t):
                    for dc in range(DC):
                        b2 = 2 + dc % 2
                        last_mm = None
                        for ec in range(EC):
                            ws = []
                            if ec == 0:
                                ws = [(DV, qp_dv[t]), (GD, w2ready), bw(b2)]
                            last_mm = kb.emit(
                                "tensor",
                                lambda tn, ec_=ec, dc_=dc, b_=b2, t_=t: tn.matmul(
                                    pb[b_][:], wk2_sb[:, ec_, dc_ * 128:(dc_ + 1) * 128],
                                    qT[:, ec_, t_ * 512:(t_ + 1) * 512],
                                    start=(ec_ == 0), stop=(ec_ == 7)),
                                waits=ws, inc=((PE, 1) if ec == 7 else None))
                        ev = kb.emit(
                            "vector",
                            lambda v, dc_=dc, b_=b2, t_=t: v.tensor_copy(
                                q2T[:, dc_, t_ * 512:(t_ + 1) * 512], pb[b_][:]),
                            waits=[(PE, last_mm)], inc=(DV, 1))
                        bank_free[b2] = (DV, ev)

                emit_qp(0)
                emit_qp(1)
                emit_q2(0)
                emit_qp(2)
                emit_q2(1)
                emit_qp(3)
                emit_q2(2)
                emit_q2(3)

                xsf_free = [None, None]
                xs_free = [None, None]
                for sl in range(S // 512):
                    i = sl % 2
                    ld = dma(SD, lambda s, sl_=sl, i_=i: s.dma_start(
                        xsf[i_][:], xT_r[:, :, sl_ * 512:(sl_ + 1) * 512]),
                        waits=[xsf_free[i]])
                    casts = []
                    for dc in range(DC):
                        casts.append(kb.emit(
                            "scalar", lambda sc, dc_=dc, i_=i: sc.activation(
                                xs[i_][:, dc_, :], xsf[i_][:, dc_, :], AF.Copy),
                            waits=([(SD, ld), xs_free[i]] if dc == 0 else []),
                            inc=(AC, 1)))
                    st = dma(CD, lambda s, sl_=sl, i_=i: s.dma_start(
                        xbT_r[:, :, sl_ * 512:(sl_ + 1) * 512], xs[i_][:]),
                        waits=[(AC, casts[-1])])
                    rd = kb.emit("vector", lambda v, i_=i: v.reduce_sum(
                        cs_tmp[:], xsf[i_][:], axis=AX.X),
                        waits=[(SD, ld)], inc=(DV, 1))
                    if sl == 0:
                        kb.emit("vector", lambda v: v.tensor_copy(
                            cs_sb[:], cs_tmp[:]),
                            waits=[(DV, rd)], inc=(DV, 1))
                    else:
                        kb.emit("vector", lambda v: v.tensor_add(
                            cs_sb[:], cs_sb[:], cs_tmp[:]),
                            waits=[(DV, rd)], inc=(DV, 1))
                    xsf_free[i] = [(AC, casts[-1]), (DV, rd)]
                    xs_free[i] = [(CD, st)]
                kb.emit("vector", lambda v: v.tensor_scalar(
                    cs_sb[:], cs_sb[:], 2.0 / S, None, op0=ALU.mult),
                    waits=[(DV, kb.at(DV))], inc=(DV, 1))
                kb.barrier()

            ph2 = ExitStack()
            with ph2:
                kg = [ph2.enter_context(nc.sbuf_tensor(f"kg{i}", [128, EC, 1024], BF16))
                      for i in range(2)]
                vg = [ph2.enter_context(nc.sbuf_tensor(f"vg{i}", [128, 8, 1024], BF16))
                      for i in range(2)]
                Pt = [ph2.enter_context(nc.sbuf_tensor(f"Pt{i}", [128, 8, 512], BF16))
                      for i in range(2)]
                Obuf = [ph2.enter_context(nc.sbuf_tensor(f"O{i}", [128, DC, 512], F32))
                        for i in range(2)]
                xow = ph2.enter_context(nc.sbuf_tensor("xow", [128, DC, 512], F32))
                ub2 = ph2.enter_context(nc.sbuf_tensor("ub2", [128, DC, 512], BF16))
                y1f_st = [ph2.enter_context(nc.sbuf_tensor(f"y1f{i}", [128, 512], F32))
                          for i in range(2)]
                y1b_st = [ph2.enter_context(nc.sbuf_tensor(f"y1b{i}", [128, 512], BF16))
                          for i in range(2)]

                kg_free = [None, None]
                vg_free = [None, None]
                Pt_free = [None, None]
                y1f_free = [None, None]
                y1b_free = [None, None]
                exp_done = {}
                vg_ld = {}
                kg_ld = {}
                xow_ld = {}
                qb_lpe = {}
                epi_udv = {}
                epi_ctx = {}
                xc_dv = {}

                def emit_xc(qb):
                    last = None
                    for dc in range(DC):
                        last = kb.emit("vector", lambda v, dc_=dc: v.tensor_scalar(
                            xow[:, dc_, :], xow[:, dc_, :], 2.0, cs_sb[:, dc_:dc_ + 1],
                            op0=ALU.mult, op1=ALU.subtract),
                            waits=[(SD, xow_ld[qb]), (DV, kb.at(DV))], inc=(DV, 1))
                    xc_dv[qb] = last

                def emit_slot(qb, g):
                    O = Obuf[qb % 2]
                    lb4 = 4 if qb % 2 == 0 else 7
                    if g < NG:
                        i = g % 2
                        kg_ld[(qb, g)] = dma(SD, lambda s, g_=g, i_=i: s.dma_start(
                            kg[i_][:], xbT_r[:, :, g_ * 1024:(g_ + 1) * 1024]),
                            waits=[kg_free[i]])
                        vg_ld[(qb, g)] = dma(GD, lambda gp, g_=g, i_=i:
                            gp.dma_start(vg[i_][:],
                                         xN_r[:, g_ * 8:(g_ + 1) * 8, :]),
                            waits=[vg_free[i]])
                        for tt in range(8):
                            lb = tt % 2
                            last = None
                            for ec in range(EC):
                                ws = []
                                if ec == 0:
                                    ws = [(SD, kg_ld[(qb, g)]), bw(lb)]
                                last = kb.emit(
                                    "tensor",
                                    lambda t, ec_=ec, tt_=tt, i_=i, lb_=lb, qb_=qb:
                                    t.matmul(
                                        pb[lb_][:],
                                        kg[i_][:, ec_, tt_ * 128:(tt_ + 1) * 128],
                                        q2T[:, ec_, qb_ * 512:(qb_ + 1) * 512],
                                        start=(ec_ == 0), stop=(ec_ == 7)),
                                    waits=ws, inc=((PE, 1) if ec == 7 else None))
                            ws = [(PE, last)]
                            if tt == 0:
                                ws.append(Pt_free[i])
                            ex = kb.emit(
                                "scalar",
                                lambda sc, tt_=tt, i_=i, lb_=lb: sc.activation(
                                    Pt[i_][:, tt_, :], pb[lb_][:], AF.Exp,
                                    bias=f32(-CEXP), scale=f32(1.0 / 32.0)),
                                waits=ws, inc=(AC, 1))
                            exp_done[(qb, g, tt)] = ex
                            bank_free[lb] = (AC, ex)
                    if g > 0:
                        gp_ = g - 1
                        i = gp_ % 2
                        for tt in range(8):
                            ws = [(AC, exp_done[(qb, gp_, tt)])]
                            if gp_ == 0 and tt == 0:
                                ws.append(bw(lb4))
                            kb.emit(
                                "tensor",
                                lambda t, tt_=tt, i_=i, lb_=lb4,
                                       st_=(gp_ == 0 and tt == 0),
                                       sp_=(gp_ == NG - 1 and tt == 7): t.matmul(
                                    pb[lb_][0:1, :], ones_sb[:], Pt[i_][:, tt_, :],
                                    start=st_, stop=sp_),
                                waits=ws, inc=((PE, 1) if tt == 7 else None))
                        if gp_ == NG - 1:
                            qb_lpe[qb] = kb.at(PE)
                        first_half = (gp_ % 2 == 0) and PAIR_O
                        for dc in range(DC):
                            ob = 2 + dc % 2
                            last = None
                            for tt in range(8):
                                ws = []
                                if tt == 0:
                                    ws = [(AC, exp_done[(qb, gp_, 7)]),
                                          (GD, vg_ld[(qb, gp_)])]
                                    if first_half or not PAIR_O:
                                        ws.append(bw(ob))
                                last = kb.emit(
                                    "tensor",
                                    lambda t, dc_=dc, tt_=tt, i_=i, ob_=ob,
                                           st_=(tt == 0 and (first_half or not PAIR_O)),
                                           sp_=((not first_half) and tt == 7): t.matmul(
                                        pb[ob_][:],
                                        vg[i_][:, tt_, dc_ * 128:(dc_ + 1) * 128],
                                        Pt[i_][:, tt_, :],
                                        start=st_, stop=sp_),
                                    waits=ws, inc=((PE, 1) if tt == 7 else None))
                            if not first_half:
                                pair = gp_ // 2 if PAIR_O else gp_
                                if pair == 0:
                                    ev = kb.emit("vector",
                                                 lambda v, dc_=dc, ob_=ob, O_=O: v.tensor_copy(
                                                     O_[:, dc_, :], pb[ob_][:]),
                                                 waits=[(PE, last)], inc=(DV, 1))
                                else:
                                    ev = kb.emit("vector",
                                                 lambda v, dc_=dc, ob_=ob, O_=O: v.tensor_add(
                                                     O_[:, dc_, :], O_[:, dc_, :], pb[ob_][:]),
                                                 waits=[(PE, last)], inc=(DV, 1))
                                bank_free[ob] = (DV, ev)
                        Pt_free[i] = (PE, kb.at(PE))
                        vg_free[i] = (PE, kb.at(PE))
                    if g < NG:
                        kg_free[g % 2] = (PE, kb.at(PE))

                def emit_epi_a(qb):
                    O = Obuf[qb % 2]
                    lb4 = 4 if qb % 2 == 0 else 7
                    c1 = kb.emit("vector", lambda v, lb_=lb4: v.tensor_copy(
                        rC[:], pb[lb_][0:1, :]),
                        waits=[(PE, qb_lpe[qb])], inc=(DV, 1))
                    bank_free[lb4] = (DV, c1)
                    p1 = kb.emit("tensor", lambda t: t.matmul(
                        pb[5][:], ones1f_sb[:], rC[:], start=True, stop=True),
                        waits=[(DV, c1), bw(5)], inc=(PE, 1))
                    c3 = kb.emit("vector", lambda v: v.reciprocal(bc0[:], pb[5][:]),
                                 waits=[(PE, p1)], inc=(DV, 1))
                    bank_free[5] = (DV, c3)
                    for dc in range(DC):
                        kb.emit("vector", lambda v, dc_=dc, O_=O: v.tensor_mul(
                            O_[:, dc_, :], O_[:, dc_, :], bc0[:]),
                            waits=[(DV, c3), (DV, xc_dv[qb])], inc=(DV, 1))
                        kb.emit("vector", lambda v, dc_=dc, O_=O: v.tensor_add(
                            O_[:, dc_, :], O_[:, dc_, :], xow[:, dc_, :]),
                            waits=[(DV, kb.at(DV))], inc=(DV, 1))
                    epi_udv[qb] = kb.at(DV)
                    if qb + 1 < QB:
                        xow_ld[qb + 1] = dma(CD, lambda s, qb_=qb + 1: s.dma_start(
                            xow[:], xTo_r[:, :, qb_ * 512:(qb_ + 1) * 512]),
                            waits=[(DV, epi_udv[qb])])

                def emit_epi_b1(qb):
                    O = Obuf[qb % 2]
                    epi_ctx[qb] = emit_ln_a(lambda dc: O[:, dc, :], ub2,
                                            [(DV, epi_udv[qb])], (rA, rB), (5, 6))

                def emit_epi_b2(qb):
                    O = Obuf[qb % 2]

                    def y1_out(dc, b1, b2, qb_=qb, O_=O):
                        i = dc % 2
                        w1 = kb.emit("vector", lambda v, dc_=dc, i_=i: v.tensor_mul(
                            y1f_st[i_][:], O_[:, dc_, :], bc0[:]),
                            waits=[(DV, max(b1, b2)), y1f_free[i]], inc=(DV, 1))
                        w2 = kb.emit("vector", lambda v, i_=i: v.tensor_sub(
                            y1f_st[i_][:], y1f_st[i_][:], bc1[:]),
                            waits=[(DV, w1)], inc=(DV, 1))
                        cb = kb.emit("vector", lambda v, i_=i: v.tensor_copy(
                            y1b_st[i_][:], y1f_st[i_][:]),
                            waits=[(DV, w2), y1b_free[i]], inc=(DV, 1))
                        d1 = dma(CD, lambda s, dc_=dc, i_=i: s.dma_start(
                            y1f_r[:, dc_, qb_ * 512:(qb_ + 1) * 512], y1f_st[i_][:]),
                            waits=[(DV, w2)])
                        d2 = dma(CD, lambda s, dc_=dc, i_=i: s.dma_start(
                            y1b_h[dc_ * 128:(dc_ + 1) * 128,
                                  qb_ * 512:(qb_ + 1) * 512], y1b_st[i_][:]),
                            waits=[(DV, cb)])
                        y1f_free[i] = [(CD, d1), (DV, cb)]
                        y1b_free[i] = (CD, d2)

                    emit_ln_b(epi_ctx[qb], y1_out, (rA, rB, rC, rD), (5, 6), (bc0, bc1))

                xow_ld[0] = dma(SD, lambda s: s.dma_start(
                    xow[:], xTo_r[:, :, 0:512]), waits=[])
                emit_xc(0)
                for qb in range(QB):
                    for g in range(0 if qb == 0 else 3, NG + 1):
                        emit_slot(qb, g)
                    if qb + 1 < QB:
                        emit_slot(qb + 1, 0)
                        emit_epi_a(qb)
                        emit_slot(qb + 1, 1)
                        emit_epi_b1(qb)
                        emit_slot(qb + 1, 2)
                        emit_epi_b2(qb)
                        emit_xc(qb + 1)
                    else:
                        emit_epi_a(qb)
                        emit_epi_b1(qb)
                        emit_epi_b2(qb)
                kb.barrier()
        ph34 = ExitStack()
        with ph34:
            acc = ph34.enter_context(nc.sbuf_tensor("acc", [128, DC, TOK], F32))
            y1t = [ph34.enter_context(nc.sbuf_tensor(f"y1t{i}", [128, 512], F32))
                   for i in range(2)]
            ub4 = ph34.enter_context(nc.sbuf_tensor("ub4", [128, DC, 512], BF16))
            zst = [ph34.enter_context(nc.sbuf_tensor(f"zst{i}", [128, 512], F32))
                   for i in range(2)]
            acc_fin = {}
            ph3 = ExitStack()
            with ph3:
                y1s = [ph3.enter_context(nc.sbuf_tensor(f"y1s{i}", [128, TOK], BF16))
                       for i in range(2)]
                wib = [ph3.enter_context(nc.sbuf_tensor(f"wib{i}", [128, DC, 128], BF16))
                       for i in range(2)]
                wsb2 = [ph3.enter_context(nc.sbuf_tensor(f"wsb{i}", [128, DC, 128], BF16))
                        for i in range(2)]
                wob = [ph3.enter_context(nc.sbuf_tensor(f"wob{i}", [128, 8, 128], BF16))
                       for i in range(2)]
                gbt = ph3.enter_context(nc.sbuf_tensor("gbt", [128, 4, 512], BF16))
                hm = [ph3.enter_context(nc.sbuf_tensor(f"hm{i}", [128, 512], BF16))
                      for i in range(2)]
                rt_sb = ph3.enter_context(nc.sbuf_tensor("rt_sb", [128, DC, 8], BF16))
                sel_sb = ph3.enter_context(nc.sbuf_tensor("sel_sb", [8, E * 128], F32))
                id128_sb = ph3.enter_context(nc.sbuf_tensor("id128_sb", [128, 128], F32))
                id8_sb = ph3.enter_context(nc.sbuf_tensor("id8_sb", [8, 8], F32))
                gatesT = ph3.enter_context(nc.sbuf_tensor("gatesT", [8, TOK], F32))

                dma(SC, lambda s: s.dma_start(sel_sb[:], sel8))
                dma(SC, lambda s: s.dma_start(id128_sb[:], id128f))
                dma(SC, lambda s: s.dma_start(id8_sb[:], id8f))
                cph3 = kb.at(SC)

                y1s_free = [None, None]
                wib_free = [None, None]
                wsb_free = [None, None]
                wob_free = [None, None]
                hm_free = [None, None]

                gsc = ExitStack()
                with gsc:
                    scoresT = gsc.enter_context(nc.sbuf_tensor("scoresT", [8, TOK], F32))
                    s_tok = gsc.enter_context(nc.sbuf_tensor("s_tok", [128, 16, 8], F32))
                    g_tok = gsc.enter_context(nc.sbuf_tensor("g_tok", [128, 16, 8], F32))
                    m1_sb = [gsc.enter_context(nc.sbuf_tensor(f"m1_sb{i}", [128, 1], F32)) for i in range(2)]
                    m2_sb = [gsc.enter_context(nc.sbuf_tensor(f"m2_sb{i}", [128, 1], F32)) for i in range(2)]
                    msk_sb = [gsc.enter_context(nc.sbuf_tensor(f"msk_sb{i}", [128, 8], F32)) for i in range(2)]
                    sk2_sb = [gsc.enter_context(nc.sbuf_tensor(f"sk2_sb{i}", [128, 8], F32)) for i in range(2)]
                    sig_sb = [gsc.enter_context(nc.sbuf_tensor(f"sig_sb{i}", [128, 8], F32)) for i in range(2)]

                    rt_ld = dma(GD, lambda g: g.dma_start(rt_sb[:], rtr_r))
                    last_rt = {}
                    for dc in range(DC):
                        i = dc % 2
                        ld = dma(SD, lambda s, dc_=dc, i_=i: s.dma_start(
                            y1s[i_][:], y1b_h[dc_ * 128:(dc_ + 1) * 128, :]),
                            waits=(y1s_free[i] or []))
                        for at in range(4):
                            ws = [(SD, ld), (GD, rt_ld)]
                            if dc == 0:
                                ws.append(bw(at))
                            last_rt[at] = kb.emit(
                                "tensor",
                                lambda t, dc_=dc, at_=at, i_=i: t.matmul(
                                    pb[at_][0:8, :], rt_sb[:, dc_, :],
                                    y1s[i_][:, at_ * 512:(at_ + 1) * 512],
                                    start=(dc_ == 0), stop=(dc_ == 7)),
                                waits=ws, inc=(PE, 1) if at == 3 else None)
                        y1s_free[i] = [(PE, kb.at(PE))]
                    rt_pe = kb.at(PE)
                    for at in range(4):
                        ev = kb.emit("vector", lambda v, at_=at: v.tensor_copy(
                            scoresT[:, at_ * 512:(at_ + 1) * 512], pb[at_][0:8, :]),
                            waits=[(PE, rt_pe)], inc=(DV, 1))
                        bank_free[at] = (DV, ev)
                    sc_dv = kb.at(DV)

                    for j in range(16):
                        bk4 = 4 + j % 2
                        tp = kb.emit("tensor", lambda t, j_=j, b_=bk4: t.transpose(
                            pb[b_][:, 0:8], scoresT[:, j_ * 128:(j_ + 1) * 128], id8_sb[:]),
                            waits=[(DV, sc_dv), (SC, cph3), bw(bk4)], inc=(PE, 1))
                        cv = kb.emit("vector", lambda v, j_=j, b_=bk4: v.tensor_copy(
                            s_tok[:, j_, :], pb[b_][:, 0:8]),
                            waits=[(PE, tp)], inc=(DV, 1))
                        bank_free[bk4] = (DV, cv)
                    stok_dv = kb.at(DV)
                    gmul = {}
                    for j in range(16):
                        p = j % 2
                        r1 = kb.emit("vector", lambda v, j_=j, p_=p: v.reduce_max(
                            m1_sb[p_][:], s_tok[:, j_, :], axis=AX.X),
                            waits=[(DV, stok_dv if j < 2 else gmul[j - 2])], inc=(DV, 1))
                        k1 = kb.emit("vector", lambda v, j_=j, p_=p: v.tensor_scalar(
                            msk_sb[p_][:], s_tok[:, j_, :], m1_sb[p_][:, 0:1], None,
                            op0=ALU.is_ge),
                            waits=[(DV, r1)], inc=(DV, 1))
                        k2 = kb.emit("vector", lambda v, j_=j, p_=p: v.scalar_tensor_tensor(
                            sk2_sb[p_][:], msk_sb[p_][:], -1e30, s_tok[:, j_, :],
                            op0=ALU.mult, op1=ALU.add),
                            waits=[(DV, k1)], inc=(DV, 1))
                        r2 = kb.emit("vector", lambda v, p_=p: v.reduce_max(
                            m2_sb[p_][:], sk2_sb[p_][:], axis=AX.X),
                            waits=[(DV, k2)], inc=(DV, 1))
                        k3 = kb.emit("vector", lambda v, j_=j, p_=p: v.tensor_scalar(
                            msk_sb[p_][:], s_tok[:, j_, :], m2_sb[p_][:, 0:1], None,
                            op0=ALU.is_ge),
                            waits=[(DV, r2)], inc=(DV, 1))
                        sg = kb.emit("scalar", lambda sc, j_=j, p_=p: sc.activation(
                            sig_sb[p_][:], s_tok[:, j_, :], AF.Sigmoid),
                            waits=[(DV, stok_dv if j < 2 else gmul[j - 2])], inc=(AC, 1))
                        gmul[j] = kb.emit("vector", lambda v, j_=j, p_=p: v.tensor_mul(
                            g_tok[:, j_, :], sig_sb[p_][:], msk_sb[p_][:]),
                            waits=[(DV, k3), (AC, sg)], inc=(DV, 1))
                    for j in range(16):
                        bk4 = 4 + j % 2
                        tp = kb.emit("tensor", lambda t, j_=j, b_=bk4: t.transpose(
                            pb[b_][0:8, 0:128], g_tok[:, j_, :], id128_sb[:]),
                            waits=[(DV, gmul[j]), bw(bk4)], inc=(PE, 1))
                        cv = kb.emit("vector", lambda v, j_=j, b_=bk4: v.tensor_copy(
                            gatesT[:, j_ * 128:(j_ + 1) * 128], pb[b_][0:8, 0:128]),
                            waits=[(PE, tp)], inc=(DV, 1))
                        bank_free[bk4] = (DV, cv)
                    gat_dv = kb.at(DV)
                    if debug >= 2:
                        dma(SC, lambda s: s.dma_start(gat_h, gatesT[:]),
                            waits=[(DV, gat_dv)])
                        dma(SC, lambda s: s.dma_start(sco_h, scoresT[:]),
                            waits=[(DV, gat_dv)])
                        dma(SC, lambda s: s.dma_start(
                            stok_h, s_tok[:].rearrange("p a b -> p (a b)")),
                            waits=[(DV, gat_dv)])
                        dma(SC, lambda s: s.dma_start(
                            gtok_h, g_tok[:].rearrange("p a b -> p (a b)")),
                            waits=[(DV, gat_dv)])
                routed = ph3.enter_context(nc.sbuf_tensor("routed", [128, DC, TOK], BF16))
                hbuf = ph3.enter_context(nc.sbuf_tensor("hbuf", [128, 8, TOK], BF16))
                rows2 = tuple(ph3.enter_context(nc.sbuf_tensor(f"r2_{i}", [1, 512], F32))
                              for i in range(4))

                y1t_free = [None, None]
                zst_free = [None, None]
                ln2_ctx = {}

                def ln2_emit(ap):
                    rowsets = [(rA, rB, rC, rD), rows2]
                    for k, at in enumerate((2 * ap, 2 * ap + 1)):
                        for dc in range(DC):
                            i = dc % 2
                            ld = dma(SD, lambda s, dc_=dc, at_=at, i_=i: s.dma_start(
                                y1t[i_][:],
                                y1f_r[:, dc_, at_ * 512:(at_ + 1) * 512]),
                                waits=[y1t_free[i]])
                            ad = kb.emit("vector",
                                lambda v, dc_=dc, at_=at, i_=i: v.tensor_add(
                                    acc[:, dc_, at_ * 512:(at_ + 1) * 512],
                                    acc[:, dc_, at_ * 512:(at_ + 1) * 512],
                                    y1t[i_][:]),
                                waits=[(SD, ld), (DV, acc_fin[at])], inc=(DV, 1))
                            y1t_free[i] = [(DV, ad)]
                        x2dv = kb.at(DV)
                        ln2_ctx[at] = emit_ln_a(
                            lambda dc, at_=at: acc[:, dc, at_ * 512:(at_ + 1) * 512],
                            ub4, [(DV, x2dv)], rowsets[k][0:2], (5, 6))
                    for k, at in enumerate((2 * ap, 2 * ap + 1)):
                        def z_out(dc, b1, b2, at_=at):
                            j = dc % 2
                            w1 = kb.emit("vector",
                                lambda v, dc_=dc, j_=j, a_=at_: v.tensor_mul(
                                    zst[j_][:],
                                    acc[:, dc_, a_ * 512:(a_ + 1) * 512], bc0[:]),
                                waits=[(DV, max(b1, b2)), zst_free[j]],
                                inc=(DV, 1))
                            w2 = kb.emit("vector", lambda v, j_=j: v.tensor_sub(
                                zst[j_][:], zst[j_][:], bc1[:]),
                                waits=[(DV, w1)], inc=(DV, 1))
                            d1 = dma(CD, lambda s, dc_=dc, j_=j, a_=at_: s.dma_start(
                                zT_r[:, dc_, a_ * 512:(a_ + 1) * 512], zst[j_][:]),
                                waits=[(DV, w2)])
                            zst_free[j] = (CD, d1)

                        emit_ln_b(ln2_ctx[at], z_out, rowsets[k], (5, 6),
                                  (bc0, bc1))

                routed_war = None
                h_war = None
                gb_dv = {}
                routed_dv = 0

                def emit_gb_routed(e):
                    nonlocal routed_dv
                    if e < 8:
                        for at in range(4):
                            bnk = 4 + at % 2
                            pp = kb.emit("tensor", lambda t, e_=e, at_=at, b_=bnk: t.matmul(
                                pb[b_][:], sel_sb[:, e_ * 128:(e_ + 1) * 128],
                                gatesT[:, at_ * 512:(at_ + 1) * 512],
                                start=True, stop=True),
                                waits=[(DV, gat_dv), bw(bnk)], inc=(PE, 1))
                            gb_dv[at] = kb.emit("vector", lambda v, at_=at, b_=bnk:
                                v.tensor_copy(gbt[:, at_, :], pb[b_][:]),
                                waits=[(PE, pp)], inc=(DV, 1))
                            bank_free[bnk] = (DV, gb_dv[at])
                    for dc in range(DC):
                        i = dc % 2
                        ld = dma(SD, lambda s, dc_=dc, i_=i: s.dma_start(
                            y1s[i_][:], y1b_h[dc_ * 128:(dc_ + 1) * 128, :]),
                            waits=(y1s_free[i] or []))
                        if e < 8:
                            lastv = None
                            for at in range(4):
                                lastv = kb.emit("vector",
                                    lambda v, dc_=dc, at_=at, i_=i: v.tensor_mul(
                                        routed[:, dc_, at_ * 512:(at_ + 1) * 512],
                                        y1s[i_][:, at_ * 512:(at_ + 1) * 512],
                                        gbt[:, at_, :]),
                                    waits=[(SD, ld), (DV, gb_dv[at]), routed_war],
                                    inc=(DV, 1))
                        else:
                            lastv = kb.emit("vector", lambda v, dc_=dc, i_=i: v.tensor_copy(
                                routed[:, dc_, :], y1s[i_][:]),
                                waits=[(SD, ld), routed_war], inc=(DV, 1))
                        y1s_free[i] = [(DV, lastv)]
                    routed_dv = kb.at(DV)

                emit_gb_routed(0)
                for e in range(9):
                    for half in range(2):
                        for fcl in range(8):
                            fc = half * 8 + fcl
                            i = fc % 2
                            ldw = dma(GD, lambda g, e_=e, fc_=fc, i_=i: g.dma_start(
                                wib[i_][:], win_r[e_][:, :, fc_ * 128:(fc_ + 1) * 128]),
                                waits=[wib_free[i]])
                            lds = dma(GD, lambda g, e_=e, fc_=fc, i_=i: g.dma_start(
                                wsb2[i_][:], wsw_r[e_][:, :, fc_ * 128:(fc_ + 1) * 128]),
                                waits=[wsb_free[i]])
                            for at in range(4):
                                lastm = lasts = None
                                for dc in range(DC):
                                    ws = []
                                    if dc == 0:
                                        ws = [(GD, lds), (DV, routed_dv), bw(at)]
                                    lastm = kb.emit("tensor",
                                        lambda t, at_=at, dc_=dc, i_=i: t.matmul(
                                            pb[at_][:], wib[i_][:, dc_, :],
                                            routed[:, dc_, at_ * 512:(at_ + 1) * 512],
                                            start=(dc_ == 0), stop=(dc_ == 7)),
                                        waits=ws, inc=(PE, 1) if dc == 7 else None)
                                    ws2 = [bw(4 + at)] if dc == 0 else []
                                    lasts = kb.emit("tensor",
                                        lambda t, at_=at, dc_=dc, i_=i: t.matmul(
                                            pb[4 + at_][:], wsb2[i_][:, dc_, :],
                                            routed[:, dc_, at_ * 512:(at_ + 1) * 512],
                                            start=(dc_ == 0), stop=(dc_ == 7)),
                                        waits=ws2, inc=(PE, 1) if dc == 7 else None)
                                j = (fc * 4 + at) % 2
                                sl = kb.emit("scalar", lambda sc, at_=at, j_=j: sc.activation(
                                    hm[j_][:], pb[at_][:], AF.Silu),
                                    waits=[(PE, lastm), hm_free[j]], inc=(AC, 1))
                                bank_free[at] = (AC, sl)
                                hmv = kb.emit("vector",
                                    lambda v, fcl_=fcl, at_=at, j_=j: v.tensor_mul(
                                        hbuf[:, fcl_, at_ * 512:(at_ + 1) * 512],
                                        hm[j_][:], pb[4 + at_][:]),
                                    waits=[(AC, sl), (PE, lasts), h_war], inc=(DV, 1))
                                bank_free[4 + at] = (DV, hmv)
                                hm_free[j] = (DV, hmv)
                            wib_free[i] = (PE, kb.at(PE))
                            wsb_free[i] = (PE, kb.at(PE))
                        h_dv = kb.at(DV)
                        if half == 1:
                            routed_war = (PE, kb.at(PE))
                            if e < 8:
                                emit_gb_routed(e + 1)
                        if e == 8 and half == 1:
                            for ap in range(2):
                                for dc in range(DC):
                                    i = dc % 2
                                    ld1 = dma(GD, lambda g, a_=ap, dc_=dc, i_=i:
                                        g.dma_start(
                                            wob[i_][:],
                                            wout_r[8][:, 8:16,
                                                      dc_ * 128:(dc_ + 1) * 128]),
                                        waits=[wob_free[i]])
                                    for at in (2 * ap, 2 * ap + 1):
                                        bnk = (dc % 2) * 2 + (at % 2)
                                        last = None
                                        for fcl in range(8):
                                            ws = []
                                            if fcl == 0:
                                                ws = [(GD, ld1), (DV, h_dv),
                                                      bw(bnk)]
                                            last = kb.emit("tensor",
                                                lambda t, at_=at, fc_=fcl, i_=i,
                                                       b_=bnk: t.matmul(
                                                    pb[b_][:], wob[i_][:, fc_, :],
                                                    hbuf[:, fc_,
                                                         at_ * 512:(at_ + 1) * 512],
                                                    start=(fc_ == 0),
                                                    stop=(fc_ == 7)),
                                                waits=ws,
                                                inc=(PE, 1) if fcl == 7 else None)
                                        ev = kb.emit("vector",
                                            lambda v, dc_=dc, at_=at, b_=bnk:
                                            v.tensor_add(
                                                acc[:, dc_, at_ * 512:(at_ + 1) * 512],
                                                acc[:, dc_, at_ * 512:(at_ + 1) * 512],
                                                pb[b_][:]),
                                            waits=[(PE, last)], inc=(DV, 1))
                                        bank_free[bnk] = (DV, ev)
                                    wob_free[i] = (PE, kb.at(PE))
                                for at in (2 * ap, 2 * ap + 1):
                                    acc_fin[at] = kb.at(DV)
                                ln2_emit(ap)
                        else:
                            for dc in range(DC):
                                i = dc % 2
                                ld1 = dma(GD, lambda g, e_=e, h_=half, dc_=dc, i_=i: g.dma_start(
                                    wob[i_][:], wout_r[e_][:, h_ * 8:(h_ + 1) * 8,
                                                           dc_ * 128:(dc_ + 1) * 128]),
                                    waits=[wob_free[i]])
                                for at in range(4):
                                    bnk = (dc % 2) * 4 + at
                                    last = None
                                    for fcl in range(8):
                                        ws = []
                                        if fcl == 0:
                                            ws = [(GD, ld1), (DV, h_dv), bw(bnk)]
                                        last = kb.emit("tensor",
                                            lambda t, at_=at, fc_=fcl, i_=i, b_=bnk:
                                            t.matmul(
                                                pb[b_][:], wob[i_][:, fc_, :],
                                                hbuf[:, fc_, at_ * 512:(at_ + 1) * 512],
                                                start=(fc_ == 0), stop=(fc_ == 7)),
                                            waits=ws, inc=(PE, 1) if fcl == 7 else None)
                                    if e == 0 and half == 0:
                                        ev = kb.emit("vector",
                                            lambda v, dc_=dc, at_=at, b_=bnk: v.tensor_copy(
                                                acc[:, dc_, at_ * 512:(at_ + 1) * 512], pb[b_][:]),
                                            waits=[(PE, last)], inc=(DV, 1))
                                    else:
                                        ev = kb.emit("vector",
                                            lambda v, dc_=dc, at_=at, b_=bnk: v.tensor_add(
                                                acc[:, dc_, at_ * 512:(at_ + 1) * 512],
                                                acc[:, dc_, at_ * 512:(at_ + 1) * 512], pb[b_][:]),
                                            waits=[(PE, last)], inc=(DV, 1))
                                    bank_free[bnk] = (DV, ev)
                                wob_free[i] = (PE, kb.at(PE))
                        h_war = (PE, kb.at(PE))
                if debug >= 2:
                    acc_dv = kb.at(DV)
                    for dc in range(DC):
                        dma(SD, lambda s, dc_=dc: s.dma_start(
                            acc_h.rearrange("(c p) t -> p c t", p=128)[:, dc_, :],
                            acc[:, dc_, :]), waits=[(DV, acc_dv)])
                kb.barrier()

            kb.barrier()
            kb.barrier()
        kb.build()
    return nc


_CACHE = {}


def _consts():
    sel = np.zeros((8, E * 128), np.float32)
    for e in range(E):
        sel[e, e * 128:(e + 1) * 128] = 1.0
    return {
        "ones128": np.ones((128, 1), BF),
        "ones1f": np.ones((1, 128), np.float32),
        "id128f": np.eye(128, dtype=np.float32),
        "id8f": np.eye(8, dtype=np.float32),
        "sel8": sel,
    }


def kernel(L0_attn_x, L0_attn_Wq, L0_attn_bq, L0_attn_Wk, L0_attn_bk,
           L0_ffn_router_DE, L0_ffn_w_in_shared_FD, L0_ffn_w_out_shared_DF,
           L0_ffn_w_swiglu_FD, L0_ffn_w_in_eDF, L0_ffn_w_swiglu_eDF,
           L0_ffn_w_out_eFD):
    x = np.ascontiguousarray(np.asarray(L0_attn_x, dtype=np.float32))
    B = x.shape[0]
    common = {
        "wqT": np.ascontiguousarray(np.asarray(L0_attn_Wq).T),
        "wk2": np.ascontiguousarray(np.asarray(L0_attn_Wk)),
        "bq": np.ascontiguousarray(np.asarray(L0_attn_bq).reshape(EC, 128).T),
        "router": np.ascontiguousarray(np.asarray(L0_ffn_router_DE)),
        "wish": np.ascontiguousarray(np.asarray(L0_ffn_w_in_shared_FD).T),
        "wssh": np.ascontiguousarray(np.asarray(L0_ffn_w_swiglu_FD).T),
        "wosh": np.ascontiguousarray(np.asarray(L0_ffn_w_out_shared_DF).T),
        "win": np.ascontiguousarray(np.asarray(L0_ffn_w_in_eDF)),
        "wsw": np.ascontiguousarray(np.asarray(L0_ffn_w_swiglu_eDF)),
        "wout": np.ascontiguousarray(np.asarray(L0_ffn_w_out_eFD)),
    }
    common.update(_consts())
    xTb = [np.ascontiguousarray(x[b].T) for b in range(B)]
    in_maps = []
    for c in range(8):
        b = c // 4
        o = (c % 4) * TOK
        m = dict(common)
        m["xT"] = xTb[b]
        m["xN"] = x[b]
        m["xTo"] = np.ascontiguousarray(xTb[b][:, o:o + TOK])
        in_maps.append(m)
    if "nc" not in _CACHE:
        _CACHE["nc"] = build_program(debug=False)
    nc = _CACHE["nc"]
    res = run_bass_kernel_spmd(nc, in_maps, core_ids=list(range(8)))
    out = np.empty((B, S, D), np.float32)
    for c in range(8):
        b = c // 4
        o = (c % 4) * TOK
        out[b, o:o + TOK, :] = res.results[c]["zT"].T
    return out


# version 25
# speedup vs baseline: 1.2304x; 1.0416x over previous
import numpy as np
import ml_dtypes
from contextlib import ExitStack
import concourse.bass as bass
import concourse.mybir as mybir
from concourse.bass_utils import run_bass_kernel_spmd

F32 = mybir.dt.float32
BF16 = mybir.dt.bfloat16
AF = mybir.ActivationFunctionType
ALU = mybir.AluOpType
AX = mybir.AxisListType
f32 = np.float32
BF = ml_dtypes.bfloat16

D = 1024
S = 8192
TOK = 2048
E = 8
FF = 2048
DC = 8
EC = 8
FC = 16
QB = 4
NG = 8
CEXP = 9.0
PAIR_O = False

ENGINES = ["sync", "tensor", "vector", "scalar", "gpsimd"]
SD, GD, SC, CD = "sd", "gd", "sc", "cd"
NPOOL = 10
PE, DV, AC = "pe", "dve", "act"


class KB:

    def __init__(self, nc):
        self.nc = nc
        self.ops = {e: [] for e in ENGINES}
        self.cnt = {}
        self.semh = {}

    def at(self, sem):
        return self.cnt.get(sem, 0)

    def emit(self, eng, fn, waits=(), inc=None):
        def norm(x):
            if isinstance(x[1], tuple):
                return x[1]
            return x

        w = []
        for x in waits:
            if x is None:
                continue
            if isinstance(x, list):
                w.extend(norm(y) for y in x if y is not None and norm(y)[1] > 0)
            elif norm(x)[1] > 0:
                w.append(norm(x))
        v = None
        if inc is not None:
            self.cnt[inc[0]] = self.cnt.get(inc[0], 0) + inc[1]
            v = self.cnt[inc[0]]
        self.ops[eng].append((fn, w, inc))
        return v

    def barrier(self):
        snap = [(s, v) for s, v in self.cnt.items() if v > 0]
        for eng in ENGINES:
            self.ops[eng].append((None, list(snap), None))

    def build(self):
        for name in list(self.cnt):
            if name not in self.semh:
                self.semh[name] = self.nc.alloc_semaphore(name)

        def replayer(eng_name):
            def f(eng):
                last = {}
                for fn, waits, inc in self.ops[eng_name]:
                    for s_, v_ in waits:
                        if last.get(s_, 0) >= v_:
                            continue
                        eng.wait_ge(self.semh[s_], v_)
                        last[s_] = v_
                    if fn is None:
                        continue
                    inst = fn(eng)
                    if inc is not None:
                        inst.then_inc(self.semh[inc[0]], inc[1])
            return f

        with self.nc.Block() as blk:
            blk.sync(replayer("sync"))
            blk.tensor(replayer("tensor"))
            blk.vector(replayer("vector"))
            blk.scalar(replayer("scalar"))
            blk.gpsimd(replayer("gpsimd"))


def build_program(debug=False):
    nc = bass.Bass("TRN2", target_bir_lowering=False, debug=False)
    kb = KB(nc)

    xT = nc.dram_tensor("xT", [D, S], F32, kind="ExternalInput").ap()
    xTo = nc.dram_tensor("xTo", [D, TOK], F32, kind="ExternalInput").ap()
    xN = nc.dram_tensor("xN", [S, D], F32, kind="ExternalInput").ap()
    wqT = nc.dram_tensor("wqT", [D, D], F32, kind="ExternalInput").ap()
    wk2 = nc.dram_tensor("wk2", [D, D], F32, kind="ExternalInput").ap()
    bq = nc.dram_tensor("bq", [128, EC], F32, kind="ExternalInput").ap()
    router = nc.dram_tensor("router", [D, E], F32, kind="ExternalInput").ap()
    wish = nc.dram_tensor("wish", [D, FF], F32, kind="ExternalInput").ap()
    wssh = nc.dram_tensor("wssh", [D, FF], F32, kind="ExternalInput").ap()
    wosh = nc.dram_tensor("wosh", [FF, D], F32, kind="ExternalInput").ap()
    win = nc.dram_tensor("win", [E, D, FF], F32, kind="ExternalInput").ap()
    wsw = nc.dram_tensor("wsw", [E, D, FF], F32, kind="ExternalInput").ap()
    wout = nc.dram_tensor("wout", [E, FF, D], F32, kind="ExternalInput").ap()
    ones128 = nc.dram_tensor("ones128", [128, 1], BF16, kind="ExternalInput").ap()
    ones1f = nc.dram_tensor("ones1f", [1, 128], F32, kind="ExternalInput").ap()
    id128f = nc.dram_tensor("id128f", [128, 128], F32, kind="ExternalInput").ap()
    id8f = nc.dram_tensor("id8f", [8, 8], F32, kind="ExternalInput").ap()
    sel8 = nc.dram_tensor("sel8", [8, E * 128], F32, kind="ExternalInput").ap()

    zT = nc.dram_tensor("zT", [D, TOK], F32, kind="ExternalOutput").ap()
    xbT_h = nc.dram_tensor("xbT_h", [D, S], BF16, kind="Internal").ap()
    y1b_h = nc.dram_tensor("y1b_h", [D, TOK], BF16, kind="Internal").ap()
    y1f_h = nc.dram_tensor("y1f_h", [D, TOK], F32,
                           kind="ExternalOutput" if debug else "Internal").ap()
    if debug >= 2:
        gat_h = nc.dram_tensor("gat_h", [8, TOK], F32, kind="ExternalOutput").ap()
        acc_h = nc.dram_tensor("acc_h", [D, TOK], F32, kind="ExternalOutput").ap()
        sco_h = nc.dram_tensor("sco_h", [8, TOK], F32, kind="ExternalOutput").ap()
        stok_h = nc.dram_tensor("stok_h", [128, 128], F32, kind="ExternalOutput").ap()
        gtok_h = nc.dram_tensor("gtok_h", [128, 128], F32, kind="ExternalOutput").ap()

    xT_r = xT.rearrange("(c p) t -> p c t", p=128)
    xN_r = xN.rearrange("(c p) d -> p c d", p=128)
    xTo_r = xTo.rearrange("(c p) t -> p c t", p=128)
    wqT_r = wqT.rearrange("(c p) e -> p c e", p=128)
    wk2_r = wk2.rearrange("(c p) d -> p c d", p=128)
    xbT_r = xbT_h.rearrange("(c p) t -> p c t", p=128)
    y1f_r = y1f_h.rearrange("(c p) t -> p c t", p=128)
    zT_r = zT.rearrange("(c p) t -> p c t", p=128)
    rtr_r = router.rearrange("(c p) e -> p c e", p=128)
    win_r = [win[e].rearrange("(c p) f -> p c f", p=128) for e in range(E)] + \
            [wish.rearrange("(c p) f -> p c f", p=128)]
    wsw_r = [wsw[e].rearrange("(c p) f -> p c f", p=128) for e in range(E)] + \
            [wssh.rearrange("(c p) f -> p c f", p=128)]
    wout_r = [wout[e].rearrange("(c p) d -> p c d", p=128) for e in range(E)] + \
             [wosh.rearrange("(c p) d -> p c d", p=128)]

    es = ExitStack()
    with es:
        sb = lambda name, shape, dt: es.enter_context(nc.sbuf_tensor(name, shape, dt))
        ones_sb = sb("ones_sb", [128, 1], BF16)
        ones1f_sb = sb("ones1f_sb", [1, 128], F32)
        bq_sb = sb("bq_sb", [128, EC], F32)
        cs_sb = sb("cs_sb", [128, DC], F32)
        cs_tmp = sb("cs_tmp", [128, DC], F32)
        tmp1_sb = sb("tmp1_sb", [128, 1], F32)
        rA = sb("rA", [1, 512], F32)
        rB = sb("rB", [1, 512], F32)
        rC = sb("rC", [1, 512], F32)
        rD = sb("rD", [1, 512], F32)
        bc0 = sb("bc0", [128, 512], F32)
        bc1 = sb("bc1", [128, 512], F32)

        pb = [es.enter_context(nc.psum_tensor(f"pb{i}", [128, 512], F32))
              for i in range(8)]
        bank_free = {}
        state = {"ln_war": None}

        dma_idx = {SD: 0, GD: 0, CD: 0}

        def dma(qsem, fn, waits=()):
            eng = {GD: "gpsimd", CD: "scalar"}.get(qsem, "sync")
            if qsem == SC:
                return kb.emit(eng, fn, waits=waits, inc=(SC, 16))
            phys = f"{qsem}{dma_idx[qsem] % NPOOL}"
            dma_idx[qsem] += 1
            v = kb.emit(eng, fn, waits=waits, inc=(phys, 16))
            return (phys, v)

        def bw(i):
            return bank_free.get(i)

        ln_war = {}

        def emit_ln_a(src, ub, ready, rows, banks):
            rS, rQ = rows
            bA, bB = banks
            war = ln_war.get(ub.name)
            cps = []
            for dc in range(DC):
                cps.append(kb.emit("vector", lambda v, dc_=dc: v.tensor_copy(
                    ub[:, dc_, :], src(dc_)),
                    waits=list(ready) + ([war] if dc == 0 else []), inc=(DV, 1)))
            b = bw(bA)
            last = None
            for dc in range(DC):
                last = kb.emit("tensor", lambda t, dc_=dc, b_=bA: t.matmul(
                    pb[b_][0:1, :], ones_sb[:], ub[:, dc_, :],
                    start=(dc_ == 0), stop=(dc_ == DC - 1)),
                    waits=[(DV, cps[dc]), b if dc == 0 else None],
                    inc=(PE, 1) if dc == DC - 1 else None)
            s1 = kb.emit("vector", lambda v, b_=bA: v.tensor_copy(rS[:], pb[b_][0:1, :]),
                         waits=[(PE, last)], inc=(DV, 1))
            bank_free[bA] = (DV, s1)
            sqs = []
            for dc in range(DC):
                sqs.append(kb.emit("vector", lambda v, dc_=dc: v.tensor_mul(
                    ub[:, dc_, :], ub[:, dc_, :], ub[:, dc_, :]),
                    waits=[(PE, last)], inc=(DV, 1)))
            b = bw(bB)
            last2 = None
            for dc in range(DC):
                last2 = kb.emit("tensor", lambda t, dc_=dc, b_=bB: t.matmul(
                    pb[b_][0:1, :], ones_sb[:], ub[:, dc_, :],
                    start=(dc_ == 0), stop=(dc_ == DC - 1)),
                    waits=[(DV, sqs[dc]), b if dc == 0 else None],
                    inc=(PE, 1) if dc == DC - 1 else None)
            ln_war[ub.name] = (PE, last2)
            s2 = kb.emit("vector", lambda v, b_=bB: v.tensor_copy(rQ[:], pb[b_][0:1, :]),
                         waits=[(PE, last2)], inc=(DV, 1))
            bank_free[bB] = (DV, s2)
            return dict(s1=s1, s2=s2)

        def emit_ln_b(ctx, out_cb, rows, banks, bcs):
            rS, rQ, rT, rU = rows
            bA, bB = banks
            bcA, bcB = bcs
            kb.emit("vector", lambda v: v.tensor_scalar(
                rS[:], rS[:], 1.0 / D, None, op0=ALU.mult),
                waits=[(DV, ctx["s1"])], inc=(DV, 1))
            kb.emit("vector", lambda v: v.tensor_scalar(
                rQ[:], rQ[:], 1.0 / D, None, op0=ALU.mult),
                waits=[(DV, ctx["s2"])], inc=(DV, 1))
            kb.emit("vector", lambda v: v.tensor_mul(rT[:], rS[:], rS[:]),
                    waits=[(DV, kb.at(DV))], inc=(DV, 1))
            vv = kb.emit("vector", lambda v: v.tensor_sub(rQ[:], rQ[:], rT[:]),
                         waits=[(DV, kb.at(DV))], inc=(DV, 1))
            a1 = kb.emit("scalar", lambda sc: sc.activation(
                rT[:], rQ[:], AF.Ln, bias=f32(1e-5)),
                waits=[(DV, vv)], inc=(AC, 1))
            a2 = kb.emit("scalar", lambda sc: sc.activation(
                rU[:], rT[:], AF.Exp, scale=f32(-0.5)),
                waits=[(AC, a1)], inc=(AC, 1))
            mr = kb.emit("vector", lambda v: v.tensor_mul(rT[:], rS[:], rU[:]),
                         waits=[(AC, a2)], inc=(DV, 1))
            b = bw(bA)
            p1 = kb.emit("tensor", lambda t, b_=bA: t.matmul(
                pb[b_][:], ones1f_sb[:], rU[:], start=True, stop=True),
                waits=[(AC, a2), b], inc=(PE, 1))
            b1 = kb.emit("vector", lambda v, b_=bA: v.tensor_copy(bcA[:], pb[b_][:]),
                         waits=[(PE, p1)], inc=(DV, 1))
            bank_free[bA] = (DV, b1)
            b = bw(bB)
            p2 = kb.emit("tensor", lambda t, b_=bB: t.matmul(
                pb[b_][:], ones1f_sb[:], rT[:], start=True, stop=True),
                waits=[(DV, mr), b], inc=(PE, 1))
            b2 = kb.emit("vector", lambda v, b_=bB: v.tensor_copy(bcB[:], pb[b_][:]),
                         waits=[(PE, p2)], inc=(DV, 1))
            bank_free[bB] = (DV, b2)
            for dc in range(DC):
                out_cb(dc, b1, b2)

        dma(SC, lambda s: s.dma_start(ones_sb[:], ones128))
        dma(SC, lambda s: s.dma_start(ones1f_sb[:], ones1f))
        dma(SC, lambda s: s.dma_start(bq_sb[:], bq))
        c0 = kb.at(SC)

        ph12 = ExitStack()
        with ph12:
            q2T = ph12.enter_context(nc.sbuf_tensor("q2T", [128, DC, TOK], BF16))

            ph1 = ExitStack()
            with ph1:
                wq_sb = ph1.enter_context(nc.sbuf_tensor("wq_sb", [128, DC, D], BF16))
                wk2_sb = ph1.enter_context(nc.sbuf_tensor("wk2_sb", [128, DC, D], BF16))
                qT = ph1.enter_context(nc.sbuf_tensor("qT", [128, EC, TOK], BF16))
                xq = [ph1.enter_context(nc.sbuf_tensor(f"xq{i}", [128, DC, 512], BF16))
                      for i in range(4)]
                xsf = [ph1.enter_context(nc.sbuf_tensor(f"xsf{i}", [128, DC, 512], F32))
                       for i in range(2)]
                xs = [ph1.enter_context(nc.sbuf_tensor(f"xs{i}", [128, DC, 512], BF16))
                      for i in range(2)]

                wready = dma(GD, lambda g: g.dma_start(wq_sb[:], wqT_r))
                w2ready = dma(GD, lambda g: g.dma_start(wk2_sb[:], wk2_r))

                lds = {}
                qp_dv = {}
                for t in range(QB):
                    lds[t] = dma(GD, lambda g, t_=t: g.dma_start(
                        xq[t_][:], xTo_r[:, :, t_ * 512:(t_ + 1) * 512]), waits=[])

                def emit_qp(t):
                    lpe = None
                    for ec in range(EC):
                        bi = ec % 2
                        last_mm = None
                        for dc in range(DC):
                            ws = []
                            if dc == 0:
                                ws = [(GD, lds[t]), (GD, wready), bw(bi)]
                            last_mm = kb.emit(
                                "tensor",
                                lambda tn, ec_=ec, dc_=dc, t_=t, b_=bi: tn.matmul(
                                    pb[b_][:], wq_sb[:, dc_, ec_ * 128:(ec_ + 1) * 128],
                                    xq[t_][:, dc_, :], start=(dc_ == 0), stop=(dc_ == 7)),
                                waits=ws, inc=((PE, 1) if dc == 7 else None))
                        lpe = last_mm
                        ev = kb.emit(
                            "vector",
                            lambda v, ec_=ec, b_=bi, t_=t: v.tensor_scalar(
                                qT[:, ec_, t_ * 512:(t_ + 1) * 512], pb[b_][:],
                                bq_sb[:, ec_:ec_ + 1], None, op0=ALU.add),
                            waits=[(PE, last_mm), (SC, c0)], inc=(DV, 1))
                        bank_free[bi] = (DV, ev)
                    qp_dv[t] = kb.at(DV)

                def emit_q2(t):
                    for dc in range(DC):
                        b2 = 2 + dc % 2
                        last_mm = None
                        for ec in range(EC):
                            ws = []
                            if ec == 0:
                                ws = [(DV, qp_dv[t]), (GD, w2ready), bw(b2)]
                            last_mm = kb.emit(
                                "tensor",
                                lambda tn, ec_=ec, dc_=dc, b_=b2, t_=t: tn.matmul(
                                    pb[b_][:], wk2_sb[:, ec_, dc_ * 128:(dc_ + 1) * 128],
                                    qT[:, ec_, t_ * 512:(t_ + 1) * 512],
                                    start=(ec_ == 0), stop=(ec_ == 7)),
                                waits=ws, inc=((PE, 1) if ec == 7 else None))
                        ev = kb.emit(
                            "vector",
                            lambda v, dc_=dc, b_=b2, t_=t: v.tensor_copy(
                                q2T[:, dc_, t_ * 512:(t_ + 1) * 512], pb[b_][:]),
                            waits=[(PE, last_mm)], inc=(DV, 1))
                        bank_free[b2] = (DV, ev)

                emit_qp(0)
                emit_qp(1)
                emit_q2(0)
                emit_qp(2)
                emit_q2(1)
                emit_qp(3)
                emit_q2(2)
                emit_q2(3)

                xsf_free = [None, None]
                xs_free = [None, None]
                for sl in range(S // 512):
                    i = sl % 2
                    ld = dma(SD, lambda s, sl_=sl, i_=i: s.dma_start(
                        xsf[i_][:], xT_r[:, :, sl_ * 512:(sl_ + 1) * 512]),
                        waits=[xsf_free[i]])
                    casts = []
                    for dc in range(DC):
                        casts.append(kb.emit(
                            "scalar", lambda sc, dc_=dc, i_=i: sc.activation(
                                xs[i_][:, dc_, :], xsf[i_][:, dc_, :], AF.Copy),
                            waits=([(SD, ld), xs_free[i]] if dc == 0 else []),
                            inc=(AC, 1)))
                    st = dma(CD, lambda s, sl_=sl, i_=i: s.dma_start(
                        xbT_r[:, :, sl_ * 512:(sl_ + 1) * 512], xs[i_][:]),
                        waits=[(AC, casts[-1])])
                    rd = kb.emit("vector", lambda v, i_=i: v.reduce_sum(
                        cs_tmp[:], xsf[i_][:], axis=AX.X),
                        waits=[(SD, ld)] + ([(DV, ad)] if sl > 0 else []),
                        inc=(DV, 1))
                    ad = kb.emit("vector", lambda v, first=(sl == 0):
                        (v.tensor_copy(cs_sb[:], cs_tmp[:]) if first else
                         v.tensor_add(cs_sb[:], cs_sb[:], cs_tmp[:])),
                        waits=[(DV, rd)], inc=(DV, 1))
                    xsf_free[i] = [(AC, casts[-1]), (DV, rd)]
                    xs_free[i] = [(CD, st)]
                kb.emit("vector", lambda v: v.tensor_scalar(
                    cs_sb[:], cs_sb[:], 2.0 / S, None, op0=ALU.mult),
                    waits=[(DV, kb.at(DV))], inc=(DV, 1))
                kb.barrier()

            ph2 = ExitStack()
            with ph2:
                kg = [ph2.enter_context(nc.sbuf_tensor(f"kg{i}", [128, EC, 1024], BF16))
                      for i in range(2)]
                vg = [ph2.enter_context(nc.sbuf_tensor(f"vg{i}", [128, 8, 1024], BF16))
                      for i in range(2)]
                Pt = [ph2.enter_context(nc.sbuf_tensor(f"Pt{i}", [128, 8, 512], BF16))
                      for i in range(2)]
                Obuf = [ph2.enter_context(nc.sbuf_tensor(f"O{i}", [128, DC, 512], F32))
                        for i in range(2)]
                xow = ph2.enter_context(nc.sbuf_tensor("xow", [128, DC, 512], F32))
                ub2 = ph2.enter_context(nc.sbuf_tensor("ub2", [128, DC, 512], BF16))
                y1f_st = [ph2.enter_context(nc.sbuf_tensor(f"y1f{i}", [128, 512], F32))
                          for i in range(2)]
                y1b_st = [ph2.enter_context(nc.sbuf_tensor(f"y1b{i}", [128, 512], BF16))
                          for i in range(2)]

                kg_free = [None, None]
                vg_free = [None, None]
                Pt_free = [None, None]
                y1f_free = [None, None]
                y1b_free = [None, None]
                exp_done = {}
                vg_ld = {}
                kg_ld = {}
                xow_ld = {}
                qb_lpe = {}
                epi_udv = {}
                epi_ctx = {}
                xc_dv = {}

                def emit_xc(qb):
                    last = None
                    for dc in range(DC):
                        last = kb.emit("vector", lambda v, dc_=dc: v.tensor_scalar(
                            xow[:, dc_, :], xow[:, dc_, :], 2.0, cs_sb[:, dc_:dc_ + 1],
                            op0=ALU.mult, op1=ALU.subtract),
                            waits=[(SD, xow_ld[qb]), (DV, kb.at(DV))], inc=(DV, 1))
                    xc_dv[qb] = last

                def emit_slot(qb, g):
                    O = Obuf[qb % 2]
                    lb4 = 4 if qb % 2 == 0 else 7
                    if g < NG:
                        i = g % 2
                        kg_ld[(qb, g)] = dma(SD, lambda s, g_=g, i_=i: s.dma_start(
                            kg[i_][:], xbT_r[:, :, g_ * 1024:(g_ + 1) * 1024]),
                            waits=[kg_free[i]])
                        vg_ld[(qb, g)] = dma(GD, lambda gp, g_=g, i_=i:
                            gp.dma_start(vg[i_][:],
                                         xN_r[:, g_ * 8:(g_ + 1) * 8, :]),
                            waits=[vg_free[i]])
                        for tt in range(8):
                            lb = tt % 2
                            last = None
                            for ec in range(EC):
                                ws = []
                                if ec == 0:
                                    ws = [(SD, kg_ld[(qb, g)]), bw(lb)]
                                last = kb.emit(
                                    "tensor",
                                    lambda t, ec_=ec, tt_=tt, i_=i, lb_=lb, qb_=qb:
                                    t.matmul(
                                        pb[lb_][:],
                                        kg[i_][:, ec_, tt_ * 128:(tt_ + 1) * 128],
                                        q2T[:, ec_, qb_ * 512:(qb_ + 1) * 512],
                                        start=(ec_ == 0), stop=(ec_ == 7)),
                                    waits=ws, inc=((PE, 1) if ec == 7 else None))
                            ws = [(PE, last)]
                            if tt == 0:
                                ws.append(Pt_free[i])
                            ex = kb.emit(
                                "scalar",
                                lambda sc, tt_=tt, i_=i, lb_=lb: sc.activation(
                                    Pt[i_][:, tt_, :], pb[lb_][:], AF.Exp,
                                    bias=f32(-CEXP), scale=f32(1.0 / 32.0)),
                                waits=ws, inc=(AC, 1))
                            exp_done[(qb, g, tt)] = ex
                            bank_free[lb] = (AC, ex)
                    if g > 0:
                        gp_ = g - 1
                        i = gp_ % 2
                        for tt in range(8):
                            ws = [(AC, exp_done[(qb, gp_, tt)])]
                            if gp_ == 0 and tt == 0:
                                ws.append(bw(lb4))
                            kb.emit(
                                "tensor",
                                lambda t, tt_=tt, i_=i, lb_=lb4,
                                       st_=(gp_ == 0 and tt == 0),
                                       sp_=(gp_ == NG - 1 and tt == 7): t.matmul(
                                    pb[lb_][0:1, :], ones_sb[:], Pt[i_][:, tt_, :],
                                    start=st_, stop=sp_),
                                waits=ws, inc=((PE, 1) if tt == 7 else None))
                        if gp_ == NG - 1:
                            qb_lpe[qb] = kb.at(PE)
                        first_half = (gp_ % 2 == 0) and PAIR_O
                        for dc in range(DC):
                            ob = 2 + dc % 2
                            last = None
                            for tt in range(8):
                                ws = []
                                if tt == 0:
                                    ws = [(AC, exp_done[(qb, gp_, 7)]),
                                          (GD, vg_ld[(qb, gp_)])]
                                    if first_half or not PAIR_O:
                                        ws.append(bw(ob))
                                last = kb.emit(
                                    "tensor",
                                    lambda t, dc_=dc, tt_=tt, i_=i, ob_=ob,
                                           st_=(tt == 0 and (first_half or not PAIR_O)),
                                           sp_=((not first_half) and tt == 7): t.matmul(
                                        pb[ob_][:],
                                        vg[i_][:, tt_, dc_ * 128:(dc_ + 1) * 128],
                                        Pt[i_][:, tt_, :],
                                        start=st_, stop=sp_),
                                    waits=ws, inc=((PE, 1) if tt == 7 else None))
                            if not first_half:
                                pair = gp_ // 2 if PAIR_O else gp_
                                if pair == 0:
                                    ev = kb.emit("vector",
                                                 lambda v, dc_=dc, ob_=ob, O_=O: v.tensor_copy(
                                                     O_[:, dc_, :], pb[ob_][:]),
                                                 waits=[(PE, last)], inc=(DV, 1))
                                else:
                                    ev = kb.emit("vector",
                                                 lambda v, dc_=dc, ob_=ob, O_=O: v.tensor_add(
                                                     O_[:, dc_, :], O_[:, dc_, :], pb[ob_][:]),
                                                 waits=[(PE, last)], inc=(DV, 1))
                                bank_free[ob] = (DV, ev)
                        Pt_free[i] = (PE, kb.at(PE))
                        vg_free[i] = (PE, kb.at(PE))
                    if g < NG:
                        kg_free[g % 2] = (PE, kb.at(PE))

                def emit_epi_a(qb):
                    O = Obuf[qb % 2]
                    lb4 = 4 if qb % 2 == 0 else 7
                    c1 = kb.emit("vector", lambda v, lb_=lb4: v.tensor_copy(
                        rC[:], pb[lb_][0:1, :]),
                        waits=[(PE, qb_lpe[qb])], inc=(DV, 1))
                    bank_free[lb4] = (DV, c1)
                    p1 = kb.emit("tensor", lambda t: t.matmul(
                        pb[5][:], ones1f_sb[:], rC[:], start=True, stop=True),
                        waits=[(DV, c1), bw(5)], inc=(PE, 1))
                    c3 = kb.emit("vector", lambda v: v.reciprocal(bc0[:], pb[5][:]),
                                 waits=[(PE, p1)], inc=(DV, 1))
                    bank_free[5] = (DV, c3)
                    for dc in range(DC):
                        kb.emit("vector", lambda v, dc_=dc, O_=O: v.tensor_mul(
                            O_[:, dc_, :], O_[:, dc_, :], bc0[:]),
                            waits=[(DV, c3), (DV, xc_dv[qb])], inc=(DV, 1))
                        kb.emit("vector", lambda v, dc_=dc, O_=O: v.tensor_add(
                            O_[:, dc_, :], O_[:, dc_, :], xow[:, dc_, :]),
                            waits=[(DV, kb.at(DV))], inc=(DV, 1))
                    epi_udv[qb] = kb.at(DV)
                    if qb + 1 < QB:
                        xow_ld[qb + 1] = dma(CD, lambda s, qb_=qb + 1: s.dma_start(
                            xow[:], xTo_r[:, :, qb_ * 512:(qb_ + 1) * 512]),
                            waits=[(DV, epi_udv[qb])])

                def emit_epi_b1(qb):
                    O = Obuf[qb % 2]
                    epi_ctx[qb] = emit_ln_a(lambda dc: O[:, dc, :], ub2,
                                            [(DV, epi_udv[qb])], (rA, rB), (5, 6))

                def emit_epi_b2(qb):
                    O = Obuf[qb % 2]

                    def y1_out(dc, b1, b2, qb_=qb, O_=O):
                        i = dc % 2
                        w1 = kb.emit("vector", lambda v, dc_=dc, i_=i: v.tensor_mul(
                            y1f_st[i_][:], O_[:, dc_, :], bc0[:]),
                            waits=[(DV, max(b1, b2)), y1f_free[i]], inc=(DV, 1))
                        w2 = kb.emit("vector", lambda v, i_=i: v.tensor_sub(
                            y1f_st[i_][:], y1f_st[i_][:], bc1[:]),
                            waits=[(DV, w1)], inc=(DV, 1))
                        cb = kb.emit("vector", lambda v, i_=i: v.tensor_copy(
                            y1b_st[i_][:], y1f_st[i_][:]),
                            waits=[(DV, w2), y1b_free[i]], inc=(DV, 1))
                        d1 = dma(CD, lambda s, dc_=dc, i_=i: s.dma_start(
                            y1f_r[:, dc_, qb_ * 512:(qb_ + 1) * 512], y1f_st[i_][:]),
                            waits=[(DV, w2)])
                        d2 = dma(CD, lambda s, dc_=dc, i_=i: s.dma_start(
                            y1b_h[dc_ * 128:(dc_ + 1) * 128,
                                  qb_ * 512:(qb_ + 1) * 512], y1b_st[i_][:]),
                            waits=[(DV, cb)])
                        y1f_free[i] = [(CD, d1), (DV, cb)]
                        y1b_free[i] = (CD, d2)

                    emit_ln_b(epi_ctx[qb], y1_out, (rA, rB, rC, rD), (5, 6), (bc0, bc1))

                xow_ld[0] = dma(SD, lambda s: s.dma_start(
                    xow[:], xTo_r[:, :, 0:512]), waits=[])
                emit_xc(0)
                for qb in range(QB):
                    for g in range(0 if qb == 0 else 3, NG + 1):
                        emit_slot(qb, g)
                    if qb + 1 < QB:
                        emit_slot(qb + 1, 0)
                        emit_epi_a(qb)
                        emit_slot(qb + 1, 1)
                        emit_epi_b1(qb)
                        emit_slot(qb + 1, 2)
                        emit_epi_b2(qb)
                        emit_xc(qb + 1)
                    else:
                        emit_epi_a(qb)
                        emit_epi_b1(qb)
                        emit_epi_b2(qb)
                kb.barrier()
        ph34 = ExitStack()
        with ph34:
            acc = ph34.enter_context(nc.sbuf_tensor("acc", [128, DC, TOK], F32))
            ph3 = ExitStack()
            with ph3:
                y1s = [ph3.enter_context(nc.sbuf_tensor(f"y1s{i}", [128, TOK], BF16))
                       for i in range(2)]
                wib = [ph3.enter_context(nc.sbuf_tensor(f"wib{i}", [128, DC, 128], BF16))
                       for i in range(2)]
                wsb2 = [ph3.enter_context(nc.sbuf_tensor(f"wsb{i}", [128, DC, 128], BF16))
                        for i in range(2)]
                wob = [ph3.enter_context(nc.sbuf_tensor(f"wob{i}", [128, 8, 128], BF16))
                       for i in range(2)]
                gbt = ph3.enter_context(nc.sbuf_tensor("gbt", [128, 4, 512], BF16))
                hm = [ph3.enter_context(nc.sbuf_tensor(f"hm{i}", [128, 512], BF16))
                      for i in range(2)]
                rt_sb = ph3.enter_context(nc.sbuf_tensor("rt_sb", [128, DC, 8], BF16))
                sel_sb = ph3.enter_context(nc.sbuf_tensor("sel_sb", [8, E * 128], F32))
                id128_sb = ph3.enter_context(nc.sbuf_tensor("id128_sb", [128, 128], F32))
                id8_sb = ph3.enter_context(nc.sbuf_tensor("id8_sb", [8, 8], F32))
                gatesT = ph3.enter_context(nc.sbuf_tensor("gatesT", [8, TOK], F32))

                dma(SC, lambda s: s.dma_start(sel_sb[:], sel8))
                dma(SC, lambda s: s.dma_start(id128_sb[:], id128f))
                dma(SC, lambda s: s.dma_start(id8_sb[:], id8f))
                cph3 = kb.at(SC)

                y1s_free = [None, None]
                wib_free = [None, None]
                wsb_free = [None, None]
                wob_free = [None, None]
                hm_free = [None, None]

                gsc = ExitStack()
                with gsc:
                    scoresT = gsc.enter_context(nc.sbuf_tensor("scoresT", [8, TOK], F32))
                    s_tok = gsc.enter_context(nc.sbuf_tensor("s_tok", [128, 16, 8], F32))
                    g_tok = gsc.enter_context(nc.sbuf_tensor("g_tok", [128, 16, 8], F32))
                    m1_sb = [gsc.enter_context(nc.sbuf_tensor(f"m1_sb{i}", [128, 1], F32)) for i in range(2)]
                    m2_sb = [gsc.enter_context(nc.sbuf_tensor(f"m2_sb{i}", [128, 1], F32)) for i in range(2)]
                    msk_sb = [gsc.enter_context(nc.sbuf_tensor(f"msk_sb{i}", [128, 8], F32)) for i in range(2)]
                    sk2_sb = [gsc.enter_context(nc.sbuf_tensor(f"sk2_sb{i}", [128, 8], F32)) for i in range(2)]
                    sig_sb = [gsc.enter_context(nc.sbuf_tensor(f"sig_sb{i}", [128, 8], F32)) for i in range(2)]

                    rt_ld = dma(GD, lambda g: g.dma_start(rt_sb[:], rtr_r))
                    last_rt = {}
                    for dc in range(DC):
                        i = dc % 2
                        ld = dma(SD, lambda s, dc_=dc, i_=i: s.dma_start(
                            y1s[i_][:], y1b_h[dc_ * 128:(dc_ + 1) * 128, :]),
                            waits=(y1s_free[i] or []))
                        for at in range(4):
                            ws = [(SD, ld), (GD, rt_ld)]
                            if dc == 0:
                                ws.append(bw(at))
                            last_rt[at] = kb.emit(
                                "tensor",
                                lambda t, dc_=dc, at_=at, i_=i: t.matmul(
                                    pb[at_][0:8, :], rt_sb[:, dc_, :],
                                    y1s[i_][:, at_ * 512:(at_ + 1) * 512],
                                    start=(dc_ == 0), stop=(dc_ == 7)),
                                waits=ws, inc=(PE, 1) if at == 3 else None)
                        y1s_free[i] = [(PE, kb.at(PE))]
                    rt_pe = kb.at(PE)
                    for at in range(4):
                        ev = kb.emit("vector", lambda v, at_=at: v.tensor_copy(
                            scoresT[:, at_ * 512:(at_ + 1) * 512], pb[at_][0:8, :]),
                            waits=[(PE, rt_pe)], inc=(DV, 1))
                        bank_free[at] = (DV, ev)
                    sc_dv = kb.at(DV)

                    for j in range(16):
                        bk4 = 4 + j % 2
                        tp = kb.emit("tensor", lambda t, j_=j, b_=bk4: t.transpose(
                            pb[b_][:, 0:8], scoresT[:, j_ * 128:(j_ + 1) * 128], id8_sb[:]),
                            waits=[(DV, sc_dv), (SC, cph3), bw(bk4)], inc=(PE, 1))
                        cv = kb.emit("vector", lambda v, j_=j, b_=bk4: v.tensor_copy(
                            s_tok[:, j_, :], pb[b_][:, 0:8]),
                            waits=[(PE, tp)], inc=(DV, 1))
                        bank_free[bk4] = (DV, cv)
                    stok_dv = kb.at(DV)
                    gmul = {}
                    for j in range(16):
                        p = j % 2
                        r1 = kb.emit("vector", lambda v, j_=j, p_=p: v.reduce_max(
                            m1_sb[p_][:], s_tok[:, j_, :], axis=AX.X),
                            waits=[(DV, stok_dv if j < 2 else gmul[j - 2])], inc=(DV, 1))
                        k1 = kb.emit("vector", lambda v, j_=j, p_=p: v.tensor_scalar(
                            msk_sb[p_][:], s_tok[:, j_, :], m1_sb[p_][:, 0:1], None,
                            op0=ALU.is_ge),
                            waits=[(DV, r1)], inc=(DV, 1))
                        k2 = kb.emit("vector", lambda v, j_=j, p_=p: v.scalar_tensor_tensor(
                            sk2_sb[p_][:], msk_sb[p_][:], -1e30, s_tok[:, j_, :],
                            op0=ALU.mult, op1=ALU.add),
                            waits=[(DV, k1)], inc=(DV, 1))
                        r2 = kb.emit("vector", lambda v, p_=p: v.reduce_max(
                            m2_sb[p_][:], sk2_sb[p_][:], axis=AX.X),
                            waits=[(DV, k2)], inc=(DV, 1))
                        k3 = kb.emit("vector", lambda v, j_=j, p_=p: v.tensor_scalar(
                            msk_sb[p_][:], s_tok[:, j_, :], m2_sb[p_][:, 0:1], None,
                            op0=ALU.is_ge),
                            waits=[(DV, r2)], inc=(DV, 1))
                        sg = kb.emit("scalar", lambda sc, j_=j, p_=p: sc.activation(
                            sig_sb[p_][:], s_tok[:, j_, :], AF.Sigmoid),
                            waits=[(DV, stok_dv if j < 2 else gmul[j - 2])], inc=(AC, 1))
                        gmul[j] = kb.emit("vector", lambda v, j_=j, p_=p: v.tensor_mul(
                            g_tok[:, j_, :], sig_sb[p_][:], msk_sb[p_][:]),
                            waits=[(DV, k3), (AC, sg)], inc=(DV, 1))
                    for j in range(16):
                        bk4 = 4 + j % 2
                        tp = kb.emit("tensor", lambda t, j_=j, b_=bk4: t.transpose(
                            pb[b_][0:8, 0:128], g_tok[:, j_, :], id128_sb[:]),
                            waits=[(DV, gmul[j]), bw(bk4)], inc=(PE, 1))
                        cv = kb.emit("vector", lambda v, j_=j, b_=bk4: v.tensor_copy(
                            gatesT[:, j_ * 128:(j_ + 1) * 128], pb[b_][0:8, 0:128]),
                            waits=[(PE, tp)], inc=(DV, 1))
                        bank_free[bk4] = (DV, cv)
                    gat_dv = kb.at(DV)
                    if debug >= 2:
                        dma(SC, lambda s: s.dma_start(gat_h, gatesT[:]),
                            waits=[(DV, gat_dv)])
                        dma(SC, lambda s: s.dma_start(sco_h, scoresT[:]),
                            waits=[(DV, gat_dv)])
                        dma(SC, lambda s: s.dma_start(
                            stok_h, s_tok[:].rearrange("p a b -> p (a b)")),
                            waits=[(DV, gat_dv)])
                        dma(SC, lambda s: s.dma_start(
                            gtok_h, g_tok[:].rearrange("p a b -> p (a b)")),
                            waits=[(DV, gat_dv)])
                routed = ph3.enter_context(nc.sbuf_tensor("routed", [128, DC, TOK], BF16))
                hbuf = ph3.enter_context(nc.sbuf_tensor("hbuf", [128, 8, TOK], BF16))

                routed_war = None
                h_war = None
                gb_dv = {}
                routed_dv = 0

                def emit_gb_routed(e):
                    nonlocal routed_dv
                    if e < 8:
                        for at in range(4):
                            bnk = 4 + at % 2
                            pp = kb.emit("tensor", lambda t, e_=e, at_=at, b_=bnk: t.matmul(
                                pb[b_][:], sel_sb[:, e_ * 128:(e_ + 1) * 128],
                                gatesT[:, at_ * 512:(at_ + 1) * 512],
                                start=True, stop=True),
                                waits=[(DV, gat_dv), bw(bnk)], inc=(PE, 1))
                            gb_dv[at] = kb.emit("vector", lambda v, at_=at, b_=bnk:
                                v.tensor_copy(gbt[:, at_, :], pb[b_][:]),
                                waits=[(PE, pp)], inc=(DV, 1))
                            bank_free[bnk] = (DV, gb_dv[at])
                    for dc in range(DC):
                        i = dc % 2
                        ld = dma(SD, lambda s, dc_=dc, i_=i: s.dma_start(
                            y1s[i_][:], y1b_h[dc_ * 128:(dc_ + 1) * 128, :]),
                            waits=(y1s_free[i] or []))
                        if e < 8:
                            lastv = None
                            for at in range(4):
                                lastv = kb.emit("vector",
                                    lambda v, dc_=dc, at_=at, i_=i: v.tensor_mul(
                                        routed[:, dc_, at_ * 512:(at_ + 1) * 512],
                                        y1s[i_][:, at_ * 512:(at_ + 1) * 512],
                                        gbt[:, at_, :]),
                                    waits=[(SD, ld), (DV, gb_dv[at]), routed_war],
                                    inc=(DV, 1))
                        else:
                            lastv = kb.emit("vector", lambda v, dc_=dc, i_=i: v.tensor_copy(
                                routed[:, dc_, :], y1s[i_][:]),
                                waits=[(SD, ld), routed_war], inc=(DV, 1))
                        y1s_free[i] = [(DV, lastv)]
                    routed_dv = kb.at(DV)

                emit_gb_routed(0)
                for e in range(9):
                    for half in range(2):
                        for fcl in range(8):
                            fc = half * 8 + fcl
                            i = fc % 2
                            ldw = dma(GD, lambda g, e_=e, fc_=fc, i_=i: g.dma_start(
                                wib[i_][:], win_r[e_][:, :, fc_ * 128:(fc_ + 1) * 128]),
                                waits=[wib_free[i]])
                            lds = dma(GD, lambda g, e_=e, fc_=fc, i_=i: g.dma_start(
                                wsb2[i_][:], wsw_r[e_][:, :, fc_ * 128:(fc_ + 1) * 128]),
                                waits=[wsb_free[i]])
                            for at in range(4):
                                lastm = lasts = None
                                for dc in range(DC):
                                    ws = []
                                    if dc == 0:
                                        ws = [(GD, lds), (DV, routed_dv), bw(at)]
                                    lastm = kb.emit("tensor",
                                        lambda t, at_=at, dc_=dc, i_=i: t.matmul(
                                            pb[at_][:], wib[i_][:, dc_, :],
                                            routed[:, dc_, at_ * 512:(at_ + 1) * 512],
                                            start=(dc_ == 0), stop=(dc_ == 7)),
                                        waits=ws, inc=(PE, 1) if dc == 7 else None)
                                    ws2 = [bw(4 + at)] if dc == 0 else []
                                    lasts = kb.emit("tensor",
                                        lambda t, at_=at, dc_=dc, i_=i: t.matmul(
                                            pb[4 + at_][:], wsb2[i_][:, dc_, :],
                                            routed[:, dc_, at_ * 512:(at_ + 1) * 512],
                                            start=(dc_ == 0), stop=(dc_ == 7)),
                                        waits=ws2, inc=(PE, 1) if dc == 7 else None)
                                j = (fc * 4 + at) % 2
                                sl = kb.emit("scalar", lambda sc, at_=at, j_=j: sc.activation(
                                    hm[j_][:], pb[at_][:], AF.Silu),
                                    waits=[(PE, lastm), hm_free[j]], inc=(AC, 1))
                                bank_free[at] = (AC, sl)
                                hmv = kb.emit("vector",
                                    lambda v, fcl_=fcl, at_=at, j_=j: v.tensor_mul(
                                        hbuf[:, fcl_, at_ * 512:(at_ + 1) * 512],
                                        hm[j_][:], pb[4 + at_][:]),
                                    waits=[(AC, sl), (PE, lasts), h_war], inc=(DV, 1))
                                bank_free[4 + at] = (DV, hmv)
                                hm_free[j] = (DV, hmv)
                            wib_free[i] = (PE, kb.at(PE))
                            wsb_free[i] = (PE, kb.at(PE))
                        h_dv = kb.at(DV)
                        if half == 1:
                            routed_war = (PE, kb.at(PE))
                            if e < 8:
                                emit_gb_routed(e + 1)
                        for dc in range(DC):
                            i = dc % 2
                            ld1 = dma(GD, lambda g, e_=e, h_=half, dc_=dc, i_=i: g.dma_start(
                                wob[i_][:], wout_r[e_][:, h_ * 8:(h_ + 1) * 8,
                                                       dc_ * 128:(dc_ + 1) * 128]),
                                waits=[wob_free[i]])
                            for at in range(4):
                                bnk = (dc % 2) * 4 + at
                                last = None
                                for fcl in range(8):
                                    ws = []
                                    if fcl == 0:
                                        ws = [(GD, ld1), (DV, h_dv), bw(bnk)]
                                    last = kb.emit("tensor",
                                        lambda t, at_=at, fc_=fcl, i_=i, b_=bnk:
                                        t.matmul(
                                            pb[b_][:], wob[i_][:, fc_, :],
                                            hbuf[:, fc_, at_ * 512:(at_ + 1) * 512],
                                            start=(fc_ == 0), stop=(fc_ == 7)),
                                        waits=ws, inc=(PE, 1) if fcl == 7 else None)
                                if e == 0 and half == 0:
                                    ev = kb.emit("vector",
                                        lambda v, dc_=dc, at_=at, b_=bnk: v.tensor_copy(
                                            acc[:, dc_, at_ * 512:(at_ + 1) * 512], pb[b_][:]),
                                        waits=[(PE, last)], inc=(DV, 1))
                                else:
                                    ev = kb.emit("vector",
                                        lambda v, dc_=dc, at_=at, b_=bnk: v.tensor_add(
                                            acc[:, dc_, at_ * 512:(at_ + 1) * 512],
                                            acc[:, dc_, at_ * 512:(at_ + 1) * 512], pb[b_][:]),
                                        waits=[(PE, last)], inc=(DV, 1))
                                bank_free[bnk] = (DV, ev)
                            wob_free[i] = (PE, kb.at(PE))
                        h_war = (PE, kb.at(PE))
                if debug >= 2:
                    acc_dv = kb.at(DV)
                    for dc in range(DC):
                        dma(SD, lambda s, dc_=dc: s.dma_start(
                            acc_h.rearrange("(c p) t -> p c t", p=128)[:, dc_, :],
                            acc[:, dc_, :]), waits=[(DV, acc_dv)])
                kb.barrier()

            ph4 = ExitStack()
            with ph4:
                y1fb = [ph4.enter_context(nc.sbuf_tensor(f"y1fb{i}", [128, DC, 512], F32))
                        for i in range(2)]
                x2 = [ph4.enter_context(nc.sbuf_tensor(f"x2_{i}", [128, DC, 512], F32))
                      for i in range(2)]
                ub4 = [ph4.enter_context(nc.sbuf_tensor(f"ub4_{i}", [128, DC, 512], BF16))
                       for i in range(2)]
                zst = [ph4.enter_context(nc.sbuf_tensor(f"zst{i}", [128, 512], F32))
                       for i in range(2)]
                rows2 = tuple(ph4.enter_context(nc.sbuf_tensor(f"r2_{i}", [1, 512], F32))
                              for i in range(4))
                bc2 = ph4.enter_context(nc.sbuf_tensor("bc2", [128, 512], F32))
                bc3 = ph4.enter_context(nc.sbuf_tensor("bc3", [128, 512], F32))
                y1fb_free = [None, None]
                zst_free = [None, None]
                x2_war = [None, None]
                ctx4 = {}
                x2dv4 = {}

                rowsets = [(rA, rB, rC, rD), rows2]
                banksets = [(5, 6), (4, 7)]
                bcsets = [(bc0, bc1), (bc2, bc3)]

                def emit_x2(at):
                    i = at % 2
                    ld = dma(SD, lambda s, at_=at, i_=i: s.dma_start(
                        y1fb[i_][:], y1f_r[:, :, at_ * 512:(at_ + 1) * 512]),
                        waits=(y1fb_free[i] or []))
                    last = None
                    for dc in range(DC):
                        last = kb.emit("vector", lambda v, dc_=dc, at_=at, i_=i: v.tensor_add(
                            x2[i_][:, dc_, :], y1fb[i_][:, dc_, :],
                            acc[:, dc_, at_ * 512:(at_ + 1) * 512]),
                            waits=[(SD, ld)] + ([x2_war[i]] if dc == 0 else []),
                            inc=(DV, 1))
                    x2dv4[at] = last
                    y1fb_free[i] = [(DV, last)]

                def emit_lna4(at):
                    i = at % 2
                    ctx4[at] = emit_ln_a(lambda dc: x2[i][:, dc, :], ub4[i],
                                         [(DV, x2dv4[at])], rowsets[i][0:2], banksets[i])

                def emit_lnb4(at):
                    i = at % 2

                    def z_out(dc, b1, b2, at_=at, i_=i):
                        j = dc % 2
                        bcA, bcB = bcsets[i_]
                        w1 = kb.emit("vector", lambda v, dc_=dc, j_=j: v.tensor_mul(
                            zst[j_][:], x2[i_][:, dc_, :], bcA[:]),
                            waits=[(DV, max(b1, b2)), zst_free[j]], inc=(DV, 1))
                        w2 = kb.emit("vector", lambda v, j_=j: v.tensor_sub(
                            zst[j_][:], zst[j_][:], bcB[:]),
                            waits=[(DV, w1)], inc=(DV, 1))
                        d1 = dma(CD, lambda s, dc_=dc, j_=j: s.dma_start(
                            zT_r[:, dc_, at_ * 512:(at_ + 1) * 512], zst[j_][:]),
                            waits=[(DV, w2)])
                        zst_free[j] = (CD, d1)
                        if dc == DC - 1:
                            x2_war[i_] = (DV, kb.at(DV))

                    emit_ln_b(ctx4[at], z_out, rowsets[i], banksets[i], bcsets[i])

                emit_x2(0)
                emit_lna4(0)
                emit_x2(1)
                emit_lna4(1)
                emit_lnb4(0)
                emit_x2(2)
                emit_lna4(2)
                emit_lnb4(1)
                emit_x2(3)
                emit_lna4(3)
                emit_lnb4(2)
                emit_lnb4(3)
                kb.barrier()
                kb.barrier()
        kb.build()
    return nc


_CACHE = {}


def _consts():
    sel = np.zeros((8, E * 128), np.float32)
    for e in range(E):
        sel[e, e * 128:(e + 1) * 128] = 1.0
    return {
        "ones128": np.ones((128, 1), BF),
        "ones1f": np.ones((1, 128), np.float32),
        "id128f": np.eye(128, dtype=np.float32),
        "id8f": np.eye(8, dtype=np.float32),
        "sel8": sel,
    }


def kernel(L0_attn_x, L0_attn_Wq, L0_attn_bq, L0_attn_Wk, L0_attn_bk,
           L0_ffn_router_DE, L0_ffn_w_in_shared_FD, L0_ffn_w_out_shared_DF,
           L0_ffn_w_swiglu_FD, L0_ffn_w_in_eDF, L0_ffn_w_swiglu_eDF,
           L0_ffn_w_out_eFD):
    x = np.ascontiguousarray(np.asarray(L0_attn_x, dtype=np.float32))
    B = x.shape[0]
    common = {
        "wqT": np.ascontiguousarray(np.asarray(L0_attn_Wq).T),
        "wk2": np.ascontiguousarray(np.asarray(L0_attn_Wk)),
        "bq": np.ascontiguousarray(np.asarray(L0_attn_bq).reshape(EC, 128).T),
        "router": np.ascontiguousarray(np.asarray(L0_ffn_router_DE)),
        "wish": np.ascontiguousarray(np.asarray(L0_ffn_w_in_shared_FD).T),
        "wssh": np.ascontiguousarray(np.asarray(L0_ffn_w_swiglu_FD).T),
        "wosh": np.ascontiguousarray(np.asarray(L0_ffn_w_out_shared_DF).T),
        "win": np.ascontiguousarray(np.asarray(L0_ffn_w_in_eDF)),
        "wsw": np.ascontiguousarray(np.asarray(L0_ffn_w_swiglu_eDF)),
        "wout": np.ascontiguousarray(np.asarray(L0_ffn_w_out_eFD)),
    }
    common.update(_consts())
    xTb = [np.ascontiguousarray(x[b].T) for b in range(B)]
    in_maps = []
    for c in range(8):
        b = c // 4
        o = (c % 4) * TOK
        m = dict(common)
        m["xT"] = xTb[b]
        m["xN"] = x[b]
        m["xTo"] = np.ascontiguousarray(xTb[b][:, o:o + TOK])
        in_maps.append(m)
    if "nc" not in _CACHE:
        _CACHE["nc"] = build_program(debug=False)
    nc = _CACHE["nc"]
    res = run_bass_kernel_spmd(nc, in_maps, core_ids=list(range(8)))
    out = np.empty((B, S, D), np.float32)
    for c in range(8):
        b = c // 4
        o = (c % 4) * TOK
        out[b, o:o + TOK, :] = res.results[c]["zT"].T
    return out


# version 27
# speedup vs baseline: 1.2369x; 1.0052x over previous
import numpy as np
import ml_dtypes
from contextlib import ExitStack
import concourse.bass as bass
import concourse.mybir as mybir
from concourse.bass_utils import run_bass_kernel_spmd

F32 = mybir.dt.float32
BF16 = mybir.dt.bfloat16
AF = mybir.ActivationFunctionType
ALU = mybir.AluOpType
AX = mybir.AxisListType
f32 = np.float32
BF = ml_dtypes.bfloat16

D = 1024
S = 8192
TOK = 2048
E = 8
FF = 2048
DC = 8
EC = 8
FC = 16
QB = 4
NG = 8
CEXP = 9.0
PAIR_O = False

ENGINES = ["sync", "tensor", "vector", "scalar", "gpsimd"]
SD, GD, SC, CD = "sd", "gd", "sc", "cd"
NPOOL = 10
PE, DV, AC = "pe", "dve", "act"


class KB:

    def __init__(self, nc):
        self.nc = nc
        self.ops = {e: [] for e in ENGINES}
        self.cnt = {}
        self.semh = {}

    def at(self, sem):
        return self.cnt.get(sem, 0)

    def emit(self, eng, fn, waits=(), inc=None):
        def norm(x):
            if isinstance(x[1], tuple):
                return x[1]
            return x

        w = []
        for x in waits:
            if x is None:
                continue
            if isinstance(x, list):
                w.extend(norm(y) for y in x if y is not None and norm(y)[1] > 0)
            elif norm(x)[1] > 0:
                w.append(norm(x))
        v = None
        if inc is not None:
            self.cnt[inc[0]] = self.cnt.get(inc[0], 0) + inc[1]
            v = self.cnt[inc[0]]
        self.ops[eng].append((fn, w, inc))
        return v

    def barrier(self):
        snap = [(s, v) for s, v in self.cnt.items() if v > 0]
        for eng in ENGINES:
            self.ops[eng].append((None, list(snap), None))

    def build(self):
        for name in list(self.cnt):
            if name not in self.semh:
                self.semh[name] = self.nc.alloc_semaphore(name)

        def replayer(eng_name):
            def f(eng):
                last = {}
                for fn, waits, inc in self.ops[eng_name]:
                    for s_, v_ in waits:
                        if last.get(s_, 0) >= v_:
                            continue
                        eng.wait_ge(self.semh[s_], v_)
                        last[s_] = v_
                    if fn is None:
                        continue
                    inst = fn(eng)
                    if inc is not None:
                        inst.then_inc(self.semh[inc[0]], inc[1])
            return f

        with self.nc.Block() as blk:
            blk.sync(replayer("sync"))
            blk.tensor(replayer("tensor"))
            blk.vector(replayer("vector"))
            blk.scalar(replayer("scalar"))
            blk.gpsimd(replayer("gpsimd"))


def build_program(debug=False):
    nc = bass.Bass("TRN2", target_bir_lowering=False, debug=False)
    kb = KB(nc)

    xT = nc.dram_tensor("xT", [D, S], F32, kind="ExternalInput").ap()
    xTo = nc.dram_tensor("xTo", [D, TOK], F32, kind="ExternalInput").ap()
    xN = nc.dram_tensor("xN", [S, D], F32, kind="ExternalInput").ap()
    wqT = nc.dram_tensor("wqT", [D, D], F32, kind="ExternalInput").ap()
    wk2 = nc.dram_tensor("wk2", [D, D], F32, kind="ExternalInput").ap()
    bq = nc.dram_tensor("bq", [128, EC], F32, kind="ExternalInput").ap()
    router = nc.dram_tensor("router", [D, E], F32, kind="ExternalInput").ap()
    wish = nc.dram_tensor("wish", [D, FF], F32, kind="ExternalInput").ap()
    wssh = nc.dram_tensor("wssh", [D, FF], F32, kind="ExternalInput").ap()
    wosh = nc.dram_tensor("wosh", [FF, D], F32, kind="ExternalInput").ap()
    win = nc.dram_tensor("win", [E, D, FF], F32, kind="ExternalInput").ap()
    wsw = nc.dram_tensor("wsw", [E, D, FF], F32, kind="ExternalInput").ap()
    wout = nc.dram_tensor("wout", [E, FF, D], F32, kind="ExternalInput").ap()
    ones128 = nc.dram_tensor("ones128", [128, 1], BF16, kind="ExternalInput").ap()
    ones1f = nc.dram_tensor("ones1f", [1, 128], F32, kind="ExternalInput").ap()
    id128f = nc.dram_tensor("id128f", [128, 128], F32, kind="ExternalInput").ap()
    id8f = nc.dram_tensor("id8f", [8, 8], F32, kind="ExternalInput").ap()
    sel8 = nc.dram_tensor("sel8", [8, E * 128], F32, kind="ExternalInput").ap()

    zT = nc.dram_tensor("zT", [D, TOK], F32, kind="ExternalOutput").ap()
    xbT_h = nc.dram_tensor("xbT_h", [D, S], BF16, kind="Internal").ap()
    y1b_h = nc.dram_tensor("y1b_h", [D, TOK], BF16, kind="Internal").ap()
    y1f_h = nc.dram_tensor("y1f_h", [D, TOK], F32,
                           kind="ExternalOutput" if debug else "Internal").ap()
    if debug >= 2:
        gat_h = nc.dram_tensor("gat_h", [8, TOK], F32, kind="ExternalOutput").ap()
        acc_h = nc.dram_tensor("acc_h", [D, TOK], F32, kind="ExternalOutput").ap()
        sco_h = nc.dram_tensor("sco_h", [8, TOK], F32, kind="ExternalOutput").ap()
        stok_h = nc.dram_tensor("stok_h", [128, 128], F32, kind="ExternalOutput").ap()
        gtok_h = nc.dram_tensor("gtok_h", [128, 128], F32, kind="ExternalOutput").ap()

    xT_r = xT.rearrange("(c p) t -> p c t", p=128)
    xN_r = xN.rearrange("(c p) d -> p c d", p=128)
    xTo_r = xTo.rearrange("(c p) t -> p c t", p=128)
    wqT_r = wqT.rearrange("(c p) e -> p c e", p=128)
    wk2_r = wk2.rearrange("(c p) d -> p c d", p=128)
    xbT_r = xbT_h.rearrange("(c p) t -> p c t", p=128)
    y1f_r = y1f_h.rearrange("(c p) t -> p c t", p=128)
    zT_r = zT.rearrange("(c p) t -> p c t", p=128)
    rtr_r = router.rearrange("(c p) e -> p c e", p=128)
    win_r = [win[e].rearrange("(c p) f -> p c f", p=128) for e in range(E)] + \
            [wish.rearrange("(c p) f -> p c f", p=128)]
    wsw_r = [wsw[e].rearrange("(c p) f -> p c f", p=128) for e in range(E)] + \
            [wssh.rearrange("(c p) f -> p c f", p=128)]
    wout_r = [wout[e].rearrange("(c p) d -> p c d", p=128) for e in range(E)] + \
             [wosh.rearrange("(c p) d -> p c d", p=128)]

    es = ExitStack()
    with es:
        sb = lambda name, shape, dt: es.enter_context(nc.sbuf_tensor(name, shape, dt))
        ones_sb = sb("ones_sb", [128, 1], BF16)
        ones1f_sb = sb("ones1f_sb", [1, 128], F32)
        bq_sb = sb("bq_sb", [128, EC], F32)
        cs_sb = sb("cs_sb", [128, DC], F32)
        cs_tmp = sb("cs_tmp", [128, DC], F32)
        tmp1_sb = sb("tmp1_sb", [128, 1], F32)
        rA = sb("rA", [1, 512], F32)
        rB = sb("rB", [1, 512], F32)
        rC = sb("rC", [1, 512], F32)
        rD = sb("rD", [1, 512], F32)
        bc0 = sb("bc0", [128, 512], F32)
        bc1 = sb("bc1", [128, 512], F32)
        rt_sb = sb("rt_sb", [128, DC, 8], BF16)
        scoresT = sb("scoresT", [8, TOK], F32)
        gatesT = scoresT
        s_tok = sb("s_tok", [128, 16, 8], F32)
        g_tok = sb("g_tok", [128, 16, 8], F32)
        m1_sb = [sb(f"m1_sb{i}", [128, 1], F32) for i in range(2)]
        m2_sb = [sb(f"m2_sb{i}", [128, 1], F32) for i in range(2)]
        msk_sb = [sb(f"msk_sb{i}", [128, 8], F32) for i in range(2)]
        sk2_sb = [sb(f"sk2_sb{i}", [128, 8], F32) for i in range(2)]
        sig_sb = [sb(f"sig_sb{i}", [128, 8], F32) for i in range(2)]
        id128_sb = sb("id128_sb", [128, 128], F32)
        id8_sb = sb("id8_sb", [8, 8], F32)

        pb = [es.enter_context(nc.psum_tensor(f"pb{i}", [128, 512], F32))
              for i in range(8)]
        bank_free = {}
        state = {"ln_war": None}

        dma_idx = {SD: 0, GD: 0, CD: 0}

        def dma(qsem, fn, waits=()):
            eng = {GD: "gpsimd", CD: "scalar"}.get(qsem, "sync")
            if qsem == SC:
                return kb.emit(eng, fn, waits=waits, inc=(SC, 16))
            phys = f"{qsem}{dma_idx[qsem] % NPOOL}"
            dma_idx[qsem] += 1
            v = kb.emit(eng, fn, waits=waits, inc=(phys, 16))
            return (phys, v)

        def bw(i):
            return bank_free.get(i)

        ln_war = {}

        def emit_ln_a(src, ub, ready, rows, banks):
            rS, rQ = rows
            bA, bB = banks
            war = ln_war.get(ub.name)
            cps = []
            for dc in range(DC):
                cps.append(kb.emit("vector", lambda v, dc_=dc: v.tensor_copy(
                    ub[:, dc_, :], src(dc_)),
                    waits=list(ready) + ([war] if dc == 0 else []), inc=(DV, 1)))
            b = bw(bA)
            last = None
            for dc in range(DC):
                last = kb.emit("tensor", lambda t, dc_=dc, b_=bA: t.matmul(
                    pb[b_][0:1, :], ones_sb[:], ub[:, dc_, :],
                    start=(dc_ == 0), stop=(dc_ == DC - 1)),
                    waits=[(DV, cps[dc]), b if dc == 0 else None],
                    inc=(PE, 1) if dc == DC - 1 else None)
            s1 = kb.emit("vector", lambda v, b_=bA: v.tensor_copy(rS[:], pb[b_][0:1, :]),
                         waits=[(PE, last)], inc=(DV, 1))
            bank_free[bA] = (DV, s1)
            sqs = []
            for dc in range(DC):
                sqs.append(kb.emit("vector", lambda v, dc_=dc: v.tensor_mul(
                    ub[:, dc_, :], ub[:, dc_, :], ub[:, dc_, :]),
                    waits=[(PE, last)], inc=(DV, 1)))
            b = bw(bB)
            last2 = None
            for dc in range(DC):
                last2 = kb.emit("tensor", lambda t, dc_=dc, b_=bB: t.matmul(
                    pb[b_][0:1, :], ones_sb[:], ub[:, dc_, :],
                    start=(dc_ == 0), stop=(dc_ == DC - 1)),
                    waits=[(DV, sqs[dc]), b if dc == 0 else None],
                    inc=(PE, 1) if dc == DC - 1 else None)
            ln_war[ub.name] = (PE, last2)
            s2 = kb.emit("vector", lambda v, b_=bB: v.tensor_copy(rQ[:], pb[b_][0:1, :]),
                         waits=[(PE, last2)], inc=(DV, 1))
            bank_free[bB] = (DV, s2)
            return dict(s1=s1, s2=s2)

        def emit_ln_b(ctx, out_cb, rows, banks, bcs):
            rS, rQ, rT, rU = rows
            bA, bB = banks
            bcA, bcB = bcs
            kb.emit("vector", lambda v: v.tensor_scalar(
                rS[:], rS[:], 1.0 / D, None, op0=ALU.mult),
                waits=[(DV, ctx["s1"])], inc=(DV, 1))
            kb.emit("vector", lambda v: v.tensor_scalar(
                rQ[:], rQ[:], 1.0 / D, None, op0=ALU.mult),
                waits=[(DV, ctx["s2"])], inc=(DV, 1))
            kb.emit("vector", lambda v: v.tensor_mul(rT[:], rS[:], rS[:]),
                    waits=[(DV, kb.at(DV))], inc=(DV, 1))
            vv = kb.emit("vector", lambda v: v.tensor_sub(rQ[:], rQ[:], rT[:]),
                         waits=[(DV, kb.at(DV))], inc=(DV, 1))
            a1 = kb.emit("scalar", lambda sc: sc.activation(
                rT[:], rQ[:], AF.Ln, bias=f32(1e-5)),
                waits=[(DV, vv)], inc=(AC, 1))
            a2 = kb.emit("scalar", lambda sc: sc.activation(
                rU[:], rT[:], AF.Exp, scale=f32(-0.5)),
                waits=[(AC, a1)], inc=(AC, 1))
            mr = kb.emit("vector", lambda v: v.tensor_mul(rT[:], rS[:], rU[:]),
                         waits=[(AC, a2)], inc=(DV, 1))
            b = bw(bA)
            p1 = kb.emit("tensor", lambda t, b_=bA: t.matmul(
                pb[b_][:], ones1f_sb[:], rU[:], start=True, stop=True),
                waits=[(AC, a2), b], inc=(PE, 1))
            b1 = kb.emit("vector", lambda v, b_=bA: v.tensor_copy(bcA[:], pb[b_][:]),
                         waits=[(PE, p1)], inc=(DV, 1))
            bank_free[bA] = (DV, b1)
            b = bw(bB)
            p2 = kb.emit("tensor", lambda t, b_=bB: t.matmul(
                pb[b_][:], ones1f_sb[:], rT[:], start=True, stop=True),
                waits=[(DV, mr), b], inc=(PE, 1))
            b2 = kb.emit("vector", lambda v, b_=bB: v.tensor_copy(bcB[:], pb[b_][:]),
                         waits=[(PE, p2)], inc=(DV, 1))
            bank_free[bB] = (DV, b2)
            for dc in range(DC):
                out_cb(dc, b1, b2)

        dma(SC, lambda s: s.dma_start(ones_sb[:], ones128))
        dma(SC, lambda s: s.dma_start(ones1f_sb[:], ones1f))
        dma(SC, lambda s: s.dma_start(bq_sb[:], bq))
        c0 = kb.at(SC)
        dma(SC, lambda s: s.dma_start(id128_sb[:], id128f))
        dma(SC, lambda s: s.dma_start(id8_sb[:], id8f))
        cgt = kb.at(SC)
        rt_tok = dma(GD, lambda g: g.dma_start(rt_sb[:], rtr_r))

        ph12 = ExitStack()
        with ph12:
            q2T = ph12.enter_context(nc.sbuf_tensor("q2T", [128, DC, TOK], BF16))

            ph1 = ExitStack()
            with ph1:
                wq_sb = ph1.enter_context(nc.sbuf_tensor("wq_sb", [128, DC, D], BF16))
                wk2_sb = ph1.enter_context(nc.sbuf_tensor("wk2_sb", [128, DC, D], BF16))
                qT = ph1.enter_context(nc.sbuf_tensor("qT", [128, EC, TOK], BF16))
                xq = [ph1.enter_context(nc.sbuf_tensor(f"xq{i}", [128, DC, 512], BF16))
                      for i in range(4)]
                xsf = [ph1.enter_context(nc.sbuf_tensor(f"xsf{i}", [128, DC, 512], F32))
                       for i in range(2)]
                xs = [ph1.enter_context(nc.sbuf_tensor(f"xs{i}", [128, DC, 512], BF16))
                      for i in range(2)]

                wready = dma(GD, lambda g: g.dma_start(wq_sb[:], wqT_r))
                w2ready = dma(GD, lambda g: g.dma_start(wk2_sb[:], wk2_r))

                lds = {}
                qp_dv = {}
                for t in range(QB):
                    lds[t] = dma(GD, lambda g, t_=t: g.dma_start(
                        xq[t_][:], xTo_r[:, :, t_ * 512:(t_ + 1) * 512]), waits=[])

                def emit_qp(t):
                    lpe = None
                    for ec in range(EC):
                        bi = ec % 2
                        last_mm = None
                        for dc in range(DC):
                            ws = []
                            if dc == 0:
                                ws = [(GD, lds[t]), (GD, wready), bw(bi)]
                            last_mm = kb.emit(
                                "tensor",
                                lambda tn, ec_=ec, dc_=dc, t_=t, b_=bi: tn.matmul(
                                    pb[b_][:], wq_sb[:, dc_, ec_ * 128:(ec_ + 1) * 128],
                                    xq[t_][:, dc_, :], start=(dc_ == 0), stop=(dc_ == 7)),
                                waits=ws, inc=((PE, 1) if dc == 7 else None))
                        lpe = last_mm
                        ev = kb.emit(
                            "vector",
                            lambda v, ec_=ec, b_=bi, t_=t: v.tensor_scalar(
                                qT[:, ec_, t_ * 512:(t_ + 1) * 512], pb[b_][:],
                                bq_sb[:, ec_:ec_ + 1], None, op0=ALU.add),
                            waits=[(PE, last_mm), (SC, c0)], inc=(DV, 1))
                        bank_free[bi] = (DV, ev)
                    qp_dv[t] = kb.at(DV)

                def emit_q2(t):
                    for dc in range(DC):
                        b2 = 2 + dc % 2
                        last_mm = None
                        for ec in range(EC):
                            ws = []
                            if ec == 0:
                                ws = [(DV, qp_dv[t]), (GD, w2ready), bw(b2)]
                            last_mm = kb.emit(
                                "tensor",
                                lambda tn, ec_=ec, dc_=dc, b_=b2, t_=t: tn.matmul(
                                    pb[b_][:], wk2_sb[:, ec_, dc_ * 128:(dc_ + 1) * 128],
                                    qT[:, ec_, t_ * 512:(t_ + 1) * 512],
                                    start=(ec_ == 0), stop=(ec_ == 7)),
                                waits=ws, inc=((PE, 1) if ec == 7 else None))
                        ev = kb.emit(
                            "vector",
                            lambda v, dc_=dc, b_=b2, t_=t: v.tensor_copy(
                                q2T[:, dc_, t_ * 512:(t_ + 1) * 512], pb[b_][:]),
                            waits=[(PE, last_mm)], inc=(DV, 1))
                        bank_free[b2] = (DV, ev)

                emit_qp(0)
                emit_qp(1)
                emit_q2(0)
                emit_qp(2)
                emit_q2(1)
                emit_qp(3)
                emit_q2(2)
                emit_q2(3)

                xsf_free = [None, None]
                xs_free = [None, None]
                for sl in range(S // 512):
                    i = sl % 2
                    ld = dma(SD, lambda s, sl_=sl, i_=i: s.dma_start(
                        xsf[i_][:], xT_r[:, :, sl_ * 512:(sl_ + 1) * 512]),
                        waits=[xsf_free[i]])
                    casts = []
                    for dc in range(DC):
                        casts.append(kb.emit(
                            "scalar", lambda sc, dc_=dc, i_=i: sc.activation(
                                xs[i_][:, dc_, :], xsf[i_][:, dc_, :], AF.Copy),
                            waits=([(SD, ld), xs_free[i]] if dc == 0 else []),
                            inc=(AC, 1)))
                    st = dma(CD, lambda s, sl_=sl, i_=i: s.dma_start(
                        xbT_r[:, :, sl_ * 512:(sl_ + 1) * 512], xs[i_][:]),
                        waits=[(AC, casts[-1])])
                    rd = kb.emit("vector", lambda v, i_=i: v.reduce_sum(
                        cs_tmp[:], xsf[i_][:], axis=AX.X),
                        waits=[(SD, ld)] + ([(DV, ad)] if sl > 0 else []),
                        inc=(DV, 1))
                    ad = kb.emit("vector", lambda v, first=(sl == 0):
                        (v.tensor_copy(cs_sb[:], cs_tmp[:]) if first else
                         v.tensor_add(cs_sb[:], cs_sb[:], cs_tmp[:])),
                        waits=[(DV, rd)], inc=(DV, 1))
                    xsf_free[i] = [(AC, casts[-1]), (DV, rd)]
                    xs_free[i] = [(CD, st)]
                kb.emit("vector", lambda v: v.tensor_scalar(
                    cs_sb[:], cs_sb[:], 2.0 / S, None, op0=ALU.mult),
                    waits=[(DV, kb.at(DV))], inc=(DV, 1))
                kb.barrier()

            ph2 = ExitStack()
            with ph2:
                kg = [ph2.enter_context(nc.sbuf_tensor(f"kg{i}", [128, EC, 1024], BF16))
                      for i in range(2)]
                vg = [ph2.enter_context(nc.sbuf_tensor(f"vg{i}", [128, 8, 1024], BF16))
                      for i in range(2)]
                Pt = [ph2.enter_context(nc.sbuf_tensor(f"Pt{i}", [128, 8, 512], BF16))
                      for i in range(2)]
                Obuf = [ph2.enter_context(nc.sbuf_tensor(f"O{i}", [128, DC, 512], F32))
                        for i in range(2)]
                xow = ph2.enter_context(nc.sbuf_tensor("xow", [128, DC, 512], F32))
                ub2 = ph2.enter_context(nc.sbuf_tensor("ub2", [128, DC, 512], BF16))
                y1f_st = [ph2.enter_context(nc.sbuf_tensor(f"y1f{i}", [128, 512], F32))
                          for i in range(2)]
                y1b_st = [ph2.enter_context(nc.sbuf_tensor(f"y1b{i}", [128, 512], BF16))
                          for i in range(8)]

                kg_free = [None, None]
                vg_free = [None, None]
                Pt_free = [None, None]
                y1f_free = [None, None]
                y1b_free = [None] * 8
                y1cb_dv = {}
                rs_sev = {}
                exp_done = {}
                vg_ld = {}
                kg_ld = {}
                xow_ld = {}
                qb_lpe = {}
                epi_udv = {}
                epi_ctx = {}
                xc_dv = {}

                def emit_xc(qb):
                    last = None
                    for dc in range(DC):
                        last = kb.emit("vector", lambda v, dc_=dc: v.tensor_scalar(
                            xow[:, dc_, :], xow[:, dc_, :], 2.0, cs_sb[:, dc_:dc_ + 1],
                            op0=ALU.mult, op1=ALU.subtract),
                            waits=[(SD, xow_ld[qb]), (DV, kb.at(DV))], inc=(DV, 1))
                    xc_dv[qb] = last

                def emit_slot(qb, g):
                    O = Obuf[qb % 2]
                    lb4 = 4 if qb % 2 == 0 else 7
                    if g < NG:
                        i = g % 2
                        kg_ld[(qb, g)] = dma(SD, lambda s, g_=g, i_=i: s.dma_start(
                            kg[i_][:], xbT_r[:, :, g_ * 1024:(g_ + 1) * 1024]),
                            waits=[kg_free[i]])
                        vg_ld[(qb, g)] = dma(GD, lambda gp, g_=g, i_=i:
                            gp.dma_start(vg[i_][:],
                                         xN_r[:, g_ * 8:(g_ + 1) * 8, :]),
                            waits=[vg_free[i]])
                        for tt in range(8):
                            lb = tt % 2
                            last = None
                            for ec in range(EC):
                                ws = []
                                if ec == 0:
                                    ws = [(SD, kg_ld[(qb, g)]), bw(lb)]
                                last = kb.emit(
                                    "tensor",
                                    lambda t, ec_=ec, tt_=tt, i_=i, lb_=lb, qb_=qb:
                                    t.matmul(
                                        pb[lb_][:],
                                        kg[i_][:, ec_, tt_ * 128:(tt_ + 1) * 128],
                                        q2T[:, ec_, qb_ * 512:(qb_ + 1) * 512],
                                        start=(ec_ == 0), stop=(ec_ == 7)),
                                    waits=ws, inc=((PE, 1) if ec == 7 else None))
                            ws = [(PE, last)]
                            if tt == 0:
                                ws.append(Pt_free[i])
                            ex = kb.emit(
                                "scalar",
                                lambda sc, tt_=tt, i_=i, lb_=lb: sc.activation(
                                    Pt[i_][:, tt_, :], pb[lb_][:], AF.Exp,
                                    bias=f32(-CEXP), scale=f32(1.0 / 32.0)),
                                waits=ws, inc=(AC, 1))
                            exp_done[(qb, g, tt)] = ex
                            bank_free[lb] = (AC, ex)
                    if g > 0:
                        gp_ = g - 1
                        i = gp_ % 2
                        for tt in range(8):
                            ws = [(AC, exp_done[(qb, gp_, tt)])]
                            if gp_ == 0 and tt == 0:
                                ws.append(bw(lb4))
                            kb.emit(
                                "tensor",
                                lambda t, tt_=tt, i_=i, lb_=lb4,
                                       st_=(gp_ == 0 and tt == 0),
                                       sp_=(gp_ == NG - 1 and tt == 7): t.matmul(
                                    pb[lb_][0:1, :], ones_sb[:], Pt[i_][:, tt_, :],
                                    start=st_, stop=sp_),
                                waits=ws, inc=((PE, 1) if tt == 7 else None))
                        if gp_ == NG - 1:
                            qb_lpe[qb] = kb.at(PE)
                        first_half = (gp_ % 2 == 0) and PAIR_O
                        for dc in range(DC):
                            ob = 2 + dc % 2
                            last = None
                            for tt in range(8):
                                ws = []
                                if tt == 0:
                                    ws = [(AC, exp_done[(qb, gp_, 7)]),
                                          (GD, vg_ld[(qb, gp_)])]
                                    if first_half or not PAIR_O:
                                        ws.append(bw(ob))
                                last = kb.emit(
                                    "tensor",
                                    lambda t, dc_=dc, tt_=tt, i_=i, ob_=ob,
                                           st_=(tt == 0 and (first_half or not PAIR_O)),
                                           sp_=((not first_half) and tt == 7): t.matmul(
                                        pb[ob_][:],
                                        vg[i_][:, tt_, dc_ * 128:(dc_ + 1) * 128],
                                        Pt[i_][:, tt_, :],
                                        start=st_, stop=sp_),
                                    waits=ws, inc=((PE, 1) if tt == 7 else None))
                            if not first_half:
                                pair = gp_ // 2 if PAIR_O else gp_
                                if pair == 0:
                                    ev = kb.emit("vector",
                                                 lambda v, dc_=dc, ob_=ob, O_=O: v.tensor_copy(
                                                     O_[:, dc_, :], pb[ob_][:]),
                                                 waits=[(PE, last)], inc=(DV, 1))
                                else:
                                    ev = kb.emit("vector",
                                                 lambda v, dc_=dc, ob_=ob, O_=O: v.tensor_add(
                                                     O_[:, dc_, :], O_[:, dc_, :], pb[ob_][:]),
                                                 waits=[(PE, last)], inc=(DV, 1))
                                bank_free[ob] = (DV, ev)
                        Pt_free[i] = (PE, kb.at(PE))
                        vg_free[i] = (PE, kb.at(PE))
                    if g < NG:
                        kg_free[g % 2] = (PE, kb.at(PE))

                def emit_epi_a(qb):
                    O = Obuf[qb % 2]
                    lb4 = 4 if qb % 2 == 0 else 7
                    c1 = kb.emit("vector", lambda v, lb_=lb4: v.tensor_copy(
                        rC[:], pb[lb_][0:1, :]),
                        waits=[(PE, qb_lpe[qb])], inc=(DV, 1))
                    bank_free[lb4] = (DV, c1)
                    p1 = kb.emit("tensor", lambda t: t.matmul(
                        pb[5][:], ones1f_sb[:], rC[:], start=True, stop=True),
                        waits=[(DV, c1), bw(5)], inc=(PE, 1))
                    c3 = kb.emit("vector", lambda v: v.reciprocal(bc0[:], pb[5][:]),
                                 waits=[(PE, p1)], inc=(DV, 1))
                    bank_free[5] = (DV, c3)
                    for dc in range(DC):
                        kb.emit("vector", lambda v, dc_=dc, O_=O: v.tensor_mul(
                            O_[:, dc_, :], O_[:, dc_, :], bc0[:]),
                            waits=[(DV, c3), (DV, xc_dv[qb])], inc=(DV, 1))
                        kb.emit("vector", lambda v, dc_=dc, O_=O: v.tensor_add(
                            O_[:, dc_, :], O_[:, dc_, :], xow[:, dc_, :]),
                            waits=[(DV, kb.at(DV))], inc=(DV, 1))
                    epi_udv[qb] = kb.at(DV)
                    if qb + 1 < QB:
                        xow_ld[qb + 1] = dma(CD, lambda s, qb_=qb + 1: s.dma_start(
                            xow[:], xTo_r[:, :, qb_ * 512:(qb_ + 1) * 512]),
                            waits=[(DV, epi_udv[qb])])

                def emit_epi_b1(qb):
                    O = Obuf[qb % 2]
                    epi_ctx[qb] = emit_ln_a(lambda dc: O[:, dc, :], ub2,
                                            [(DV, epi_udv[qb])], (rA, rB), (5, 6))

                def emit_epi_b2(qb):
                    O = Obuf[qb % 2]

                    def y1_out(dc, b1, b2, qb_=qb, O_=O):
                        i = dc % 2
                        w1 = kb.emit("vector", lambda v, dc_=dc, i_=i: v.tensor_mul(
                            y1f_st[i_][:], O_[:, dc_, :], bc0[:]),
                            waits=[(DV, max(b1, b2)), y1f_free[i]], inc=(DV, 1))
                        w2 = kb.emit("vector", lambda v, i_=i: v.tensor_sub(
                            y1f_st[i_][:], y1f_st[i_][:], bc1[:]),
                            waits=[(DV, w1)], inc=(DV, 1))
                        cb = kb.emit("vector", lambda v, dc_=dc, i_=i: v.tensor_copy(
                            y1b_st[dc_][:], y1f_st[i_][:]),
                            waits=[(DV, w2), y1b_free[dc]], inc=(DV, 1))
                        d1 = dma(CD, lambda s, dc_=dc, i_=i: s.dma_start(
                            y1f_r[:, dc_, qb_ * 512:(qb_ + 1) * 512], y1f_st[i_][:]),
                            waits=[(DV, w2)])
                        d2 = dma(CD, lambda s, dc_=dc: s.dma_start(
                            y1b_h[dc_ * 128:(dc_ + 1) * 128,
                                  qb_ * 512:(qb_ + 1) * 512], y1b_st[dc_][:]),
                            waits=[(DV, cb)])
                        y1f_free[i] = [(CD, d1), (DV, cb)]
                        y1b_free[dc] = [(CD, d2)]

                    emit_ln_b(epi_ctx[qb], y1_out, (rA, rB, rC, rD), (5, 6), (bc0, bc1))
                    y1cb_dv[qb] = kb.at(DV)

                def emit_scores(qb):
                    lb4 = 4 if qb % 2 == 0 else 7
                    last = None
                    for dc in range(DC):
                        ws = []
                        if dc == 0:
                            ws = [(DV, y1cb_dv[qb]), bw(lb4), (GD, rt_tok)]
                        last = kb.emit("tensor", lambda t, dc_=dc, b_=lb4: t.matmul(
                            pb[b_][0:8, :], rt_sb[:, dc_, :], y1b_st[dc_][:],
                            start=(dc_ == 0), stop=(dc_ == 7)),
                            waits=ws, inc=((PE, 1) if dc == 7 else None))
                    rs_pe = kb.at(PE)
                    sev = kb.emit("vector", lambda v, qb_=qb, b_=lb4: v.tensor_copy(
                        scoresT[:, qb_ * 512:(qb_ + 1) * 512], pb[b_][0:8, :]),
                        waits=[(PE, last)], inc=(DV, 1))
                    bank_free[lb4] = (DV, sev)
                    rs_sev[qb] = sev
                    for dc in range(DC):
                        y1b_free[dc] = (y1b_free[dc] or []) + [(PE, rs_pe)]

                gmul = {}

                def emit_gates(qb):
                    for j in range(4 * qb, 4 * qb + 4):
                        bk4 = 5 + j % 2
                        tp = kb.emit("tensor", lambda t, j_=j, b_=bk4: t.transpose(
                            pb[b_][:, 0:8], scoresT[:, j_ * 128:(j_ + 1) * 128],
                            id8_sb[:]),
                            waits=[(DV, rs_sev[qb]), (SC, cgt), bw(bk4)],
                            inc=(PE, 1))
                        cv = kb.emit("vector", lambda v, j_=j, b_=bk4: v.tensor_copy(
                            s_tok[:, j_, :], pb[b_][:, 0:8]),
                            waits=[(PE, tp)], inc=(DV, 1))
                        bank_free[bk4] = (DV, cv)
                        p = j % 2
                        rg = [(DV, gmul[j - 2])] if j >= 2 else []
                        r1 = kb.emit("vector", lambda v, j_=j, p_=p: v.reduce_max(
                            m1_sb[p_][:], s_tok[:, j_, :], axis=AX.X),
                            waits=[(DV, cv)] + rg, inc=(DV, 1))
                        k1 = kb.emit("vector", lambda v, j_=j, p_=p: v.tensor_scalar(
                            msk_sb[p_][:], s_tok[:, j_, :], m1_sb[p_][:, 0:1], None,
                            op0=ALU.is_ge),
                            waits=[(DV, r1)], inc=(DV, 1))
                        k2 = kb.emit("vector", lambda v, j_=j, p_=p: v.scalar_tensor_tensor(
                            sk2_sb[p_][:], msk_sb[p_][:], -1e30, s_tok[:, j_, :],
                            op0=ALU.mult, op1=ALU.add),
                            waits=[(DV, k1)], inc=(DV, 1))
                        r2 = kb.emit("vector", lambda v, p_=p: v.reduce_max(
                            m2_sb[p_][:], sk2_sb[p_][:], axis=AX.X),
                            waits=[(DV, k2)], inc=(DV, 1))
                        k3 = kb.emit("vector", lambda v, j_=j, p_=p: v.tensor_scalar(
                            msk_sb[p_][:], s_tok[:, j_, :], m2_sb[p_][:, 0:1], None,
                            op0=ALU.is_ge),
                            waits=[(DV, r2)], inc=(DV, 1))
                        sg = kb.emit("scalar", lambda sc, j_=j, p_=p: sc.activation(
                            sig_sb[p_][:], s_tok[:, j_, :], AF.Sigmoid),
                            waits=[(DV, cv)] + rg, inc=(AC, 1))
                        gmul[j] = kb.emit("vector", lambda v, j_=j, p_=p: v.tensor_mul(
                            g_tok[:, j_, :], sig_sb[p_][:], msk_sb[p_][:]),
                            waits=[(DV, k3), (AC, sg)], inc=(DV, 1))
                        tp2 = kb.emit("tensor", lambda t, j_=j, b_=bk4: t.transpose(
                            pb[b_][0:8, 0:128], g_tok[:, j_, :], id128_sb[:]),
                            waits=[(DV, gmul[j]), bw(bk4)], inc=(PE, 1))
                        cv2 = kb.emit("vector", lambda v, j_=j, b_=bk4: v.tensor_copy(
                            gatesT[:, j_ * 128:(j_ + 1) * 128], pb[b_][0:8, 0:128]),
                            waits=[(PE, tp2)], inc=(DV, 1))
                        bank_free[bk4] = (DV, cv2)

                xow_ld[0] = dma(SD, lambda s: s.dma_start(
                    xow[:], xTo_r[:, :, 0:512]), waits=[])
                emit_xc(0)
                for qb in range(QB):
                    for g in range(0 if qb == 0 else 3, NG + 1):
                        emit_slot(qb, g)
                        if qb > 0 and g == 4:
                            emit_scores(qb - 1)
                        if qb > 0 and g == 6:
                            emit_gates(qb - 1)
                    if qb + 1 < QB:
                        emit_slot(qb + 1, 0)
                        emit_epi_a(qb)
                        emit_slot(qb + 1, 1)
                        emit_epi_b1(qb)
                        emit_slot(qb + 1, 2)
                        emit_epi_b2(qb)
                        emit_xc(qb + 1)
                    else:
                        emit_epi_a(qb)
                        emit_epi_b1(qb)
                        emit_epi_b2(qb)
                        emit_scores(qb)
                        emit_gates(qb)
                gat_dv = kb.at(DV)
                kb.barrier()
        ph34 = ExitStack()
        with ph34:
            acc = ph34.enter_context(nc.sbuf_tensor("acc", [128, DC, TOK], F32))
            ph3 = ExitStack()
            with ph3:
                y1s = [ph3.enter_context(nc.sbuf_tensor(f"y1s{i}", [128, TOK], BF16))
                       for i in range(2)]
                wib = [ph3.enter_context(nc.sbuf_tensor(f"wib{i}", [128, DC, 128], BF16))
                       for i in range(2)]
                wsb2 = [ph3.enter_context(nc.sbuf_tensor(f"wsb{i}", [128, DC, 128], BF16))
                        for i in range(2)]
                wob = [ph3.enter_context(nc.sbuf_tensor(f"wob{i}", [128, 8, 128], BF16))
                       for i in range(2)]
                gbt = ph3.enter_context(nc.sbuf_tensor("gbt", [128, 4, 512], BF16))
                hm = [ph3.enter_context(nc.sbuf_tensor(f"hm{i}", [128, 512], BF16))
                      for i in range(2)]
                sel_sb = ph3.enter_context(nc.sbuf_tensor("sel_sb", [8, E * 128], F32))

                dma(SC, lambda s: s.dma_start(sel_sb[:], sel8))
                cph3 = kb.at(SC)

                y1s_free = [None, None]
                wib_free = [None, None]
                wsb_free = [None, None]
                wob_free = [None, None]
                hm_free = [None, None]

                if debug >= 2:
                    dma(SC, lambda s: s.dma_start(gat_h, gatesT[:]),
                        waits=[(DV, gat_dv)])
                    dma(SC, lambda s: s.dma_start(sco_h, scoresT[:]),
                        waits=[(DV, gat_dv)])
                    dma(SC, lambda s: s.dma_start(
                        stok_h, s_tok[:].rearrange("p a b -> p (a b)")),
                        waits=[(DV, gat_dv)])
                    dma(SC, lambda s: s.dma_start(
                        gtok_h, g_tok[:].rearrange("p a b -> p (a b)")),
                        waits=[(DV, gat_dv)])
                routed = ph3.enter_context(nc.sbuf_tensor("routed", [128, DC, TOK], BF16))
                hbuf = ph3.enter_context(nc.sbuf_tensor("hbuf", [128, 8, TOK], BF16))

                routed_war = None
                h_war = None
                gb_dv = {}
                routed_dv = 0

                def emit_gb_routed(e):
                    nonlocal routed_dv
                    if e < 8:
                        for at in range(4):
                            bnk = 4 + at % 2
                            pp = kb.emit("tensor", lambda t, e_=e, at_=at, b_=bnk: t.matmul(
                                pb[b_][:], sel_sb[:, e_ * 128:(e_ + 1) * 128],
                                gatesT[:, at_ * 512:(at_ + 1) * 512],
                                start=True, stop=True),
                                waits=[(DV, gat_dv), (SC, cph3), bw(bnk)],
                                inc=(PE, 1))
                            gb_dv[at] = kb.emit("vector", lambda v, at_=at, b_=bnk:
                                v.tensor_copy(gbt[:, at_, :], pb[b_][:]),
                                waits=[(PE, pp)], inc=(DV, 1))
                            bank_free[bnk] = (DV, gb_dv[at])
                    for dc in range(DC):
                        i = dc % 2
                        ld = dma(SD, lambda s, dc_=dc, i_=i: s.dma_start(
                            y1s[i_][:], y1b_h[dc_ * 128:(dc_ + 1) * 128, :]),
                            waits=(y1s_free[i] or []))
                        if e < 8:
                            lastv = None
                            for at in range(4):
                                lastv = kb.emit("vector",
                                    lambda v, dc_=dc, at_=at, i_=i: v.tensor_mul(
                                        routed[:, dc_, at_ * 512:(at_ + 1) * 512],
                                        y1s[i_][:, at_ * 512:(at_ + 1) * 512],
                                        gbt[:, at_, :]),
                                    waits=[(SD, ld), (DV, gb_dv[at]), routed_war],
                                    inc=(DV, 1))
                        else:
                            lastv = kb.emit("vector", lambda v, dc_=dc, i_=i: v.tensor_copy(
                                routed[:, dc_, :], y1s[i_][:]),
                                waits=[(SD, ld), routed_war], inc=(DV, 1))
                        y1s_free[i] = [(DV, lastv)]
                    routed_dv = kb.at(DV)

                emit_gb_routed(0)
                for e in range(9):
                    for half in range(2):
                        for fcl in range(8):
                            fc = half * 8 + fcl
                            i = fc % 2
                            ldw = dma(GD, lambda g, e_=e, fc_=fc, i_=i: g.dma_start(
                                wib[i_][:], win_r[e_][:, :, fc_ * 128:(fc_ + 1) * 128]),
                                waits=[wib_free[i]])
                            lds = dma(GD, lambda g, e_=e, fc_=fc, i_=i: g.dma_start(
                                wsb2[i_][:], wsw_r[e_][:, :, fc_ * 128:(fc_ + 1) * 128]),
                                waits=[wsb_free[i]])
                            for at in range(4):
                                lastm = lasts = None
                                for dc in range(DC):
                                    ws = []
                                    if dc == 0:
                                        ws = [(GD, lds), (DV, routed_dv), bw(at)]
                                    lastm = kb.emit("tensor",
                                        lambda t, at_=at, dc_=dc, i_=i: t.matmul(
                                            pb[at_][:], wib[i_][:, dc_, :],
                                            routed[:, dc_, at_ * 512:(at_ + 1) * 512],
                                            start=(dc_ == 0), stop=(dc_ == 7)),
                                        waits=ws, inc=(PE, 1) if dc == 7 else None)
                                    ws2 = [bw(4 + at)] if dc == 0 else []
                                    lasts = kb.emit("tensor",
                                        lambda t, at_=at, dc_=dc, i_=i: t.matmul(
                                            pb[4 + at_][:], wsb2[i_][:, dc_, :],
                                            routed[:, dc_, at_ * 512:(at_ + 1) * 512],
                                            start=(dc_ == 0), stop=(dc_ == 7)),
                                        waits=ws2, inc=(PE, 1) if dc == 7 else None)
                                j = (fc * 4 + at) % 2
                                sl = kb.emit("scalar", lambda sc, at_=at, j_=j: sc.activation(
                                    hm[j_][:], pb[at_][:], AF.Silu),
                                    waits=[(PE, lastm), hm_free[j]], inc=(AC, 1))
                                bank_free[at] = (AC, sl)
                                hmv = kb.emit("vector",
                                    lambda v, fcl_=fcl, at_=at, j_=j: v.tensor_mul(
                                        hbuf[:, fcl_, at_ * 512:(at_ + 1) * 512],
                                        hm[j_][:], pb[4 + at_][:]),
                                    waits=[(AC, sl), (PE, lasts), h_war], inc=(DV, 1))
                                bank_free[4 + at] = (DV, hmv)
                                hm_free[j] = (DV, hmv)
                            wib_free[i] = (PE, kb.at(PE))
                            wsb_free[i] = (PE, kb.at(PE))
                        h_dv = kb.at(DV)
                        if half == 1:
                            routed_war = (PE, kb.at(PE))
                            if e < 8:
                                emit_gb_routed(e + 1)
                        for dc in range(DC):
                            i = dc % 2
                            ld1 = dma(GD, lambda g, e_=e, h_=half, dc_=dc, i_=i: g.dma_start(
                                wob[i_][:], wout_r[e_][:, h_ * 8:(h_ + 1) * 8,
                                                       dc_ * 128:(dc_ + 1) * 128]),
                                waits=[wob_free[i]])
                            for at in range(4):
                                bnk = (dc % 2) * 4 + at
                                last = None
                                for fcl in range(8):
                                    ws = []
                                    if fcl == 0:
                                        ws = [(GD, ld1), (DV, h_dv), bw(bnk)]
                                    last = kb.emit("tensor",
                                        lambda t, at_=at, fc_=fcl, i_=i, b_=bnk:
                                        t.matmul(
                                            pb[b_][:], wob[i_][:, fc_, :],
                                            hbuf[:, fc_, at_ * 512:(at_ + 1) * 512],
                                            start=(fc_ == 0), stop=(fc_ == 7)),
                                        waits=ws, inc=(PE, 1) if fcl == 7 else None)
                                if e == 0 and half == 0:
                                    ev = kb.emit("vector",
                                        lambda v, dc_=dc, at_=at, b_=bnk: v.tensor_copy(
                                            acc[:, dc_, at_ * 512:(at_ + 1) * 512], pb[b_][:]),
                                        waits=[(PE, last)], inc=(DV, 1))
                                else:
                                    ev = kb.emit("vector",
                                        lambda v, dc_=dc, at_=at, b_=bnk: v.tensor_add(
                                            acc[:, dc_, at_ * 512:(at_ + 1) * 512],
                                            acc[:, dc_, at_ * 512:(at_ + 1) * 512], pb[b_][:]),
                                        waits=[(PE, last)], inc=(DV, 1))
                                bank_free[bnk] = (DV, ev)
                            wob_free[i] = (PE, kb.at(PE))
                        h_war = (PE, kb.at(PE))
                if debug >= 2:
                    acc_dv = kb.at(DV)
                    for dc in range(DC):
                        dma(SD, lambda s, dc_=dc: s.dma_start(
                            acc_h.rearrange("(c p) t -> p c t", p=128)[:, dc_, :],
                            acc[:, dc_, :]), waits=[(DV, acc_dv)])
                kb.barrier()

            ph4 = ExitStack()
            with ph4:
                y1fb = [ph4.enter_context(nc.sbuf_tensor(f"y1fb{i}", [128, DC, 512], F32))
                        for i in range(2)]
                x2 = [ph4.enter_context(nc.sbuf_tensor(f"x2_{i}", [128, DC, 512], F32))
                      for i in range(2)]
                ub4 = [ph4.enter_context(nc.sbuf_tensor(f"ub4_{i}", [128, DC, 512], BF16))
                       for i in range(2)]
                zst = [ph4.enter_context(nc.sbuf_tensor(f"zst{i}", [128, 512], F32))
                       for i in range(2)]
                rows2 = tuple(ph4.enter_context(nc.sbuf_tensor(f"r2_{i}", [1, 512], F32))
                              for i in range(4))
                bc2 = ph4.enter_context(nc.sbuf_tensor("bc2", [128, 512], F32))
                bc3 = ph4.enter_context(nc.sbuf_tensor("bc3", [128, 512], F32))
                y1fb_free = [None, None]
                zst_free = [None, None]
                x2_war = [None, None]
                ctx4 = {}
                x2dv4 = {}

                rowsets = [(rA, rB, rC, rD), rows2]
                banksets = [(5, 6), (4, 7)]
                bcsets = [(bc0, bc1), (bc2, bc3)]

                def emit_x2(at):
                    i = at % 2
                    ld = dma(SD, lambda s, at_=at, i_=i: s.dma_start(
                        y1fb[i_][:], y1f_r[:, :, at_ * 512:(at_ + 1) * 512]),
                        waits=(y1fb_free[i] or []))
                    last = None
                    for dc in range(DC):
                        last = kb.emit("vector", lambda v, dc_=dc, at_=at, i_=i: v.tensor_add(
                            x2[i_][:, dc_, :], y1fb[i_][:, dc_, :],
                            acc[:, dc_, at_ * 512:(at_ + 1) * 512]),
                            waits=[(SD, ld)] + ([x2_war[i]] if dc == 0 else []),
                            inc=(DV, 1))
                    x2dv4[at] = last
                    y1fb_free[i] = [(DV, last)]

                def emit_lna4(at):
                    i = at % 2
                    ctx4[at] = emit_ln_a(lambda dc: x2[i][:, dc, :], ub4[i],
                                         [(DV, x2dv4[at])], rowsets[i][0:2], banksets[i])

                def emit_lnb4(at):
                    i = at % 2

                    def z_out(dc, b1, b2, at_=at, i_=i):
                        j = dc % 2
                        bcA, bcB = bcsets[i_]
                        w1 = kb.emit("vector", lambda v, dc_=dc, j_=j: v.tensor_mul(
                            zst[j_][:], x2[i_][:, dc_, :], bcA[:]),
                            waits=[(DV, max(b1, b2)), zst_free[j]], inc=(DV, 1))
                        w2 = kb.emit("vector", lambda v, j_=j: v.tensor_sub(
                            zst[j_][:], zst[j_][:], bcB[:]),
                            waits=[(DV, w1)], inc=(DV, 1))
                        d1 = dma(CD, lambda s, dc_=dc, j_=j: s.dma_start(
                            zT_r[:, dc_, at_ * 512:(at_ + 1) * 512], zst[j_][:]),
                            waits=[(DV, w2)])
                        zst_free[j] = (CD, d1)
                        if dc == DC - 1:
                            x2_war[i_] = (DV, kb.at(DV))

                    emit_ln_b(ctx4[at], z_out, rowsets[i], banksets[i], bcsets[i])

                emit_x2(0)
                emit_lna4(0)
                emit_x2(1)
                emit_lna4(1)
                emit_lnb4(0)
                emit_x2(2)
                emit_lna4(2)
                emit_lnb4(1)
                emit_x2(3)
                emit_lna4(3)
                emit_lnb4(2)
                emit_lnb4(3)
                kb.barrier()
                kb.barrier()
        kb.build()
    return nc


_CACHE = {}


def _consts():
    sel = np.zeros((8, E * 128), np.float32)
    for e in range(E):
        sel[e, e * 128:(e + 1) * 128] = 1.0
    return {
        "ones128": np.ones((128, 1), BF),
        "ones1f": np.ones((1, 128), np.float32),
        "id128f": np.eye(128, dtype=np.float32),
        "id8f": np.eye(8, dtype=np.float32),
        "sel8": sel,
    }


def kernel(L0_attn_x, L0_attn_Wq, L0_attn_bq, L0_attn_Wk, L0_attn_bk,
           L0_ffn_router_DE, L0_ffn_w_in_shared_FD, L0_ffn_w_out_shared_DF,
           L0_ffn_w_swiglu_FD, L0_ffn_w_in_eDF, L0_ffn_w_swiglu_eDF,
           L0_ffn_w_out_eFD):
    x = np.ascontiguousarray(np.asarray(L0_attn_x, dtype=np.float32))
    B = x.shape[0]
    common = {
        "wqT": np.ascontiguousarray(np.asarray(L0_attn_Wq).T),
        "wk2": np.ascontiguousarray(np.asarray(L0_attn_Wk)),
        "bq": np.ascontiguousarray(np.asarray(L0_attn_bq).reshape(EC, 128).T),
        "router": np.ascontiguousarray(np.asarray(L0_ffn_router_DE)),
        "wish": np.ascontiguousarray(np.asarray(L0_ffn_w_in_shared_FD).T),
        "wssh": np.ascontiguousarray(np.asarray(L0_ffn_w_swiglu_FD).T),
        "wosh": np.ascontiguousarray(np.asarray(L0_ffn_w_out_shared_DF).T),
        "win": np.ascontiguousarray(np.asarray(L0_ffn_w_in_eDF)),
        "wsw": np.ascontiguousarray(np.asarray(L0_ffn_w_swiglu_eDF)),
        "wout": np.ascontiguousarray(np.asarray(L0_ffn_w_out_eFD)),
    }
    common.update(_consts())
    xTb = [np.ascontiguousarray(x[b].T) for b in range(B)]
    in_maps = []
    for c in range(8):
        b = c // 4
        o = (c % 4) * TOK
        m = dict(common)
        m["xT"] = xTb[b]
        m["xN"] = x[b]
        m["xTo"] = np.ascontiguousarray(xTb[b][:, o:o + TOK])
        in_maps.append(m)
    if "nc" not in _CACHE:
        _CACHE["nc"] = build_program(debug=False)
    nc = _CACHE["nc"]
    res = run_bass_kernel_spmd(nc, in_maps, core_ids=list(range(8)))
    out = np.empty((B, S, D), np.float32)
    for c in range(8):
        b = c // 4
        o = (c % 4) * TOK
        out[b, o:o + TOK, :] = res.results[c]["zT"].T
    return out
